# version 01
# baseline (speedup 1.0000x reference)
import numpy as np
import ml_dtypes
import concourse.bass as bass
import concourse.mybir as mybir
from concourse.bass_utils import run_bass_kernel_spmd

F32 = mybir.dt.float32
BF16 = mybir.dt.bfloat16
AF = mybir.ActivationFunctionType
ALU = mybir.AluOpType
AXX = mybir.AxisListType.X

NCORES = 8
NB = 4
NL = 512
T = NB * NL
D = 1024
NCH = D // 128
NTT = 4
TTW = 512
SEQ = 4096
NH = 16
HD = 64
VDIM = NH * 65

BF = ml_dtypes.bfloat16
MOE_DBG = False
MOE_NEXP = 8
MOE_SKIP_FC3 = False
ESEM = {"t": "pe", "v": "ve", "s": "se", "g": "ge"}



def _blocks(wT, mblk=128):
    K, M = wT.shape
    kc = K // 128
    nb = M // mblk
    return np.ascontiguousarray(
        wT.reshape(kc, 128, nb, mblk).transpose(2, 1, 0, 3).reshape(nb, 128, kc * mblk))


def _chunkmajor(wT):
    K, M = wT.shape
    kc = K // 128
    return np.ascontiguousarray(wT.reshape(kc, 128, M).transpose(1, 0, 2).reshape(128, kc * M))


def _rope_tables(core):
    n = core * NL + np.arange(NL, dtype=np.float64)
    inv = 1.0 / (10000.0 ** (np.arange(0, 64, 2, dtype=np.float64) / 64.0))
    fr = n[:, None] * inv[None, :]
    emb = np.concatenate([fr, fr], axis=1)
    s = 64.0 ** -0.25
    cs = (np.cos(emb) * s).astype(np.float32).T
    sn = (np.sin(emb) * s).astype(np.float32).T
    return (np.ascontiguousarray(np.tile(cs, (2, NB))),
            np.ascontiguousarray(np.tile(sn, (2, NB))))


def _rotperm():
    m = np.zeros((128, 128), np.float32)
    for blk in (0, 64):
        for i in range(32):
            m[blk + 32 + i, blk + i] = -1.0
            m[blk + i, blk + 32 + i] = 1.0
    return m.astype(BF)


def _rollperm():
    m = np.zeros((128, 128), np.float32)
    for i in range(128):
        m[(i - 1) % 128, i] = 1.0
    return m.astype(BF)


def _pp(vec, n=NCH):
    return np.ascontiguousarray(np.asarray(vec).reshape(n, 128).T)


def host_prep(inputs):
    g = {}
    T_ = lambda a: np.ascontiguousarray(np.asarray(a).T)
    A_ = lambda a: np.ascontiguousarray(np.asarray(a))

    g["l0a_wqt"] = _chunkmajor(T_(inputs["L0_attn_Wq"]))
    g["l0a_wkt"] = _chunkmajor(T_(inputs["L0_attn_Wk"]))
    g["l0a_wvt"] = _chunkmajor(T_(inputs["L0_attn_Wv"]))
    g["l0a_wot"] = _chunkmajor(T_(inputs["L0_attn_Wout"]))
    g["l0a_bout"] = _pp(inputs["L0_attn_bout"])

    qkvT = T_(inputs["L0_ffn_qkv_w"])
    g["l0f_wqt"] = _chunkmajor(qkvT[:, 0:1024])
    g["l0f_wkt"] = _chunkmajor(qkvT[:, 1024:2048])
    g["l0f_wvt"] = _chunkmajor(qkvT[:, 2048:3072])
    qb = A_(inputs["L0_ffn_qkv_b"])
    g["l0f_qb"] = _pp(qb[0:1024])
    g["l0f_kb"] = _pp(qb[1024:2048])
    g["l0f_vb"] = qb[2048:3072].reshape(1, 1024).copy()
    g["l0f_wot"] = _chunkmajor(T_(inputs["L0_ffn_out_w"]))
    g["l0f_bout"] = _pp(inputs["L0_ffn_out_b"])
    g["l0f_n1w"] = _pp(inputs["L0_ffn_norm1_w"])
    g["l0f_n2w"] = _pp(inputs["L0_ffn_norm2_w"])

    w1 = np.asarray(inputs["L0_ffn_w1"]); w2 = np.asarray(inputs["L0_ffn_w2"])
    w3 = np.asarray(inputs["L0_ffn_w3"])
    g["l0f_w1b"] = np.stack([_blocks(T_(w1[e])) for e in range(8)])
    g["l0f_w2b"] = np.stack([_blocks(T_(w2[e])) for e in range(8)])
    g["l0f_w3b"] = np.stack([_blocks(T_(w3[e])) for e in range(8)])
    g["l0f_b1"] = np.stack([_pp(np.asarray(inputs["L0_ffn_b1"])[e], 16) for e in range(8)])
    g["l0f_b2"] = np.stack([_pp(np.asarray(inputs["L0_ffn_b2"])[e], 16) for e in range(8)])
    g["l0f_b3"] = A_(inputs["L0_ffn_b3"]).reshape(8, 1, 1024).copy()
    g["l0f_gatet"] = _chunkmajor(T_(inputs["L0_ffn_gate_w"]))

    g["l1a_wq"] = _chunkmajor(A_(inputs["L1_attn_Wq"]))
    g["l1a_wk"] = _chunkmajor(A_(inputs["L1_attn_Wk"]))
    g["l1a_wv"] = _chunkmajor(A_(inputs["L1_attn_Wv"]))
    g["l1a_wbm"] = _chunkmajor(A_(inputs["L1_attn_Wb"]))
    g["l1a_wo"] = _chunkmajor(A_(inputs["L1_attn_Wo"]))
    g["l1a_bo"] = _pp(inputs["L1_attn_bo"])

    g["l1f_wg"] = _chunkmajor(A_(inputs["L1_ffn_Wg"]))
    g["l1f_alpha"] = A_(inputs["L1_ffn_alpha"]).reshape(1, 8).copy()
    g["l1f_lnw"] = np.stack([_pp(np.asarray(inputs["L1_ffn_ln_w"])[e]) for e in range(8)])
    g["l1f_lnb"] = np.stack([_pp(np.asarray(inputs["L1_ffn_ln_b"])[e]) for e in range(8)])
    fc1 = np.asarray(inputs["L1_ffn_fc1_w"]); fc2 = np.asarray(inputs["L1_ffn_fc2_w"])
    g["l1f_fc1b"] = np.stack([_blocks(A_(fc1[e])) for e in range(8)])
    g["l1f_fc2b"] = np.stack([_blocks(A_(fc2[e])) for e in range(8)])
    g["l1f_b1"] = np.stack([_pp(np.asarray(inputs["L1_ffn_fc1_b"])[e], 32) for e in range(8)])
    g["l1f_b2"] = A_(inputs["L1_ffn_fc2_b"]).reshape(8, 1, 1024).copy()

    g["rotperm"] = _rotperm()
    g["rollperm"] = _rollperm()
    g["iota8"] = np.tile(np.arange(8, dtype=np.float32), (128, 1)).copy()

    x = np.asarray(inputs["L0_attn_x"])
    percore = []
    for c in range(NCORES):
        d = {"x_in": np.ascontiguousarray(x[:, c * NL:(c + 1) * NL, :])}
        cs, sn = _rope_tables(c)
        d["ropecs"] = cs
        d["ropesn"] = sn
        percore.append(d)
    return g, percore



class Ph:
    ENGS = ("t", "v", "s", "g", "y")

    def __init__(self, P):
        self.P = P
        self.nc = P.nc
        self.ops = {e: [] for e in self.ENGS}

    def wait(self, eng, sem, val):
        if val <= 0:
            return
        h = self.P.sems[sem]
        self.ops[eng].append(lambda e, h=h, v=val: e.wait_ge(h, v))

    def raw(self, eng, fn):
        self.ops[eng].append(fn)

    def do(self, eng, fn, inc="auto", n=1):
        if inc == "auto":
            inc = ESEM.get(eng)
        if inc is None:
            self.ops[eng].append(fn)
            return None
        h = self.P.sems[inc]
        self.ops[eng].append(lambda e, fn=fn, h=h, n=n: fn(e).then_inc(h, n))
        self.P.cnt[inc] += n
        return self.P.cnt[inc]

    def flush(self):
        with self.nc.Block() as block:
            for eng, sect in (("y", block.sync), ("g", block.gpsimd),
                              ("t", block.tensor), ("s", block.scalar),
                              ("v", block.vector)):
                ops = self.ops[eng]
                if ops:
                    sect(lambda e, ops=ops: [f(e) for f in ops])
        self.ops = {e: [] for e in self.ENGS}


class Seq:
    def __init__(self, ph, deps=()):
        self.ph = ph
        self.deps = list(deps)

    def do(self, eng, fn, inc="auto", n=1):
        for sem, cntv in self.deps:
            self.ph.wait(eng, sem, cntv)
        c = self.ph.do(eng, fn, inc=inc, n=n)
        if inc == "auto":
            inc = ESEM.get(eng)
        if inc is not None:
            self.deps = [(inc, c)]
        return c

    def also(self, eng, fn, inc="auto", n=1):
        c = self.ph.do(eng, fn, inc=inc, n=n)
        if inc == "auto":
            inc = ESEM.get(eng)
        if inc is not None:
            self.deps.append((inc, c))
        return c


class Prog:
    def __init__(self, nc):
        self.nc = nc
        self.cnt = {}
        self.sems = {}
        for name in ("wl", "al", "st", "pe", "ve", "se", "ge", "cc"):
            self.sems[name] = nc.alloc_semaphore(name)
            self.cnt[name] = 0
        self.dram = {}
        self.sb = {}
        self.innames = []

    def phase(self):
        return Ph(self)

    def din(self, name, shape, dtype=F32):
        if name not in self.dram:
            self.dram[name] = self.nc.dram_tensor(name, list(shape), dtype, kind="ExternalInput")
            self.innames.append(name)
        return self.dram[name]

    def dout(self, name, shape, dtype=F32):
        self.dram[name] = self.nc.dram_tensor(name, list(shape), dtype, kind="ExternalOutput")
        return self.dram[name]

    def dint(self, name, shape, dtype=F32):
        if name not in self.dram:
            self.dram[name] = self.nc.dram_tensor(name, list(shape), dtype, kind="Internal")
        return self.dram[name]

    def sbuf(self, name, shape, dtype=F32):
        if name not in self.sb:
            self.sb[name] = self.nc.alloc_sbuf_tensor(name, list(shape), dtype)
        return self.sb[name]



def emit_consts(P, ph):
    idn = P.sbuf("idn", [128, 128], F32)
    ones_col = P.sbuf("ones_col", [128, 1], F32)
    ones_row = P.sbuf("ones_row", [1, 128], F32)
    ones_col_bf = P.sbuf("ones_col_bf", [128, 1], BF16)
    ones_row_bf = P.sbuf("ones_row_bf", [1, 128], BF16)
    rotp = P.sbuf("rotp", [128, 128], BF16)
    rollp = P.sbuf("rollp", [128, 128], BF16)
    iota8v = P.sbuf("iota8v", [128, 8], F32)

    ph.do("g", lambda e: e.memset(ones_col[:, :], 1.0))
    ph.do("g", lambda e: e.memset(ones_row[:, :], 1.0))
    ph.do("g", lambda e: e.memset(ones_col_bf[:, :], 1.0))
    ph.do("g", lambda e: e.memset(ones_row_bf[:, :], 1.0))
    ph.do("g", lambda e: e.memset(idn[:, :], 0.0))
    ph.do("g", lambda e: e.affine_select(
        out=idn[:, :], in_=idn[:, :], compare_op=ALU.not_equal, fill=1.0,
        base=0, pattern=[[-1, 128]], channel_multiplier=1))
    for nm, dst in (("rotperm", rotp), ("rollperm", rollp), ("iota8", iota8v)):
        dt = BF16 if nm in ("rotperm", "rollperm") else F32
        d = P.din(nm, list(dst.shape), dt)
        ph.do("y", lambda e, d=d, dst=dst: e.dma_start(dst[:, :], d[:, :]), inc="al", n=16)


def emit_input_convert(P, ph):
    nc = P.nc
    x_fm = P.sbuf("x_fm", [128, NCH * T], F32)
    x_in = P.din("x_in", [NB, NL, D], F32)
    idn = P.sb["idn"]
    with nc.sbuf_tensor("xtok_i", [128, 16 * D], F32) as xtok, \
         nc.psum_tensor("ptr_i", [128, 4 * 512], F32) as ptr:
        ap_in = x_in.ap().rearrange("b (j p) d -> p (b j) d", p=128)
        ap_out = xtok.ap().rearrange("p (j d) -> p j d", j=16)
        c_al = ph.do("y", lambda e: e.dma_start(ap_out, ap_in), inc="al", n=16)
        base_pe = P.cnt["pe"]; base_ve = P.cnt["ve"]
        i = 0
        for j in range(16):
            for dc in range(NCH):
                bank = i % 4
                ph.wait("t", "al", c_al)
                if i >= 4:
                    ph.wait("t", "ve", base_ve + i - 3)
                ph.do("t", lambda e, j=j, dc=dc, bank=bank: e.transpose(
                    ptr[:, bank * 512:bank * 512 + 128],
                    xtok[:, j * D + dc * 128: j * D + (dc + 1) * 128],
                    idn[:, :]))
                ph.wait("v", "pe", base_pe + i + 1)
                ph.do("v", lambda e, j=j, dc=dc, bank=bank: e.tensor_copy(
                    x_fm[:, dc * T + j * 128: dc * T + (j + 1) * 128],
                    ptr[:, bank * 512:bank * 512 + 128]))
                i += 1
        ph.flush()


def emit_output_convert(P, ph):
    nc = P.nc
    x_fm = P.sb["x_fm"]
    x_out = P.dout("x_out", [NB, NL, D], F32)
    idn = P.sb["idn"]
    with nc.sbuf_tensor("xtok_o", [128, 16 * D], F32) as xtok, \
         nc.psum_tensor("ptr_o", [128, 4 * 512], F32) as ptr:
        base_pe = P.cnt["pe"]; base_ve = P.cnt["ve"]
        i = 0
        for j in range(16):
            for dc in range(NCH):
                bank = i % 4
                if i >= 4:
                    ph.wait("t", "ve", base_ve + i - 3)
                ph.do("t", lambda e, j=j, dc=dc, bank=bank: e.transpose(
                    ptr[:, bank * 512:bank * 512 + 128],
                    x_fm[:, dc * T + j * 128: dc * T + (j + 1) * 128],
                    idn[:, :]))
                ph.wait("v", "pe", base_pe + i + 1)
                ph.do("v", lambda e, j=j, dc=dc, bank=bank: e.tensor_copy(
                    xtok[:, j * D + dc * 128: j * D + (dc + 1) * 128],
                    ptr[:, bank * 512:bank * 512 + 128]))
                i += 1
        c_ve = P.cnt["ve"]
        ph.wait("y", "ve", c_ve)
        ap_out = x_out.ap().rearrange("b (j p) d -> p (b j) d", p=128)
        ap_in = xtok.ap().rearrange("p (j d) -> p j d", j=16)
        c_st = ph.do("y", lambda e: e.dma_start(ap_out, ap_in), inc="st", n=16)
        ph.wait("y", "st", c_st)
        ph.flush()


def emit_fm_load(P, ph):
    x_fm = P.sbuf("x_fm", [128, NCH * T], F32)
    d = P.din("xfm_in", [128, NCH * T], F32)
    c = ph.do("y", lambda e: e.dma_start(x_fm[:, :], d[:, :]), inc="al", n=16)
    ph.wait("y", "al", c)
    ph.flush()


def emit_fm_store(P, ph):
    x_fm = P.sb["x_fm"]
    d = P.dout("xfm_out", [128, NCH * T], F32)
    c = ph.do("y", lambda e: e.dma_start(d[:, :], x_fm[:, :]), inc="st", n=16)
    ph.wait("y", "st", c)
    ph.flush()



def emit_norm_fm(P, ph, srct, dstt, kind, eps, w_sb=None, uid="", tts=None,
                 dst_stride=T, dst_base=0, src_stride=T):
    nc = P.nc
    assert not (kind == "ln" and w_sb is not None)
    if tts is None:
        tts = list(range(NTT))
    ones_src = P.sb["ones_col"] if srct.dtype == F32 else P.sb["ones_col_bf"]
    ones_col = P.sb["ones_col"]
    ones_row = P.sb["ones_row"]
    src = lambda cc, t0, tw: srct[:, cc * src_stride + t0: cc * src_stride + t0 + tw]
    dst = lambda cc, t0, tw: dstt[:, dst_base + cc * dst_stride + t0 - tts[0] * TTW:
                                  dst_base + cc * dst_stride + t0 - tts[0] * TTW + tw]
    with nc.sbuf_tensor(f"nsq{uid}", [128, NCH * TTW // 2], F32) as sq, \
         nc.sbuf_tensor(f"nrw{uid}", [1, 4 * TTW], F32) as rows, \
         nc.psum_tensor(f"nmps{uid}", [1, 512], F32) as ps_sum, \
         nc.psum_tensor(f"nmpq{uid}", [1, 512], F32) as ps_sq, \
         nc.psum_tensor(f"nmpa{uid}", [128, 512], F32) as rep_a, \
         nc.psum_tensor(f"nmpb{uid}", [128, 512], F32) as rep_b:
        sq3 = sq.ap().rearrange("p (c w) -> p c w", c=NCH // 2)
        prev_ve = P.cnt["ve"]
        for tt in tts:
            t0 = tt * TTW
            src3 = srct.ap().rearrange("p (c t) -> p c t", c=NCH)[:, :, t0:t0 + TTW]
            ph.wait("s", "ve", prev_ve)
            if kind == "ln":
                ph.wait("t", "ve", prev_ve)
                for cc in range(NCH):
                    ph.do("t", lambda e, cc=cc, t0=t0: e.matmul(
                        ps_sum[:, :], ones_src[:, :], src(cc, t0, TTW),
                        start=(cc == 0), stop=(cc == NCH - 1)),
                        inc=(None if cc < NCH - 1 else "pe"))
            for half in range(2):
                if half == 1:
                    ph.wait("s", "pe", P.cnt["pe"])
                c_sq = ph.do("s", lambda e, src3=src3, half=half: e.activation(
                    sq3, src3[:, half * (NCH // 2):(half + 1) * (NCH // 2), :],
                    AF.Square))
                ph.wait("t", "se", c_sq)
                for cc4 in range(NCH // 2):
                    cc = half * (NCH // 2) + cc4
                    ph.do("t", lambda e, cc4=cc4, cc=cc: e.matmul(
                        ps_sq[:, :], ones_col[:, :], sq[:, cc4 * TTW:(cc4 + 1) * TTW],
                        start=(cc == 0), stop=(cc == NCH - 1)),
                        inc=(None if cc < NCH - 1 else "pe"))
            c_pe = P.cnt["pe"]
            mu = rows[:, 0:TTW]; var = rows[:, TTW:2 * TTW]
            a = rows[:, 2 * TTW:3 * TTW]; b = rows[:, 3 * TTW:4 * TTW]
            s_ = Seq(ph, deps=[("pe", c_pe)])
            if kind == "ln":
                s_.do("v", lambda e: e.tensor_scalar(mu, ps_sum[0:1, :], 1.0 / D, None, ALU.mult))
                s_.do("v", lambda e: e.tensor_scalar(var, ps_sq[0:1, :], 1.0 / D, None, ALU.mult))
                s_.do("s", lambda e: e.activation(b, mu, AF.Square))
                s_.do("v", lambda e: e.tensor_sub(var, var, b))
            else:
                s_.do("v", lambda e: e.tensor_scalar(var, ps_sq[0:1, :], 1.0 / D, None, ALU.mult))
            s_.do("v", lambda e: e.tensor_scalar(var, var, float(eps), None, ALU.add))
            s_.do("s", lambda e: e.activation(b, var, AF.Sqrt))
            s_.do("v", lambda e: e.reciprocal(a, b))
            if kind == "ln":
                s_.do("v", lambda e: e.scalar_tensor_tensor(b, mu, -1.0, a, ALU.mult, ALU.mult))
            s_.do("t", lambda e: e.matmul(rep_a[:, :], ones_row[:, :], a, start=True, stop=True))
            if kind == "ln":
                s_.do("t", lambda e: e.matmul(rep_b[:, :], ones_row[:, :], b, start=True, stop=True))
            c_rep = s_.deps[0][1]
            ph.wait("v", "pe", c_rep)
            for cc in range(NCH):
                d_ap = dst(cc, t0, TTW)
                if kind == "ln":
                    ph.do("v", lambda e, cc=cc, t0=t0: e.tensor_mul(
                        sq[:, (cc % 4) * TTW:(cc % 4 + 1) * TTW], src(cc, t0, TTW),
                        rep_a[:, :]))
                    ph.do("v", lambda e, cc=cc, d_ap=d_ap: e.tensor_add(
                        d_ap, sq[:, (cc % 4) * TTW:(cc % 4 + 1) * TTW], rep_b[:, :]))
                else:
                    if w_sb is None:
                        ph.do("v", lambda e, cc=cc, t0=t0, d_ap=d_ap: e.tensor_mul(
                            d_ap, src(cc, t0, TTW), rep_a[:, :]))
                    else:
                        ph.do("v", lambda e, cc=cc, t0=t0, d_ap=d_ap: e.scalar_tensor_tensor(
                            d_ap, src(cc, t0, TTW), w_sb[:, cc:cc + 1], rep_a[:, :],
                            ALU.mult, ALU.mult))
            prev_ve = P.cnt["ve"]
    ph.flush()



def emit_proj_fm(P, ph, wsb, c_w, zt, outt, prj_ps, scale=1.0, bias_sb=None,
                 deps=(), hcs=None, outofs=None):
    if hcs is None:
        hcs = list(range(NCH))
    if outofs is None:
        outofs = lambda hc: hc * T
    base_se = P.cnt["se"]
    i = 0
    for hc in hcs:
        oofs = outofs(hc)
        for tt in range(NTT):
            t0 = tt * TTW
            bank = i % 2
            if i == 0:
                ph.wait("t", "wl", c_w)
                for sem, v in deps:
                    ph.wait("t", sem, v)
            if i >= 2:
                ph.wait("t", "se", base_se + i - 1)
            for cc in range(NCH):
                ph.do("t", lambda e, cc=cc, hc=hc, t0=t0, bank=bank: e.matmul(
                    prj_ps[:, bank * 512:(bank + 1) * 512],
                    wsb[:, cc * D + hc * 128: cc * D + (hc + 1) * 128],
                    zt[:, cc * T + t0: cc * T + t0 + TTW],
                    start=(cc == 0), stop=(cc == NCH - 1)),
                    inc=(None if cc < NCH - 1 else "pe"))
            c_pe = P.cnt["pe"]
            ph.wait("s", "pe", c_pe)
            bb = None if bias_sb is None else bias_sb[:, hc:hc + 1]
            def ev(e, oofs=oofs, t0=t0, bank=bank, bb=bb):
                kw = dict(scale=float(scale))
                if bb is not None:
                    kw["bias"] = bb
                return e.activation(outt[:, oofs + t0: oofs + t0 + TTW],
                                    prj_ps[:, bank * 512:(bank + 1) * 512],
                                    AF.Identity, **kw)
            ph.do("s", ev)
            i += 1
    return P.cnt["se"]



def emit_rope(P, ph, buf, rp_ps, tmp, c_ready_se, cs0, sn0):
    c = P.cnt["ve"]
    for hc in range(NCH):
        c = _emit_rope_tiles(
            P, ph, lambda tt, hc=hc: buf[:, hc * T + tt * TTW: hc * T + (tt + 1) * TTW],
            rp_ps, tmp, c_ready_se, cs0, sn0)
    return c



def _emit_rope_tiles(P, ph, get_sl, rp_ps, rtmp, c_ready_se, cs0, sn0):
    rotp = P.sb["rotp"]
    base_ve = P.cnt["ve"]
    for tt in range(NTT):
        t0 = tt * TTW
        bank = tt % 2
        sl = get_sl(tt)
        if tt == 0:
            ph.wait("t", "se", c_ready_se)
            ph.wait("t", "ve", base_ve)
        if tt >= 2:
            ph.wait("t", "ve", base_ve + (tt - 2) * 3 + 3)
        ph.do("t", lambda e, sl=sl, bank=bank: e.matmul(
            rp_ps[:, bank * 512:(bank + 1) * 512], rotp[:, :], sl,
            start=True, stop=True))
        c_pe = P.cnt["pe"]
        ph.wait("v", "pe", c_pe)
        ph.do("v", lambda e, bank=bank, t0=t0: e.tensor_mul(
            rtmp[:, 0:TTW], rp_ps[:, bank * 512:(bank + 1) * 512], sn0[:, t0:t0 + TTW]))
        ph.do("v", lambda e, sl=sl, t0=t0: e.tensor_mul(
            rtmp[:, TTW:2 * TTW], sl, cs0[:, t0:t0 + TTW]))
        ph.do("v", lambda e, sl=sl: e.tensor_add(sl, rtmp[:, TTW:2 * TTW], rtmp[:, 0:TTW]))
    return P.cnt["ve"]


def _emit_m2(P, ph, get_chunk, hcs, row_out, nps, sqbuf, dep_of=None):
    ones_col = P.sb["ones_col"]
    red = {}
    k = 0
    for hc in hcs:
        ch = get_chunk(hc)
        deps = [("pe", P.cnt["pe"])]
        if dep_of is not None:
            deps.append(dep_of(hc))
        for sem, v in deps:
            ph.wait("v", sem, v)
        c_sq = ph.do("v", lambda e, ch=ch: e.tensor_mul(sqbuf[:, :], ch, ch))
        ph.wait("t", "ve", c_sq)
        for j in range(2):
            hb = j * 64
            for b in range(NB):
                idx = (hc * 2 + j) * 4 + b
                bank = k % 2
                if k >= 2:
                    ph.wait("t", "ve", red[k - 2])
                c_mm = ph.do("t", lambda e, hb=hb, b=b, bank=bank: e.matmul(
                    nps[0:1, bank * 512:(bank + 1) * 512],
                    ones_col[hb:hb + 64, :], sqbuf[hb:hb + 64, b * 512:(b + 1) * 512],
                    start=True, stop=True))
                ph.wait("v", "pe", c_mm)
                red[k] = ph.do("v", lambda e, idx=idx, bank=bank: e.reduce_max(
                    row_out[0:1, idx:idx + 1], nps[0:1, bank * 512:(bank + 1) * 512],
                    axis=AXX))
                k += 1
    return P.cnt["ve"]



def emit_attn(P, ph, cfg):
    from contextlib import ExitStack
    nc = P.nc
    uid = cfg["uid"]
    x_fm = P.sb["x_fm"]
    ones_col = P.sb["ones_col"]; ones_row = P.sb["ones_row"]
    ones_col_bf = P.sb["ones_col_bf"]; ones_row_bf = P.sb["ones_row_bf"]; ones_row_bf = P.sb["ones_row_bf"]
    ginK = P.dint("ginK", [D, T], BF16)
    goutK = P.dint("goutK", [NCORES * D, T], BF16)
    ginV = P.dint("ginV", [T, D], BF16)
    goutV = P.dint("goutV", [NCORES * T, D], BF16)
    gin_s = P.dint("gin_s", [1, 64], F32)
    gout_s = P.dint("gout_s", [1, 64], F32)

    dwq = P.din(cfg["wq"], [128, NCH * D], F32)
    dwk = P.din(cfg["wk"], [128, NCH * D], F32)
    dwv = P.din(cfg["wv"], [128, NCH * D], F32)
    dwo = P.din(cfg["wo"], [128, NCH * D], F32)
    dbout = P.din(cfg["bout"], [128, NCH], F32)

    with ExitStack() as outer:
        qbuf = outer.enter_context(nc.sbuf_tensor(f"q{uid}", [128, NCH * T], BF16))
        rowbuf = outer.enter_context(nc.sbuf_tensor(f"rw{uid}", [1, 256], F32))
        negc_rep = outer.enter_context(nc.sbuf_tensor(f"ncr{uid}", [128, 64], F32))
        bout_sb = outer.enter_context(nc.sbuf_tensor(f"bo{uid}", [128, NCH], F32))
        qm2 = rowbuf[0:1, 0:64]
        km2 = rowbuf[0:1, 64:128]
        km2g = rowbuf[0:1, 128:192]
        t1 = rowbuf[0:1, 192:256]
        ph.do("y", lambda e: e.dma_start(bout_sb[:, :], dbout[:, :]), inc="al", n=16)

        with ExitStack() as stA:
            zbuf = stA.enter_context(nc.sbuf_tensor(f"z{uid}", [128, NCH * T], BF16))
            wsb0 = stA.enter_context(nc.sbuf_tensor(f"w0{uid}", [128, NCH * D], BF16))
            wsb1 = stA.enter_context(nc.sbuf_tensor(f"w1{uid}", [128, NCH * D], BF16))
            gain_sb = bias_q = bias_k = vb_sb = None
            if cfg.get("gain"):
                gain_d = P.din(cfg["gain"], [128, NCH], F32)
                gain_sb = stA.enter_context(nc.sbuf_tensor(f"gn{uid}", [128, NCH], F32))
                ph.do("y", lambda e: e.dma_start(gain_sb[:, :], gain_d[:, :]), inc="al", n=16)
            if cfg.get("qb"):
                dqb = P.din(cfg["qb"], [128, NCH], F32)
                dkb = P.din(cfg["kb"], [128, NCH], F32)
                bias_q = stA.enter_context(nc.sbuf_tensor(f"qb{uid}", [128, NCH], F32))
                bias_k = stA.enter_context(nc.sbuf_tensor(f"kb{uid}", [128, NCH], F32))
                ph.do("y", lambda e: e.dma_start(bias_q[:, :], dqb[:, :]), inc="al", n=16)
                ph.do("y", lambda e: e.dma_start(bias_k[:, :], dkb[:, :]), inc="al", n=16)
            if cfg.get("vb"):
                dvb = P.din(cfg["vb"], [1, D], F32)
                vb_sb = stA.enter_context(nc.sbuf_tensor(f"vb{uid}", [1, D], BF16))
                ph.do("g", lambda e: e.dma_start(vb_sb[:, :], dvb[:, :]), inc="al", n=16)
            cs0 = sn0 = None
            if cfg["rope"]:
                dcs = P.din("ropecs", [128, T], F32)
                dsn = P.din("ropesn", [128, T], F32)
                cs0 = stA.enter_context(nc.sbuf_tensor(f"cs{uid}", [128, T], F32))
                sn0 = stA.enter_context(nc.sbuf_tensor(f"sn{uid}", [128, T], F32))
                ph.do("y", lambda e: e.dma_start(cs0[:, :], dcs[:, :]), inc="al", n=16)
                ph.do("y", lambda e: e.dma_start(sn0[:, :], dsn[:, :]), inc="al", n=16)
            c_misc = P.cnt["al"]
            scale = 1.0 if cfg["rope"] else 64.0 ** -0.25

            ph.wait("v", "al", c_misc)
            ph.wait("s", "al", c_misc)
            emit_norm_fm(P, ph, x_fm, zbuf, cfg["kind"], cfg["eps"], w_sb=gain_sb, uid=uid)
            c_z = P.cnt["ve"]

            with nc.psum_tensor(f"prj{uid}", [128, 1024], F32) as prj_ps, \
                 nc.psum_tensor(f"rp{uid}", [128, 1024], F32) as rp_ps, \
                 nc.psum_tensor(f"nps{uid}", [1, 1024], F32) as nps, \
                 nc.psum_tensor(f"sps{uid}", [128, 64], F32) as sps, \
                 nc.sbuf_tensor(f"kch{uid}", [128, 2 * T], BF16) as kch, \
                 nc.sbuf_tensor(f"ksq{uid}", [128, T], F32) as ksq, \
                 nc.sbuf_tensor(f"rt{uid}", [128, 2 * TTW], F32) as rtmp, \
                 nc.sbuf_tensor(f"vst{uid}", [128, 2 * D], BF16) as vstage:

                c_wq = ph.do("g", lambda e: e.dma_start(
                    wsb0.ap().rearrange("p (c m) -> p c m", c=NCH),
                    dwq.ap().rearrange("p (c m) -> p c m", c=NCH)), inc="wl", n=16)
                c_qse = emit_proj_fm(P, ph, wsb0, c_wq, zbuf, qbuf, prj_ps,
                                     scale=scale, bias_sb=bias_q, deps=[("ve", c_z)])
                c_wk = ph.do("g", lambda e: e.dma_start(
                    wsb1.ap().rearrange("p (c m) -> p c m", c=NCH),
                    dwk.ap().rearrange("p (c m) -> p c m", c=NCH)), inc="wl", n=16)
                if cfg["rope"]:
                    emit_rope(P, ph, qbuf, rp_ps, rtmp, c_qse, cs0, sn0)
                    qdep = None
                else:
                    qdep = lambda _hc: ("se", c_qse)
                _emit_m2(P, ph, lambda hc: qbuf[:, hc * T:(hc + 1) * T],
                         range(NCH), qm2, nps, ksq, dep_of=qdep)

                kdma = {}
                kv_done = {}
                for hc in range(NCH):
                    half = hc % 2
                    buf = kch[:, half * T:(half + 1) * T]
                    deps = []
                    if hc >= 2:
                        deps = [("st", kdma[hc - 2]), ("ve", kv_done[hc - 2])]
                    c_se = emit_proj_fm(P, ph, wsb1, c_wk, zbuf, buf, prj_ps,
                                        scale=scale, bias_sb=bias_k, deps=deps,
                                        hcs=[hc], outofs=lambda _hc: 0)
                    if cfg["rope"]:
                        c_rdy = _emit_rope_tiles(
                            P, ph, lambda tt, buf=buf: buf[:, tt * TTW:(tt + 1) * TTW],
                            rp_ps, rtmp, c_se, cs0, sn0)
                        kdep = ("ve", c_rdy)
                    else:
                        kdep = ("se", c_se)
                    c_m2 = _emit_m2(P, ph, lambda _hc, buf=buf: buf, [hc], km2,
                                    nps, ksq, dep_of=lambda _hc, kdep=kdep: kdep)
                    kv_done[hc] = c_m2
                    ph.wait("y", kdep[0], kdep[1])
                    kdma[hc] = ph.do("y", lambda e, hc=hc, buf=buf: e.dma_start(
                        ginK.ap()[hc * 128:(hc + 1) * 128, :], buf), inc="st", n=16)
                c_kcontrib = P.cnt["st"]

                ph.wait("g", "se", c_qse)
                c_wv = ph.do("g", lambda e: e.dma_start(
                    wsb0.ap().rearrange("p (c m) -> p c m", c=NCH),
                    dwv.ap().rearrange("p (c m) -> p c m", c=NCH)), inc="wl", n=16)
                vdma = {}
                base_vse = P.cnt["se"]
                for tk in range(16):
                    for hh in range(2):
                        i = tk * 2 + hh
                        bank = i % 2
                        if i == 0:
                            ph.wait("t", "wl", c_wv)
                        if i >= 2:
                            ph.wait("t", "se", base_vse + i - 1)
                        if tk >= 2 and hh == 0:
                            ph.wait("t", "st", vdma[tk - 2])
                        for cc in range(NCH):
                            last = (cc == NCH - 1 and vb_sb is None)
                            ph.do("t", lambda e, cc=cc, tk=tk, hh=hh, bank=bank, last=last:
                                  e.matmul(prj_ps[:, bank * 512:(bank + 1) * 512],
                                           zbuf[:, cc * T + tk * 128: cc * T + (tk + 1) * 128],
                                           wsb0[:, cc * D + hh * 512: cc * D + hh * 512 + 512],
                                           start=(cc == 0), stop=last),
                                  inc=(None if not last else "pe"))
                        if vb_sb is not None:
                            ph.do("t", lambda e, hh=hh, bank=bank: e.matmul(
                                prj_ps[:, bank * 512:(bank + 1) * 512],
                                ones_row_bf[:, :],
                                vb_sb[0:1, hh * 512:hh * 512 + 512],
                                start=False, stop=True, skip_group_check=True), inc="pe")
                        c_pe = P.cnt["pe"]
                        ph.wait("s", "pe", c_pe)
                        ph.do("s", lambda e, hh=hh, bank=bank, tk=tk: e.activation(
                            vstage[:, (tk % 2) * D + hh * 512:(tk % 2) * D + hh * 512 + 512],
                            prj_ps[:, bank * 512:(bank + 1) * 512], AF.Identity))
                        if hh == 1:
                            c_se = P.cnt["se"]
                            ph.wait("y", "se", c_se)
                            vdma[tk] = ph.do("y", lambda e, tk=tk: e.dma_start(
                                ginV.ap()[tk * 128:(tk + 1) * 128, :],
                                vstage[:, (tk % 2) * D:(tk % 2) * D + D]), inc="st", n=16)
                c_vcontrib = P.cnt["st"]

                c_km = P.cnt["ve"]
                ph.wait("y", "ve", c_km)
                c_kms = ph.do("y", lambda e: e.dma_start(gin_s[:, :], km2), inc="st", n=16)
                ph.wait("g", "st", c_kms)
                c_cc1 = ph.do("g", lambda e: e.collective_compute(
                    "AllReduce", ALU.max, replica_groups=[list(range(NCORES))],
                    ins=[gin_s.ap().opt()], outs=[gout_s.ap().opt()]), inc="cc")
                ph.wait("g", "st", c_kcontrib)
                c_cc2 = ph.do("g", lambda e: e.collective_compute(
                    "AllGather", ALU.bypass, replica_groups=[list(range(NCORES))],
                    ins=[ginK.ap().opt()], outs=[goutK.ap().opt()]), inc="cc")
                ph.wait("g", "st", c_vcontrib)
                c_cc3 = ph.do("g", lambda e: e.collective_compute(
                    "AllGather", ALU.bypass, replica_groups=[list(range(NCORES))],
                    ins=[ginV.ap().opt()], outs=[goutV.ap().opt()]), inc="cc")

                ph.wait("y", "cc", c_cc1)
                c_al = ph.do("y", lambda e: e.dma_start(km2g, gout_s[:, :]), inc="al", n=16)
                sh = Seq(ph, deps=[("al", c_al)])
                sh.do("v", lambda e: e.tensor_mul(t1, qm2, km2g))
                sh.do("s", lambda e: e.activation(t1, t1, AF.Sqrt))
                sh.do("v", lambda e: e.tensor_scalar(t1, t1, -1.0, None, ALU.mult))
                sh.do("t", lambda e: e.matmul(sps[:, :], ones_row[:, :], t1,
                                              start=True, stop=True))
                sh.do("v", lambda e: e.tensor_copy(negc_rep[:, :], sps[:, :]))
                ph.flush()
                if cfg.get("dump_a"):
                    dz = P.dout("dump_z", [128, NCH * T], BF16)
                    dq = P.dout("dump_q", [128, NCH * T], BF16)
                    dnc = P.dout("dump_negc", [128, 64], F32)
                    dgk = P.dout("dump_ginK", [D, T], BF16)
                    dgv = P.dout("dump_ginV", [T, D], BF16)
                    c = ph.do("y", lambda e: e.dma_start(dz[:, :], zbuf[:, :]), inc="st", n=16)
                    c = ph.do("y", lambda e: e.dma_start(dq[:, :], qbuf[:, :]), inc="st", n=16)
                    c = ph.do("y", lambda e: e.dma_start(dnc[:, :], negc_rep[:, :]), inc="st", n=16)
                    c = ph.do("y", lambda e: e.dma_start(dgk[:, :], ginK[:, :]), inc="st", n=16)
                    c = ph.do("y", lambda e: e.dma_start(dgv[:, :], ginV[:, :]), inc="st", n=16)
                    ph.wait("y", "st", c)
                    ph.flush()
                    return

        with nc.sbuf_tensor(f"ks0{uid}", [128, 2 * 4096], BF16) as ksb0, \
             nc.sbuf_tensor(f"ks1{uid}", [128, 2 * 4096], BF16) as ksb1, \
             nc.sbuf_tensor(f"vs0{uid}", [128, 32 * 256], BF16) as vsb0, \
             nc.sbuf_tensor(f"vs1{uid}", [128, 32 * 256], BF16) as vsb1, \
             nc.sbuf_tensor(f"pb{uid}", [128, 2 * 512], BF16) as psb, \
             nc.sbuf_tensor(f"abf{uid}", [128, NCH * 512], BF16) as abuf, \
             nc.sbuf_tensor(f"rr{uid}", [1, 512], F32) as rrow, \
             nc.sbuf_tensor(f"rs{uid}", [64, 512], F32) as rep_sb, \
             nc.sbuf_tensor(f"wo{uid}", [128, NCH * D], BF16) as wo_sb, \
             nc.psum_tensor(f"scp{uid}", [128, 1024], F32) as sc_ps, \
             nc.psum_tensor(f"avp{uid}", [128, 2048], F32) as av_ps, \
             nc.psum_tensor(f"rpp{uid}", [64, 512], F32) as rep_ps:
            c_wo = ph.do("g", lambda e: e.dma_start(
                wo_sb.ap().rearrange("p (c m) -> p c m", c=NCH),
                dwo.ap().rearrange("p (c m) -> p c m", c=NCH)), inc="wl", n=16)
            ksbs = (ksb0, ksb1)
            vsbs = (vsb0, vsb1)
            c_al_iter = [0] * 16
            c_iter_done = [0] * 16

            def issue_loads(i):
                b, hg = divmod(i, 4)
                if i >= 2:
                    ph.wait("y", "pe", c_iter_done[i - 2])
                if i == 0:
                    ph.wait("y", "cc", c_cc3)
                kap = goutK.ap().rearrange("(c d) t -> d c t", c=NCORES)
                for pr in range(2):
                    h0 = hg * 4 + 2 * pr
                    src = kap[h0 * 64: h0 * 64 + 128, :, b * 512:(b + 1) * 512]
                    dst = ksbs[i % 2].ap().rearrange("p (r t) -> p r t", r=2)[
                        :, pr:pr + 1, :].rearrange(
                        "p o (c t) -> p (o c) t", c=NCORES)
                    ph.do("y", lambda e, src=src, dst=dst: e.dma_start(dst, src),
                          inc="al", n=16)
                vap = goutV.ap().rearrange("(c t) h -> c t h", c=NCORES)
                for c in range(NCORES):
                    src = vap[c:c + 1, b * 512:(b + 1) * 512,
                              hg * 256:(hg + 1) * 256].rearrange(
                                  "o (tl p) h -> p (o tl) h", p=128)
                    dst = vsbs[i % 2].ap().rearrange("p (g h) -> p g h", g=32)[
                        :, c * 4:(c + 1) * 4, :]
                    ph.do("y", lambda e, src=src, dst=dst: e.dma_start(dst, src),
                          inc="al", n=16)
                c_al_iter[i] = P.cnt["al"]

            issue_loads(0)
            issue_loads(1)
            for i in range(16):
                b, hg = divmod(i, 4)
                ksb = ksbs[i % 2]
                vsb = vsbs[i % 2]
                c_prev_norm_ve = P.cnt["ve"]
                for j in range(4):
                    h = hg * 4 + j
                    idx = h * 4 + b
                    hb = (j % 2) * 64
                    pair = j // 2
                    qap = qbuf[(h % 2) * 64:(h % 2) * 64 + 64,
                               (h // 2) * T + b * 512:(h // 2) * T + b * 512 + 512]
                    if j == 0:
                        ph.wait("t", "al", c_al_iter[i])
                    ph.wait("t", "ve", c_prev_norm_ve)
                    c_exp = {}
                    c_av = {}
                    for g in range(32):
                        bank = g % 2
                        if g >= 2:
                            ph.wait("t", "se", c_exp[g - 2])
                        c_sc = ph.do("t", lambda e, g=g, bank=bank, pair=pair, hb=hb,
                                     qap=qap, ksb=ksb:
                                     e.matmul(sc_ps[:, bank * 512:(bank + 1) * 512],
                                              ksb[hb:hb + 64,
                                                  pair * 4096 + g * 128: pair * 4096 + (g + 1) * 128],
                                              qap, start=True, stop=True))
                        ph.wait("s", "pe", c_sc)
                        c_exp[g] = ph.do("s", lambda e, bank=bank, idx=idx: e.activation(
                            psb[:, bank * 512:(bank + 1) * 512],
                            sc_ps[:, bank * 512:(bank + 1) * 512], AF.Exp,
                            bias=negc_rep[:, idx:idx + 1]))
                        ph.wait("t", "se", c_exp[g])
                        ph.do("t", lambda e, g=g, j=j, bank=bank, vsb=vsb: e.matmul(
                            av_ps[0:64, j * 512:(j + 1) * 512],
                            vsb[:, g * 256 + j * 64: g * 256 + j * 64 + 64],
                            psb[:, bank * 512:(bank + 1) * 512],
                            start=(g == 0), stop=False), inc=None)
                        c_av[g] = ph.do("t", lambda e, g=g, j=j, bank=bank: e.matmul(
                            av_ps[64:65, j * 512:(j + 1) * 512],
                            ones_col_bf[:, :],
                            psb[:, bank * 512:(bank + 1) * 512],
                            start=(g == 0), stop=(g == 31), skip_group_check=True))
                    ph.wait("v", "pe", c_av[31])
                    c_rc = ph.do("v", lambda e, j=j: e.reciprocal(
                        rrow[:, :], av_ps[64:65, j * 512:(j + 1) * 512]))
                    ph.wait("t", "ve", c_rc)
                    c_rp = ph.do("t", lambda e: e.matmul(
                        rep_ps[:, :], ones_row[0:1, 0:64], rrow[:, :],
                        start=True, stop=True))
                    ph.wait("v", "pe", c_rp)
                    ph.do("v", lambda e: e.tensor_copy(rep_sb[:, :], rep_ps[:, :]))
                    hcd = h // 2
                    hbd = (h % 2) * 64
                    ph.do("v", lambda e, j=j, hcd=hcd, hbd=hbd: e.tensor_mul(
                        abuf[hbd:hbd + 64, hcd * 512:(hcd + 1) * 512],
                        av_ps[0:64, j * 512:(j + 1) * 512], rep_sb[:, :]))
                c_iter_done[i] = P.cnt["pe"]
                if i + 2 < 16:
                    issue_loads(i + 2)
                if hg == 3:
                    c_a = P.cnt["ve"]
                    c_evac = {}
                    for hc in range(NCH):
                        bank = hc % 2
                        if hc == 0:
                            ph.wait("t", "ve", c_a)
                            ph.wait("t", "wl", c_wo)
                        if hc >= 2:
                            ph.wait("t", "ve", c_evac[hc - 2])
                        for cc in range(NCH):
                            ph.do("t", lambda e, cc=cc, hc=hc, bank=bank: e.matmul(
                                sc_ps[:, bank * 512:(bank + 1) * 512],
                                wo_sb[:, cc * D + hc * 128: cc * D + (hc + 1) * 128],
                                abuf[:, cc * 512:(cc + 1) * 512],
                                start=(cc == 0), stop=(cc == NCH - 1)),
                                inc=(None if cc < NCH - 1 else "pe"))
                        c_pe = P.cnt["pe"]
                        ph.wait("v", "pe", c_pe)
                        c_evac[hc] = ph.do("v", lambda e, hc=hc, b=b, bank=bank:
                                           e.scalar_tensor_tensor(
                            x_fm[:, hc * T + b * 512: hc * T + b * 512 + 512],
                            sc_ps[:, bank * 512:(bank + 1) * 512],
                            bout_sb[:, hc:hc + 1],
                            x_fm[:, hc * T + b * 512: hc * T + b * 512 + 512],
                            ALU.add, ALU.add))
                    c_iter_done[i] = P.cnt["pe"]
            ph.flush()


CFG_L0A = dict(uid="A", kind="ln", eps=1e-5, gain=None, rope=False,
               wq="l0a_wqt", wk="l0a_wkt", wv="l0a_wvt", wo="l0a_wot",
               bout="l0a_bout", qb=None, kb=None, vb=None)
CFG_L0F_ATT = dict(uid="F", kind="rms", eps=1e-6, gain="l0f_n1w", rope=True,
                   wq="l0f_wqt", wk="l0f_wkt", wv="l0f_wvt", wo="l0f_wot",
                   bout="l0f_bout", qb="l0f_qb", kb="l0f_kb", vb="l0f_vb")


def emit_l0f_outer(P, ph):
    nc = P.nc
    x_fm = P.sb["x_fm"]
    xsp = P.dint("xspill", [128, NCH * T], F32)
    c = ph.do("y", lambda e: e.dma_start(xsp[:, :], x_fm[:, :]), inc="st", n=16)
    ph.wait("y", "st", c)
    ph.flush()
    emit_norm_fm(P, ph, x_fm, x_fm, "ln", 1e-5, uid="L0FO")
    ph.flush()


def emit_l0f_close(P, ph):
    nc = P.nc
    x_fm = P.sb["x_fm"]
    xsp = P.dram["xspill"]
    with nc.sbuf_tensor("xrel", [128, NCH * T], F32) as xr:
        c = ph.do("y", lambda e: e.dma_start(xr[:, :], xsp[:, :]), inc="al", n=16)
        ph.wait("v", "al", c)
        for cc in range(NCH):
            for tt in range(NTT):
                ph.do("v", lambda e, cc=cc, tt=tt: e.tensor_add(
                    x_fm[:, cc * T + tt * TTW: cc * T + (tt + 1) * TTW],
                    x_fm[:, cc * T + tt * TTW: cc * T + (tt + 1) * TTW],
                    xr[:, cc * T + tt * TTW: cc * T + (tt + 1) * TTW]))
        ph.flush()


def build_launch(phases):
    nc = bass.Bass("TRN2", target_bir_lowering=False)
    P = Prog(nc)
    ph = P.phase()
    emit_consts(P, ph)
    for p in phases:
        if p == "in":
            emit_input_convert(P, ph)
        elif p == "fmload":
            emit_fm_load(P, ph)
        elif p == "l0a":
            emit_attn(P, ph, CFG_L0A)
        elif p == "l0f_outer":
            emit_l0f_outer(P, ph)
        elif p == "l0f_att":
            emit_attn(P, ph, CFG_L0F_ATT)
        elif p == "l0f_moe":
            emit_moe_l0(P, ph)
        elif p == "l0f_close":
            emit_l0f_close(P, ph)
        elif p == "l1a":
            emit_l1_attn(P, ph)
        elif p == "l1f":
            emit_moe_l1(P, ph)
        elif p == "out":
            emit_output_convert(P, ph)
        elif p == "fmstore":
            emit_fm_store(P, ph)
        else:
            raise ValueError(p)
    return P


L0F = ["l0f_outer", "l0f_att", "l0f_moe", "l0f_close"]
PLAN = [["in", "l0a"] + L0F + ["l1a", "l1f", "out"]]


def kernel(**inputs):
    g, percore = host_prep(inputs)
    hand = None
    final = [None] * NCORES
    for phases in PLAN:
        P = build_launch(phases)
        innames = P.innames
        in_maps = []
        for c in range(NCORES):
            m = {}
            for n in innames:
                if n in percore[c]:
                    m[n] = percore[c][n]
                elif n == "xfm_in":
                    m[n] = hand[c]
                else:
                    m[n] = g[n]
            in_maps.append(m)
        res = run_bass_kernel_spmd(P.nc, in_maps, core_ids=list(range(NCORES)))
        r0 = res.results
        if "xfm_out" in r0[0]:
            hand = [r0[c]["xfm_out"] for c in range(NCORES)]
        if "x_out" in r0[0]:
            final = [r0[c]["x_out"] for c in range(NCORES)]
    return np.concatenate(final, axis=1)


def emit_moe_l0(P, ph):
    raise NotImplementedError


def emit_moe_l1(P, ph):
    raise NotImplementedError


def emit_l1_attn(P, ph):
    raise NotImplementedError


def emit_gating(P, ph, zget, gname, comb_fm, uid, tks, alpha_sb=None, tk0=0):
    nc = P.nc
    iota8v = P.sb["iota8v"]
    idn = P.sb["idn"]
    dg = P.din(gname, [128, NCH * 8], F32)
    with nc.sbuf_tensor(f"gw{uid}", [128, NCH * 8], BF16) as gsb, \
         nc.sbuf_tensor(f"gt{uid}", [128, 40], F32) as gt, \
         nc.sbuf_tensor(f"i100{uid}", [128, 8], F32) as i100, \
         nc.psum_tensor(f"gp{uid}", [128, 512], F32) as gps, \
         nc.psum_tensor(f"tp{uid}", [8, 1024], F32) as tps:
        c_g = ph.do("g", lambda e: e.dma_start(gsb[:, :], dg[:, :]), inc="wl", n=16)
        lg = gt[:, 0:8]; e1 = gt[:, 8:16]; t8 = gt[:, 16:24]
        m1 = gt[:, 24:25]; m2 = gt[:, 25:26]; p1 = gt[:, 26:27]; p2 = gt[:, 27:28]
        nm1 = gt[:, 28:29]; tx = gt[:, 29:30]; g1 = gt[:, 30:31]; g2 = gt[:, 31:32]
        mk1 = gt[:, 32:40]
        c_i = ph.do("v", lambda e: e.tensor_scalar(i100[:, :], iota8v[:, :], 100.0,
                                                   None, ALU.add))
        prev = {}
        for it, tk in enumerate(tks):
            deps = [("wl", c_g), ("ve", c_i)]
            if it >= 1:
                deps.append(("ve", prev[it - 1]))
            s_ = Seq(ph, deps=deps)
            for sem, v in deps:
                ph.wait("t", sem, v)
            for cc in range(NCH):
                ph.do("t", lambda e, cc=cc, tk=tk: e.matmul(
                    gps[:, 0:8], zget(cc, tk),
                    gsb[:, cc * 8:(cc + 1) * 8],
                    start=(cc == 0), stop=(cc == NCH - 1)),
                    inc=(None if cc < NCH - 1 else "pe"))
            s_.deps = [("pe", P.cnt["pe"])]
            s_.do("v", lambda e: e.tensor_copy(lg, gps[:, 0:8]))
            s_.do("v", lambda e: e.reduce_max(m1, lg, axis=AXX))
            s_.do("v", lambda e: e.tensor_scalar(e1, lg, m1, None, ALU.is_equal))
            s_.do("v", lambda e: e.scalar_tensor_tensor(t8, e1, -100.0, i100[:, :],
                                                        ALU.mult, ALU.add))
            s_.do("v", lambda e: e.tensor_reduce(p1, t8, AXX, ALU.min))
            s_.do("v", lambda e: e.tensor_scalar(mk1, iota8v[:, :], p1, None, ALU.is_equal))
            s_.do("v", lambda e: e.scalar_tensor_tensor(t8, mk1, -1e30, lg,
                                                        ALU.mult, ALU.add))
            s_.do("v", lambda e: e.reduce_max(m2, t8, axis=AXX))
            s_.do("v", lambda e: e.tensor_scalar(e1, t8, m2, None, ALU.is_equal))
            s_.do("v", lambda e: e.scalar_tensor_tensor(t8, e1, -100.0, i100[:, :],
                                                        ALU.mult, ALU.add))
            s_.do("v", lambda e: e.tensor_reduce(p2, t8, AXX, ALU.min))
            s_.do("v", lambda e: e.tensor_scalar(nm1, m1, -1.0, None, ALU.mult))
            s_.do("s", lambda e: e.activation(tx, m2, AF.Exp, bias=nm1))
            s_.do("v", lambda e: e.tensor_scalar(tx, tx, 1.0, None, ALU.add))
            s_.do("v", lambda e: e.reciprocal(g1, tx))
            s_.do("v", lambda e: e.tensor_scalar(g2, g1, -1.0, 1.0, ALU.mult, ALU.add))
            s_.do("v", lambda e: e.tensor_scalar(mk1, mk1, g1, None, ALU.mult))
            s_.do("v", lambda e: e.tensor_scalar(e1, iota8v[:, :], p2, None, ALU.is_equal))
            s_.do("v", lambda e: e.scalar_tensor_tensor(lg, e1, g2, mk1,
                                                        ALU.mult, ALU.add))
            if alpha_sb is not None:
                s_.do("v", lambda e: e.tensor_mul(lg, lg, alpha_sb[:, :]))
            bank = it % 2
            s_.do("t", lambda e, bank=bank: e.transpose(
                tps[:, bank * 512:bank * 512 + 128], lg, idn[:, :]))
            prev[it] = s_.do("v", lambda e, tk=tk, bank=bank: e.tensor_copy(
                comb_fm[0:8, (tk - tk0) * 128:(tk - tk0 + 1) * 128],
                tps[0:8, bank * 512:bank * 512 + 128]))
        ph.flush()
    return P.cnt["ve"]



def _moe_bufs(P, st, uid, NF, ntt, mode, blk1):
    nc = P.nc
    B = {}
    B["h"] = st.enter_context(nc.sbuf_tensor(f"h{uid}", [128, NF * ntt * TTW], BF16))
    B["w1"] = st.enter_context(nc.sbuf_tensor(f"w1{uid}", [128, 2 * blk1], BF16))
    B["w2"] = st.enter_context(nc.sbuf_tensor(f"w2{uid}", [128, 2 * blk1], BF16)) \
        if mode == "l0" else None
    B["w3"] = st.enter_context(nc.sbuf_tensor(f"w3{uid}", [128, 2 * NF * 128], BF16))
    B["b1"] = st.enter_context(nc.sbuf_tensor(f"b1{uid}", [128, NF], F32))
    B["b2"] = st.enter_context(nc.sbuf_tensor(f"b2{uid}", [128, NF], F32)) \
        if mode == "l0" else None
    B["b3"] = st.enter_context(nc.sbuf_tensor(f"b3{uid}", [1, D], BF16))
    B["cr"] = st.enter_context(nc.sbuf_tensor(f"cr{uid}", [1, ntt * TTW], BF16))
    B["cp"] = st.enter_context(nc.sbuf_tensor(f"cp{uid}", [128, ntt * TTW], F32))
    B["s1"] = st.enter_context(nc.sbuf_tensor(f"s1{uid}", [128, 2 * TTW], BF16))
    B["a2"] = st.enter_context(nc.sbuf_tensor(f"a2{uid}", [128, 2 * TTW], BF16)) \
        if mode == "l0" else None
    B["ze"] = st.enter_context(nc.sbuf_tensor(f"ze{uid}", [128, NCH * ntt * TTW], BF16)) \
        if mode == "l1" else None
    return B


def _emit_moe_experts(P, ph, B, zget, comb, x_fm, uid, NF, blk1,
                      dw1, dw2, dw3, db1, db2, db3, mode, tbase, ntt,
                      lnw=None, lnb=None, comb_base=None):
    nc = P.nc
    ones_row_bf = P.sb["ones_row_bf"]
    TGW = ntt * TTW
    h_sb = B["h"]; w1s = B["w1"]; w2s = B["w2"]; w3s = B["w3"]
    b1s = B["b1"]; b2s = B["b2"]; b3r = B["b3"]
    crow = B["cr"]; crep = B["cp"]; s1t = B["s1"]; a2t = B["a2"]; zes = B["ze"]
    with nc.psum_tensor(f"ap{uid}{tbase}", [128, 2048], F32) as aps, \
         nc.psum_tensor(f"yp{uid}{tbase}", [128, 1024], F32) as yps, \
         nc.psum_tensor(f"rp{uid}{tbase}", [128, 1024], F32) as rps:
        c_prev_y = 0
        c_prev_fc1 = 0
        c_prev_b = (0, 0)
        for e in range(MOE_NEXP):
            ph.wait("y", "se", c_prev_b[0])
            ph.wait("g", "pe", c_prev_b[1])
            c_b1 = ph.do("y", lambda e_, e=e: e_.dma_start(b1s[:, :], db1.ap()[e, :, :]),
                         inc="al", n=16)
            if mode == "l0":
                c_b1 = ph.do("y", lambda e_, e=e: e_.dma_start(b2s[:, :], db2.ap()[e, :, :]),
                             inc="al", n=16)
            c_b3 = ph.do("g", lambda e_, e=e: e_.dma_start(b3r[:, :], db3.ap()[e, :, :]),
                         inc="al", n=16)
            cb = tbase if comb_base is None else comb_base
            c_cr = ph.do("g", lambda e_, e=e, cb=cb: e_.dma_start(
                crow[:, :], comb[e:e + 1, cb:cb + TGW]), inc="al", n=16)
            ph.wait("t", "al", c_cr)
            ph.wait("t", "ve", P.cnt["ve"])
            c_crepq = {}
            for q in range(ntt):
                if q >= 2:
                    ph.wait("t", "ve", c_crepq[q - 2])
                ph.do("t", lambda e_, q=q: e_.matmul(
                    rps[:, (q % 2) * 512:(q % 2) * 512 + 512], ones_row_bf[:, :],
                    crow[0:1, q * TTW:(q + 1) * TTW], start=True, stop=True))
                c_pe = P.cnt["pe"]
                ph.wait("v", "pe", c_pe)
                c_crepq[q] = ph.do("v", lambda e_, q=q: e_.tensor_copy(
                    crep[:, q * TTW:(q + 1) * TTW],
                    rps[:, (q % 2) * 512:(q % 2) * 512 + 512]))
            c_creps = P.cnt["ve"]
            if mode == "l1":
                ph.wait("s", "pe", c_prev_fc1)
                ph.wait("s", "al", c_b1)
                for cc in range(NCH):
                    for q in range(ntt):
                        ph.do("s", lambda e_, cc=cc, q=q, e=e: e_.activation(
                            zes[:, cc * TGW + q * TTW: cc * TGW + (q + 1) * TTW],
                            zget(cc, q),
                            AF.Identity, bias=lnb[:, e * NCH + cc:e * NCH + cc + 1],
                            scale=lnw[:, e * NCH + cc:e * NCH + cc + 1]))
                c_ze = P.cnt["se"]
                zin = lambda cc, q: zes[:, cc * TGW + q * TTW: cc * TGW + (q + 1) * TTW]
            else:
                c_ze = 0
                zin = zget
            wdmas = {}

            def load_blk(fc, e=e):
                dep = wdmas.get(fc - 2)
                if dep is not None:
                    ph.wait("g", "pe", dep[1])
                c1 = ph.do("g", lambda e_, fc=fc, e=e: e_.dma_start(
                    w1s.ap().rearrange("p (h w) -> p h w", h=2)[:, fc % 2:fc % 2 + 1, :],
                    dw1.ap()[e, fc:fc + 1, :, :].rearrange("o p w -> p o w")),
                    inc="wl", n=16)
                if mode == "l0":
                    c1 = ph.do("g", lambda e_, fc=fc, e=e: e_.dma_start(
                        w2s.ap().rearrange("p (h w) -> p h w", h=2)[:, fc % 2:fc % 2 + 1, :],
                        dw2.ap()[e, fc:fc + 1, :, :].rearrange("o p w -> p o w")),
                        inc="wl", n=16)
                wdmas[fc] = (c1, 0)

            load_blk(0)
            load_blk(1)
            c_sil = {}
            c_a2c = {}
            c_hh = {}
            for fc in range(NF):
                for q in range(ntt):
                    i = fc * ntt + q
                    bank = i % 2
                    if i == 0:
                        ph.wait("t", "wl", wdmas[fc][0])
                        ph.wait("t", "al", c_b1)
                        ph.wait("t", "se", c_ze)
                        ph.wait("t", "ve", c_creps)
                    elif q == 0:
                        ph.wait("t", "wl", wdmas[fc][0])
                    if i >= 2:
                        ph.wait("t", "se", c_sil[i - 2])
                    for cc in range(NCH):
                        ph.do("t", lambda e_, cc=cc, fc=fc, q=q, bank=bank: e_.matmul(
                            aps[:, bank * 512:bank * 512 + 512],
                            w1s[:, (fc % 2) * blk1 + cc * 128:(fc % 2) * blk1 + (cc + 1) * 128],
                            zin(cc, q),
                            start=(cc == 0), stop=(cc == NCH - 1)),
                            inc=(None if cc < NCH - 1 else "pe"))
                    c_a1 = P.cnt["pe"]
                    if mode == "l0":
                        if i >= 2:
                            ph.wait("t", "ve", c_a2c[i - 2])
                        for cc in range(NCH):
                            ph.do("t", lambda e_, cc=cc, fc=fc, q=q, bank=bank: e_.matmul(
                                aps[:, 1024 + bank * 512:1024 + bank * 512 + 512],
                                w2s[:, (fc % 2) * blk1 + cc * 128:(fc % 2) * blk1 + (cc + 1) * 128],
                                zin(cc, q),
                                start=(cc == 0), stop=(cc == NCH - 1)),
                                inc=(None if cc < NCH - 1 else "pe"))
                        c_a2 = P.cnt["pe"]
                    hofs = fc * TGW + q * TTW
                    ph.wait("s", "pe", c_a1)
                    if i >= 2:
                        ph.wait("s", "ve", c_hh[i - 2])
                    if mode == "l0":
                        c_sil[i] = ph.do("s", lambda e_, fc=fc, bank=bank, i=i: e_.activation(
                            s1t[:, (i % 2) * TTW:(i % 2 + 1) * TTW],
                            aps[:, bank * 512:bank * 512 + 512], AF.Silu,
                            bias=b1s[:, fc:fc + 1]))
                        ph.wait("v", "pe", c_a2)
                        c_a2c[i] = ph.do("v", lambda e_, fc=fc, q=q, bank=bank, i=i:
                                         e_.scalar_tensor_tensor(
                            a2t[:, (i % 2) * TTW:(i % 2 + 1) * TTW],
                            aps[:, 1024 + bank * 512:1024 + bank * 512 + 512],
                            b2s[:, fc:fc + 1],
                            crep[:, q * TTW:(q + 1) * TTW], ALU.add, ALU.mult))
                        ph.wait("v", "se", c_sil[i])
                        c_hh[i] = ph.do("v", lambda e_, hofs=hofs, i=i: e_.tensor_mul(
                            h_sb[:, hofs:hofs + TTW],
                            s1t[:, (i % 2) * TTW:(i % 2 + 1) * TTW],
                            a2t[:, (i % 2) * TTW:(i % 2 + 1) * TTW]))
                    else:
                        c_sil[i] = ph.do("s", lambda e_, fc=fc, bank=bank, i=i: e_.activation(
                            s1t[:, (i % 2) * TTW:(i % 2 + 1) * TTW],
                            aps[:, bank * 512:bank * 512 + 512], AF.Gelu_apprx_tanh,
                            bias=b1s[:, fc:fc + 1]))
                        ph.wait("v", "se", c_sil[i])
                        c_hh[i] = ph.do("v", lambda e_, hofs=hofs, q=q, i=i: e_.tensor_mul(
                            h_sb[:, hofs:hofs + TTW],
                            s1t[:, (i % 2) * TTW:(i % 2 + 1) * TTW],
                            crep[:, q * TTW:(q + 1) * TTW]))
                if fc + 2 < NF:
                    wdmas[fc] = (wdmas[fc][0], P.cnt["pe"])
                    load_blk(fc + 2)
            c_prev_fc1 = P.cnt["pe"]
            c_hall = P.cnt["ve"]
            if MOE_SKIP_FC3:
                c_prev_y = P.cnt["pe"]
                c_prev_b = (P.cnt["se"], P.cnt["pe"])
                continue
            w3dmas = {}

            def load_w3(dc, e=e):
                dep = w3dmas.get(dc - 2)
                if dep is not None:
                    ph.wait("g", "pe", dep[1])
                c3 = ph.do("g", lambda e_, dc=dc, e=e: e_.dma_start(
                    w3s.ap().rearrange("p (h w) -> p h w", h=2)[:, dc % 2:dc % 2 + 1, :],
                    dw3.ap()[e, dc:dc + 1, :, :].rearrange("o p w -> p o w")),
                    inc="wl", n=16)
                w3dmas[dc] = (c3, 0)

            load_w3(0)
            load_w3(1)
            c_ev = {}
            for dc in range(NCH):
                for q in range(ntt):
                    i2 = dc * ntt + q
                    bank = i2 % 2
                    if q == 0:
                        ph.wait("t", "wl", w3dmas[dc][0])
                    if i2 == 0:
                        ph.wait("t", "ve", c_hall)
                        ph.wait("t", "al", c_b3)
                    if i2 >= 2:
                        ph.wait("t", "ve", c_ev[i2 - 2])
                    for fcc in range(NF):
                        ph.do("t", lambda e_, fcc=fcc, dc=dc, q=q, bank=bank: e_.matmul(
                            yps[:, bank * 512:bank * 512 + 512],
                            w3s[:, (dc % 2) * NF * 128 + fcc * 128:(dc % 2) * NF * 128 + (fcc + 1) * 128],
                            h_sb[:, fcc * TGW + q * TTW: fcc * TGW + (q + 1) * TTW],
                            start=(fcc == 0), stop=False), inc=None)
                    ph.do("t", lambda e_, dc=dc, q=q, bank=bank: e_.matmul(
                        yps[:, bank * 512:bank * 512 + 512],
                        b3r[0:1, dc * 128:(dc + 1) * 128],
                        crow[0:1, q * TTW:(q + 1) * TTW],
                        start=False, stop=True, skip_group_check=True), inc="pe")
                    c_y = P.cnt["pe"]
                    ph.wait("v", "pe", c_y)
                    c_ev[i2] = ph.do("v", lambda e_, dc=dc, q=q, bank=bank, tbase=tbase:
                                     e_.scalar_tensor_tensor(
                        x_fm[:, dc * T + tbase + q * TTW: dc * T + tbase + (q + 1) * TTW],
                        yps[:, bank * 512:bank * 512 + 512], 1.0,
                        x_fm[:, dc * T + tbase + q * TTW: dc * T + tbase + (q + 1) * TTW],
                        ALU.mult, ALU.add))
                if dc + 2 < NCH:
                    w3dmas[dc] = (w3dmas[dc][0], P.cnt["pe"])
                    load_w3(dc + 2)
            c_prev_y = P.cnt["pe"]
            c_prev_b = (P.cnt["se"], P.cnt["pe"])


def emit_moe_l0(P, ph):
    from contextlib import ExitStack
    nc = P.nc
    x_fm = P.sb["x_fm"]
    NF = 16
    dn2 = P.din("l0f_n2w", [128, NCH], F32)
    dw1 = P.din("l0f_w1b", [8, NF, 128, NCH * 128], F32)
    dw2 = P.din("l0f_w2b", [8, NF, 128, NCH * 128], F32)
    dw3 = P.din("l0f_w3b", [8, NCH, 128, NF * 128], F32)
    db1 = P.din("l0f_b1", [8, 128, NF], F32)
    db2 = P.din("l0f_b2", [8, 128, NF], F32)
    db3 = P.din("l0f_b3", [8, 1, D], F32)
    with ExitStack() as st:
        zp = st.enter_context(nc.sbuf_tensor("l0m_z", [128, NCH * T], BF16))
        comb = st.enter_context(nc.sbuf_tensor("l0m_c", [8, T], F32))
        n2w = st.enter_context(nc.sbuf_tensor("l0m_n2", [128, NCH], F32))
        c_n2 = ph.do("y", lambda e: e.dma_start(n2w[:, :], dn2[:, :]), inc="al", n=16)
        ph.wait("v", "al", c_n2)
        emit_norm_fm(P, ph, x_fm, zp, "rms", 1e-6, w_sb=n2w, uid="l0m")
        emit_gating(P, ph,
                    lambda cc, tk: zp[:, cc * T + tk * 128: cc * T + (tk + 1) * 128],
                    "l0f_gatet", comb, "l0m", tks=range(16))
        if MOE_DBG:
            dz = P.dout("dump_mz", [128, NCH * T], BF16)
            dc_ = P.dout("dump_comb", [8, T], F32)
            c = ph.do("y", lambda e: e.dma_start(dz[:, :], zp[:, :]), inc="st", n=16)
            c = ph.do("y", lambda e: e.dma_start(dc_[:, :], comb[:, :]), inc="st", n=16)
            ph.wait("y", "st", c)
            ph.flush()
            return
        B = _moe_bufs(P, st, "l0m", NF, NTT, "l0", NCH * 128)
        _emit_moe_experts(P, ph, B,
                          lambda cc, q: zp[:, cc * T + q * TTW: cc * T + (q + 1) * TTW],
                          comb, x_fm, "l0m", NF, NCH * 128,
                          dw1, dw2, dw3, db1, db2, db3, "l0", tbase=0, ntt=NTT)
        ph.flush()


def emit_moe_l1(P, ph):
    from contextlib import ExitStack
    nc = P.nc
    x_fm = P.sb["x_fm"]
    ones_row = P.sb["ones_row"]
    NF = 32
    NTG = 4
    NTTG = 1
    dw1 = P.din("l1f_fc1b", [8, NF, 128, NCH * 128], F32)
    dw3 = P.din("l1f_fc2b", [8, NCH, 128, NF * 128], F32)
    db1 = P.din("l1f_b1", [8, 128, NF], F32)
    db3 = P.din("l1f_b2", [8, 1, D], F32)
    dlnw = P.din("l1f_lnw", [8, 128, NCH], F32)
    dlnb = P.din("l1f_lnb", [8, 128, NCH], F32)
    dalpha = P.din("l1f_alpha", [1, 8], F32)
    with ExitStack() as st:
        zp = st.enter_context(nc.sbuf_tensor("l1m_z", [128, NCH * NTTG * TTW], BF16))
        comb = st.enter_context(nc.sbuf_tensor("l1m_c", [8, NTTG * TTW], F32))
        lnw = st.enter_context(nc.sbuf_tensor("l1m_lw", [128, 8 * NCH], F32))
        lnb = st.enter_context(nc.sbuf_tensor("l1m_lb", [128, 8 * NCH], F32))
        arow = st.enter_context(nc.sbuf_tensor("l1m_ar", [1, 8], F32))
        alph = st.enter_context(nc.sbuf_tensor("l1m_al", [128, 8], F32))
        ph.do("y", lambda e: e.dma_start(
            lnw.ap().rearrange("p (e c) -> p e c", e=8),
            dlnw.ap().rearrange("e p c -> p e c")), inc="al", n=16)
        ph.do("y", lambda e: e.dma_start(
            lnb.ap().rearrange("p (e c) -> p e c", e=8),
            dlnb.ap().rearrange("e p c -> p e c")), inc="al", n=16)
        c_a = ph.do("y", lambda e: e.dma_start(arow[:, :], dalpha[:, :]), inc="al", n=16)
        with nc.psum_tensor("l1m_alp", [128, 64], F32) as alps:
            sq = Seq(ph, deps=[("al", c_a)])
            sq.do("t", lambda e: e.matmul(alps[:, 0:8], ones_row[:, :], arow[:, :],
                                          start=True, stop=True))
            c_alph = sq.do("v", lambda e: e.tensor_copy(alph[:, :], alps[:, 0:8]))
        if True:
            B = _moe_bufs(P, st, "l1m", NF, NTTG, "l1", NCH * 128)
            for tg in range(NTG):
                tbase = tg * NTTG * TTW
                tts = [tg * NTTG + k for k in range(NTTG)]
                emit_norm_fm(P, ph, x_fm, zp, "ln", 1e-5, uid=f"l1o{tg}",
                             tts=tts, dst_stride=NTTG * TTW)
                emit_gating(P, ph,
                            lambda cc, tk, tbase=tbase: zp[:, cc * NTTG * TTW + tk * 128 - tbase:
                                                           cc * NTTG * TTW + (tk + 1) * 128 - tbase],
                            "l1f_wg", comb, f"l1m{tg}",
                            tks=range(tg * NTTG * 4, (tg + 1) * NTTG * 4),
                            alpha_sb=alph, tk0=tg * NTTG * 4)
                emit_norm_fm(P, ph, zp, zp, "ln", 1e-5, uid=f"l1i{tg}",
                             tts=list(range(NTTG)), dst_stride=NTTG * TTW,
                             src_stride=NTTG * TTW)
                _emit_moe_experts(P, ph, B,
                                  lambda cc, q: zp[:, cc * NTTG * TTW + q * TTW:
                                                   cc * NTTG * TTW + (q + 1) * TTW],
                                  comb, x_fm, f"l1m{tg}", NF, NCH * 128,
                                  dw1, None, dw3, db1, None, db3, "l1",
                                  tbase=tbase, ntt=NTTG, lnw=lnw, lnb=lnb, comb_base=0)
        ph.flush()


KK_PAIRS = [(0, 1), (0, 2), (1, 2), (0, 3), (1, 3), (2, 3)]
KQ_PAIRS = [(0, 0), (0, 1), (1, 1), (0, 2), (1, 2), (2, 2),
            (0, 3), (1, 3), (2, 3), (3, 3)]


def emit_l1_attn(P, ph):
    from contextlib import ExitStack
    nc = P.nc
    x_fm = P.sb["x_fm"]
    ones_col = P.sb["ones_col"]; ones_row = P.sb["ones_row"]
    ones_col_bf = P.sb["ones_col_bf"]; ones_row_bf = P.sb["ones_row_bf"]
    rollp = P.sb["rollp"]
    dwq = P.din("l1a_wq", [128, NCH * D], F32)
    dwk = P.din("l1a_wk", [128, NCH * D], F32)
    dwv = P.din("l1a_wv", [128, NCH * D], F32)
    dwb = P.din("l1a_wbm", [128, NCH * 16], F32)
    dwo = P.din("l1a_wo", [128, NCH * D], F32)
    dbo = P.din("l1a_bo", [128, NCH], F32)

    dy = P.dint("l1a_y", [D, T], BF16)
    with ExitStack() as st:
        zbuf = st.enter_context(nc.sbuf_tensor("l1a_z", [128, NCH * T], BF16))
        beta_fm = st.enter_context(nc.sbuf_tensor("l1a_bfs", [16, T], BF16))
        brow = st.enter_context(nc.sbuf_tensor("l1a_br", [1, T], F32))
        rows = st.enter_context(nc.sbuf_tensor("l1a_rw", [1, 5 * TTW], F32))
        rowsb = st.enter_context(nc.sbuf_tensor("l1a_rwb", [1, 16 * TTW], BF16))
        bo_sb = st.enter_context(nc.sbuf_tensor("l1a_bos", [128, NCH], F32))
        c_bo = ph.do("y", lambda e: e.dma_start(bo_sb[:, :], dbo[:, :]), inc="al", n=16)

        emit_norm_fm(P, ph, x_fm, zbuf, "ln", 1e-5, uid="l1a")
        c_z = P.cnt["ve"]

        with nc.sbuf_tensor("l1a_wbs", [128, NCH * 16], BF16) as wbs, \
             nc.psum_tensor("l1a_bp", [16, 512], F32) as bps:
            c_wb = ph.do("g", lambda e: e.dma_start(wbs[:, :], dwb[:, :]), inc="wl", n=16)
            base_se = P.cnt["se"]
            for tt in range(NTT):
                if tt == 0:
                    ph.wait("t", "wl", c_wb)
                    ph.wait("t", "ve", c_z)
                if tt >= 1:
                    ph.wait("t", "se", base_se + tt)
                for cc in range(NCH):
                    ph.do("t", lambda e, cc=cc, tt=tt: e.matmul(
                        bps[:, :], wbs[:, cc * 16:(cc + 1) * 16],
                        zbuf[:, cc * T + tt * TTW: cc * T + (tt + 1) * TTW],
                        start=(cc == 0), stop=(cc == NCH - 1)),
                        inc=(None if cc < NCH - 1 else "pe"))
                c_pe = P.cnt["pe"]
                ph.wait("s", "pe", c_pe)
                ph.do("s", lambda e, tt=tt: e.activation(
                    beta_fm[:, tt * TTW:(tt + 1) * TTW], bps[:, :], AF.Sigmoid))
            c_brow = P.cnt["se"]
            ph.flush()

        with nc.sbuf_tensor("l1a_wp", [128, 2 * 3 * NCH * 128], BF16) as wpair, \
             nc.sbuf_tensor("l1a_qkv", [128, 3 * T], BF16) as qkv, \
             nc.sbuf_tensor("l1a_cat", [128, 2 * T], BF16) as cat, \
             nc.sbuf_tensor("l1a_phi", [128, T], BF16) as phif, \
             nc.sbuf_tensor("l1a_kphi", [128, T], BF16) as kphi, \
             nc.sbuf_tensor("l1a_qphi", [128, T], BF16) as qphi, \
             nc.sbuf_tensor("l1a_dp", [128, TTW], F32) as dprod, \
             nc.sbuf_tensor("l1a_u", [64, 6 * TTW], F32) as ubuf, \
             nc.sbuf_tensor("l1a_yt", [64, TTW], BF16) as ytile, \
             nc.sbuf_tensor("l1a_vt", [64, TTW], BF16) as vtile, \
             nc.psum_tensor("l1a_pj", [128, 1024], F32) as prj_ps, \
             nc.psum_tensor("l1a_rp", [128, 1024], F32) as rp_ps, \
             nc.psum_tensor("l1a_np", [1, 1024], F32) as nps, \
             nc.psum_tensor("l1a_rr", [128, 1024], F32) as rep_ps:
            arow = lambda i: rowsb[0:1, KK_PAIRS.index(i) * TTW:
                                   (KK_PAIRS.index(i) + 1) * TTW]
            crow_ = lambda i: rowsb[0:1, (6 + KQ_PAIRS.index(i)) * TTW:
                                    (6 + KQ_PAIRS.index(i) + 1) * TTW]
            qrrow = lambda s: rows[0:1, s * TTW:(s + 1) * TTW]
            mrow = rows[0:1, 4 * TTW:5 * TTW]
            PW = NCH * 128

            wdma = {}

            def load_pair(pr):
                half = pr % 2
                dep = wdma.get(pr - 2)
                if dep is not None:
                    ph.wait("g", "pe", dep[1])
                c = 0
                for wi, dw in enumerate((dwq, dwk, dwv)):
                    c = ph.do("g", lambda e, dw=dw, half=half, wi=wi, pr=pr: e.dma_start(
                        wpair.ap().rearrange("p (h w c) -> p h w c", h=2, w=3)[
                            :, half, wi, :].rearrange("p (cc m) -> p cc m", cc=NCH),
                        dw.ap().rearrange("p (cc m) -> p cc m", cc=NCH)[
                            :, :, pr * 128:(pr + 1) * 128]), inc="wl", n=16)
                wdma[pr] = (c, 0)

            load_pair(0)
            load_pair(1)
            c_prev_heads = c_z
            for pr in range(8):
                half = pr % 2
                base_se = P.cnt["se"]
                i = 0
                for wi in range(3):
                    for tt in range(NTT):
                        bank = i % 2
                        if i == 0:
                            ph.wait("t", "wl", wdma[pr][0])
                            ph.wait("t", "ve", c_prev_heads)
                        if i >= 2:
                            ph.wait("t", "se", base_se + i - 1)
                        for cc in range(NCH):
                            ph.do("t", lambda e, cc=cc, wi=wi, tt=tt, bank=bank, half=half:
                                  e.matmul(
                                      prj_ps[:, bank * 512:(bank + 1) * 512],
                                      wpair[:, half * 3 * PW + wi * PW + cc * 128:
                                            half * 3 * PW + wi * PW + (cc + 1) * 128],
                                      zbuf[:, cc * T + tt * TTW: cc * T + (tt + 1) * TTW],
                                      start=(cc == 0), stop=(cc == NCH - 1)),
                                  inc=(None if cc < NCH - 1 else "pe"))
                        c_pe = P.cnt["pe"]
                        ph.wait("s", "pe", c_pe)
                        ph.do("s", lambda e, wi=wi, tt=tt, bank=bank: e.activation(
                            qkv[:, wi * T + tt * TTW: wi * T + (tt + 1) * TTW],
                            prj_ps[:, bank * 512:(bank + 1) * 512], AF.Identity))
                        i += 1
                wdma[pr] = (wdma[pr][0], P.cnt["pe"])
                if pr + 2 < 8:
                    load_pair(pr + 2)
                c_proj = P.cnt["se"]

                sq = Seq(ph, deps=[("se", c_proj)])
                for hj in range(2):
                    h = pr * 2 + hj
                    hb = hj * 64
                    sq.do("g", lambda e, h=h: e.dma_start(
                        brow[:, :], beta_fm[h:h + 1, :]), inc="al", n=16)
                    for which, (src_off, phidst) in enumerate(
                            ((T, kphi), (0, qphi))):
                        sq.do("s", lambda e, src_off=src_off, hb=hb: e.activation(
                            cat[0:64, 0:T], qkv[src_off + 0:src_off + 0][0:0, 0:0]
                            if False else qkv[hb:hb + 64, src_off:src_off + T],
                            AF.Relu))
                        sq.do("s", lambda e, src_off=src_off, hb=hb: e.activation(
                            cat[64:128, 0:T], qkv[hb:hb + 64, src_off:src_off + T],
                            AF.Relu, scale=-1.0))
                        for tt in range(NTT):
                            bank = tt % 2
                            sq.do("t", lambda e, tt=tt, bank=bank: e.matmul(
                                rp_ps[:, bank * 512:(bank + 1) * 512], rollp[:, :],
                                cat[:, tt * TTW:(tt + 1) * TTW], start=True, stop=True))
                            sq.do("v", lambda e, tt=tt, bank=bank: e.tensor_mul(
                                phif[:, tt * TTW:(tt + 1) * TTW],
                                cat[:, tt * TTW:(tt + 1) * TTW],
                                rp_ps[:, bank * 512:(bank + 1) * 512]))
                            sq.do("t", lambda e, tt=tt, bank=bank: e.matmul(
                                nps[:, bank * 512:(bank + 1) * 512], ones_col_bf[:, :],
                                phif[:, tt * TTW:(tt + 1) * TTW], start=True, stop=True))
                            sq.do("v", lambda e, tt=tt, bank=bank: e.tensor_scalar(
                                mrow[:, tt * 128: tt * 128 + 512][0:1, 0:512]
                                if False else mrow,
                                nps[0:1, bank * 512:(bank + 1) * 512], 1e-6, None,
                                ALU.add))
                            sq.do("v", lambda e, which=which, tt=tt: e.reciprocal(
                                (qrrow(tt) if which == 1 else mrow), mrow))
                            if which == 0:
                                sq.do("t", lambda e, tt=tt, bank=bank: e.matmul(
                                    rep_ps[:, bank * 512:(bank + 1) * 512][0:64, :]
                                    if False else rep_ps[0:64, bank * 512:(bank + 1) * 512],
                                    ones_row[0:1, 0:64], mrow, start=True, stop=True))
                                sq.do("t", lambda e, tt=tt, bank=bank: e.matmul(
                                    rep_ps[64:128, bank * 512:(bank + 1) * 512],
                                    ones_row[0:1, 64:128], mrow, start=True, stop=True))
                                sq.do("v", lambda e, tt=tt, bank=bank: e.tensor_mul(
                                    kphi[:, tt * TTW:(tt + 1) * TTW],
                                    phif[:, tt * TTW:(tt + 1) * TTW],
                                    rp_ps[:, bank * 512:(bank + 1) * 512] if False
                                    else rep_ps[:, bank * 512:(bank + 1) * 512]))
                            else:
                                sq.do("v", lambda e, tt=tt: e.tensor_copy(
                                    qphi[:, tt * TTW:(tt + 1) * TTW],
                                    phif[:, tt * TTW:(tt + 1) * TTW]))
                    for (i_, s_) in KK_PAIRS:
                        sq.do("v", lambda e, i_=i_, s_=s_: e.tensor_mul(
                            dprod[:, :], kphi[:, i_ * TTW:(i_ + 1) * TTW],
                            kphi[:, s_ * TTW:(s_ + 1) * TTW]))
                        sq.do("t", lambda e: e.matmul(
                            nps[:, 0:512], ones_col[:, :], dprod[:, :],
                            start=True, stop=True))
                        sq.do("v", lambda e, i_=i_, s_=s_: e.tensor_copy(
                            arow((i_, s_)), nps[0:1, 0:512]))
                    for (i_, s_) in KQ_PAIRS:
                        sq.do("v", lambda e, i_=i_, s_=s_: e.tensor_mul(
                            dprod[:, :], kphi[:, i_ * TTW:(i_ + 1) * TTW],
                            qphi[:, s_ * TTW:(s_ + 1) * TTW]))
                        sq.do("t", lambda e: e.matmul(
                            nps[:, 0:512], ones_col[:, :], dprod[:, :],
                            start=True, stop=True))
                        sq.do("v", lambda e, i_=i_, s_=s_: e.tensor_copy(
                            crow_((i_, s_)), nps[0:1, 0:512]))

                    u = lambda i: ubuf[0:64, i * TTW:(i + 1) * TTW]
                    tmp = ubuf[0:64, 4 * TTW:5 * TTW]
                    yac = ubuf[0:64, 5 * TTW:6 * TTW]
                    vsl = lambda s: qkv[hb:hb + 64, 2 * T + s * TTW: 2 * T + (s + 1) * TTW]

                    def rep64(row_ap, bank, bf=False):
                        orow = ones_row_bf if bf else ones_row
                        sq.do("t", lambda e, row_ap=row_ap, bank=bank, orow=orow:
                              e.matmul(rep_ps[0:64, bank * 512:(bank + 1) * 512],
                                       orow[0:1, 0:64], row_ap, start=True, stop=True))
                        return rep_ps[0:64, bank * 512:(bank + 1) * 512]

                    def betarow(s):
                        return brow[0:1, s * TTW:(s + 1) * TTW]

                    for s_ in range(NB):
                        sq.do("v", lambda e, s_=s_: e.tensor_copy(vtile[:, :], vsl(s_)))
                        if s_ == 0:
                            r = rep64(betarow(0), 0)
                            sq.do("v", lambda e, r=r: e.tensor_mul(u(0), vtile[:, :], r))
                        else:
                            for ii in range(s_):
                                r = rep64(arow((ii, s_)), ii % 2, bf=True)
                                if ii == 0:
                                    sq.do("v", lambda e, r=r, ii=ii: e.tensor_mul(
                                        tmp, u(ii), r))
                                else:
                                    sq.do("v", lambda e, r=r, ii=ii: e.tensor_mul(
                                        dprod[0:64, :], u(ii), r))
                                    sq.do("v", lambda e: e.tensor_add(
                                        tmp, tmp, dprod[0:64, :]))
                            sq.do("v", lambda e: e.tensor_sub(tmp, vtile[:, :], tmp))
                            r = rep64(betarow(s_), s_ % 2)
                            sq.do("v", lambda e, r=r, s_=s_: e.tensor_mul(u(s_), tmp, r))
                        for ii in range(s_ + 1):
                            r = rep64(crow_((ii, s_)), ii % 2, bf=True)
                            if ii == 0:
                                sq.do("v", lambda e, r=r, ii=ii: e.tensor_mul(
                                    yac, u(ii), r))
                            else:
                                sq.do("v", lambda e, r=r, ii=ii: e.tensor_mul(
                                    dprod[0:64, :], u(ii), r))
                                sq.do("v", lambda e: e.tensor_add(
                                    yac, yac, dprod[0:64, :]))
                        r = rep64(qrrow(s_), s_ % 2)
                        sq.do("v", lambda e, r=r: e.tensor_mul(ytile[:, :], yac, r))
                        sq.do("y", lambda e, h=h, s_=s_: e.dma_start(
                            dy.ap()[h * 64:(h + 1) * 64, s_ * TTW:(s_ + 1) * TTW],
                            ytile[:, :]), inc="st", n=16)
                c_prev_heads = P.cnt["ve"]
            ph.flush()

        with nc.sbuf_tensor("l1a_wos", [128, NCH * D], BF16) as wo_sb, \
             nc.sbuf_tensor("l1a_ab", [128, 2 * NCH * 512], BF16) as absl, \
             nc.psum_tensor("l1a_op", [128, 1024], F32) as op_ps:
            c_wo = ph.do("g", lambda e: e.dma_start(
                wo_sb.ap().rearrange("p (c m) -> p c m", c=NCH),
                dwo.ap().rearrange("p (c m) -> p c m", c=NCH)), inc="wl", n=16)
            c_y = P.cnt["st"]
            c_ldt = {}
            c_tt_pe = {}

            def load_a(tt):
                ph.wait("y", "st", c_y)
                if tt >= 2:
                    ph.wait("y", "pe", c_tt_pe[tt - 2])
                c_ldt[tt] = ph.do("y", lambda e, tt=tt: e.dma_start(
                    absl.ap().rearrange("p (h c w) -> p h c w", h=2, c=NCH)[
                        :, tt % 2, :, :],
                    dy.ap().rearrange("(c p) t -> p c t", p=128)[
                        :, :, tt * TTW:(tt + 1) * TTW]), inc="al", n=16)

            load_a(0)
            load_a(1)
            c_ev = {}
            for tt in range(NTT):
                for hc in range(NCH):
                    i = tt * NCH + hc
                    bank = i % 2
                    if hc == 0:
                        ph.wait("t", "al", c_ldt[tt])
                        if tt == 0:
                            ph.wait("t", "wl", c_wo)
                            ph.wait("t", "al", c_bo)
                    if i >= 2:
                        ph.wait("t", "ve", c_ev[i - 2])
                    for cc in range(NCH):
                        ph.do("t", lambda e, cc=cc, hc=hc, tt=tt, bank=bank: e.matmul(
                            op_ps[:, bank * 512:(bank + 1) * 512],
                            wo_sb[:, cc * D + hc * 128: cc * D + (hc + 1) * 128],
                            absl[:, (tt % 2) * NCH * 512 + cc * 512:
                                 (tt % 2) * NCH * 512 + (cc + 1) * 512],
                            start=(cc == 0), stop=(cc == NCH - 1)),
                            inc=(None if cc < NCH - 1 else "pe"))
                    c_pe = P.cnt["pe"]
                    ph.wait("v", "pe", c_pe)
                    c_ev[i] = ph.do("v", lambda e, hc=hc, tt=tt, bank=bank:
                                    e.scalar_tensor_tensor(
                        x_fm[:, hc * T + tt * TTW: hc * T + (tt + 1) * TTW],
                        op_ps[:, bank * 512:(bank + 1) * 512],
                        bo_sb[:, hc:hc + 1],
                        x_fm[:, hc * T + tt * TTW: hc * T + (tt + 1) * TTW],
                        ALU.add, ALU.add))
                c_tt_pe[tt] = P.cnt["pe"]
                if tt + 2 < NTT:
                    load_a(tt + 2)
            ph.flush()


# version 02
# speedup vs baseline: 1.2590x; 1.2590x over previous
import numpy as np
import ml_dtypes
import concourse.bass as bass
import concourse.mybir as mybir
from concourse.bass_utils import run_bass_kernel_spmd

F32 = mybir.dt.float32
BF16 = mybir.dt.bfloat16
AF = mybir.ActivationFunctionType
ALU = mybir.AluOpType
AXX = mybir.AxisListType.X

NCORES = 8
NB = 4
NL = 512
T = NB * NL
D = 1024
NCH = D // 128
NTT = 4
TTW = 512
SEQ = 4096
NH = 16
HD = 64
VDIM = NH * 65

BF = ml_dtypes.bfloat16
MOE_DBG = False
MOE_NEXP = 8
MOE_SKIP_FC3 = False
ESEM = {"t": "pe", "v": "ve", "s": "se", "g": "ge"}



def _blocks(wT, mblk=128):
    K, M = wT.shape
    kc = K // 128
    nb = M // mblk
    return np.ascontiguousarray(
        wT.reshape(kc, 128, nb, mblk).transpose(2, 1, 0, 3).reshape(nb, 128, kc * mblk))


def _chunkmajor(wT):
    K, M = wT.shape
    kc = K // 128
    return np.ascontiguousarray(wT.reshape(kc, 128, M).transpose(1, 0, 2).reshape(128, kc * M))


def _rope_tables(core):
    n = core * NL + np.arange(NL, dtype=np.float64)
    inv = 1.0 / (10000.0 ** (np.arange(0, 64, 2, dtype=np.float64) / 64.0))
    fr = n[:, None] * inv[None, :]
    emb = np.concatenate([fr, fr], axis=1)
    s = 64.0 ** -0.25
    cs = (np.cos(emb) * s).astype(np.float32).T
    sn = (np.sin(emb) * s).astype(np.float32).T
    return (np.ascontiguousarray(np.tile(cs, (2, NB))),
            np.ascontiguousarray(np.tile(sn, (2, NB))))


def _rotperm():
    m = np.zeros((128, 128), np.float32)
    for blk in (0, 64):
        for i in range(32):
            m[blk + 32 + i, blk + i] = -1.0
            m[blk + i, blk + 32 + i] = 1.0
    return m.astype(BF)


def _rollperm():
    m = np.zeros((128, 128), np.float32)
    for i in range(128):
        m[(i - 1) % 128, i] = 1.0
    return m.astype(BF)


def _pp(vec, n=NCH):
    return np.ascontiguousarray(np.asarray(vec).reshape(n, 128).T)


def host_prep(inputs):
    g = {}
    T_ = lambda a: np.ascontiguousarray(np.asarray(a).T)
    A_ = lambda a: np.ascontiguousarray(np.asarray(a))

    g["l0a_wqt"] = _chunkmajor(T_(inputs["L0_attn_Wq"]))
    g["l0a_wkt"] = _chunkmajor(T_(inputs["L0_attn_Wk"]))
    g["l0a_wvt"] = _chunkmajor(T_(inputs["L0_attn_Wv"]))
    g["l0a_wot"] = _chunkmajor(T_(inputs["L0_attn_Wout"]))
    g["l0a_bout"] = _pp(inputs["L0_attn_bout"])

    qkvT = T_(inputs["L0_ffn_qkv_w"])
    g["l0f_wqt"] = _chunkmajor(qkvT[:, 0:1024])
    g["l0f_wkt"] = _chunkmajor(qkvT[:, 1024:2048])
    g["l0f_wvt"] = _chunkmajor(qkvT[:, 2048:3072])
    qb = A_(inputs["L0_ffn_qkv_b"])
    g["l0f_qb"] = _pp(qb[0:1024])
    g["l0f_kb"] = _pp(qb[1024:2048])
    g["l0f_vb"] = qb[2048:3072].reshape(1, 1024).copy()
    g["l0f_wot"] = _chunkmajor(T_(inputs["L0_ffn_out_w"]))
    g["l0f_bout"] = _pp(inputs["L0_ffn_out_b"])
    g["l0f_n1w"] = _pp(inputs["L0_ffn_norm1_w"])
    g["l0f_n2w"] = _pp(inputs["L0_ffn_norm2_w"])

    w1 = np.asarray(inputs["L0_ffn_w1"]); w2 = np.asarray(inputs["L0_ffn_w2"])
    w3 = np.asarray(inputs["L0_ffn_w3"])
    g["l0f_w1b"] = np.stack([_blocks(T_(w1[e])) for e in range(8)])
    g["l0f_w2b"] = np.stack([_blocks(T_(w2[e])) for e in range(8)])
    g["l0f_w3b"] = np.stack([_blocks(T_(w3[e])) for e in range(8)])
    g["l0f_b1"] = np.stack([_pp(np.asarray(inputs["L0_ffn_b1"])[e], 16) for e in range(8)])
    g["l0f_b2"] = np.stack([_pp(np.asarray(inputs["L0_ffn_b2"])[e], 16) for e in range(8)])
    g["l0f_b3"] = A_(inputs["L0_ffn_b3"]).reshape(8, 1, 1024).copy()
    g["l0f_gatet"] = _chunkmajor(T_(inputs["L0_ffn_gate_w"]))

    g["l1a_wq"] = _chunkmajor(A_(inputs["L1_attn_Wq"]))
    g["l1a_wk"] = _chunkmajor(A_(inputs["L1_attn_Wk"]))
    g["l1a_wv"] = _chunkmajor(A_(inputs["L1_attn_Wv"]))
    g["l1a_wbm"] = _chunkmajor(A_(inputs["L1_attn_Wb"]))
    g["l1a_wo"] = _chunkmajor(A_(inputs["L1_attn_Wo"]))
    g["l1a_bo"] = _pp(inputs["L1_attn_bo"])

    g["l1f_wg"] = _chunkmajor(A_(inputs["L1_ffn_Wg"]))
    g["l1f_alpha"] = A_(inputs["L1_ffn_alpha"]).reshape(1, 8).copy()
    g["l1f_lnw"] = np.stack([_pp(np.asarray(inputs["L1_ffn_ln_w"])[e]) for e in range(8)])
    g["l1f_lnb"] = np.stack([_pp(np.asarray(inputs["L1_ffn_ln_b"])[e]) for e in range(8)])
    fc1 = np.asarray(inputs["L1_ffn_fc1_w"]); fc2 = np.asarray(inputs["L1_ffn_fc2_w"])
    g["l1f_fc1b"] = np.stack([_blocks(A_(fc1[e])) for e in range(8)])
    g["l1f_fc2b"] = np.stack([_blocks(A_(fc2[e])) for e in range(8)])
    g["l1f_b1"] = np.stack([_pp(np.asarray(inputs["L1_ffn_fc1_b"])[e], 32) for e in range(8)])
    g["l1f_b2"] = A_(inputs["L1_ffn_fc2_b"]).reshape(8, 1, 1024).copy()

    g["rotperm"] = _rotperm()
    g["rollperm"] = _rollperm()
    g["iota8"] = np.tile(np.arange(8, dtype=np.float32), (128, 1)).copy()

    x = np.asarray(inputs["L0_attn_x"])
    percore = []
    for c in range(NCORES):
        d = {"x_in": np.ascontiguousarray(x[:, c * NL:(c + 1) * NL, :])}
        cs, sn = _rope_tables(c)
        d["ropecs"] = cs
        d["ropesn"] = sn
        percore.append(d)
    return g, percore



class Ph:
    ENGS = ("t", "v", "s", "g", "y")

    def __init__(self, P):
        self.P = P
        self.nc = P.nc
        self.ops = {e: [] for e in self.ENGS}

    def wait(self, eng, sem, val):
        if val <= 0:
            return
        h = self.P.sems[sem]
        self.ops[eng].append(lambda e, h=h, v=val: e.wait_ge(h, v))

    def raw(self, eng, fn):
        self.ops[eng].append(fn)

    def do(self, eng, fn, inc="auto", n=1):
        if inc == "auto":
            inc = ESEM.get(eng)
        if inc is None:
            self.ops[eng].append(fn)
            return None
        h = self.P.sems[inc]
        self.ops[eng].append(lambda e, fn=fn, h=h, n=n: fn(e).then_inc(h, n))
        self.P.cnt[inc] += n
        return self.P.cnt[inc]

    def flush(self):
        with self.nc.Block() as block:
            for eng, sect in (("y", block.sync), ("g", block.gpsimd),
                              ("t", block.tensor), ("s", block.scalar),
                              ("v", block.vector)):
                ops = self.ops[eng]
                if ops:
                    sect(lambda e, ops=ops: [f(e) for f in ops])
        self.ops = {e: [] for e in self.ENGS}


class Seq:
    def __init__(self, ph, deps=()):
        self.ph = ph
        self.deps = list(deps)

    def do(self, eng, fn, inc="auto", n=1):
        for sem, cntv in self.deps:
            self.ph.wait(eng, sem, cntv)
        c = self.ph.do(eng, fn, inc=inc, n=n)
        if inc == "auto":
            inc = ESEM.get(eng)
        if inc is not None:
            self.deps = [(inc, c)]
        return c

    def also(self, eng, fn, inc="auto", n=1):
        c = self.ph.do(eng, fn, inc=inc, n=n)
        if inc == "auto":
            inc = ESEM.get(eng)
        if inc is not None:
            self.deps.append((inc, c))
        return c


class Prog:
    def __init__(self, nc):
        self.nc = nc
        self.cnt = {}
        self.sems = {}
        for name in ("wl", "al", "st", "pe", "ve", "se", "ge", "cc"):
            self.sems[name] = nc.alloc_semaphore(name)
            self.cnt[name] = 0
        self.dram = {}
        self.sb = {}
        self.innames = []

    def phase(self):
        return Ph(self)

    def din(self, name, shape, dtype=F32):
        if name not in self.dram:
            self.dram[name] = self.nc.dram_tensor(name, list(shape), dtype, kind="ExternalInput")
            self.innames.append(name)
        return self.dram[name]

    def dout(self, name, shape, dtype=F32):
        self.dram[name] = self.nc.dram_tensor(name, list(shape), dtype, kind="ExternalOutput")
        return self.dram[name]

    def dint(self, name, shape, dtype=F32):
        if name not in self.dram:
            self.dram[name] = self.nc.dram_tensor(name, list(shape), dtype, kind="Internal")
        return self.dram[name]

    def sbuf(self, name, shape, dtype=F32):
        if name not in self.sb:
            self.sb[name] = self.nc.alloc_sbuf_tensor(name, list(shape), dtype)
        return self.sb[name]



def emit_consts(P, ph):
    idn = P.sbuf("idn", [128, 128], F32)
    ones_col = P.sbuf("ones_col", [128, 1], F32)
    ones_row = P.sbuf("ones_row", [1, 128], F32)
    ones_col_bf = P.sbuf("ones_col_bf", [128, 1], BF16)
    ones_row_bf = P.sbuf("ones_row_bf", [1, 128], BF16)
    rotp = P.sbuf("rotp", [128, 128], BF16)
    rollp = P.sbuf("rollp", [128, 128], BF16)
    iota8v = P.sbuf("iota8v", [128, 8], F32)

    ph.do("g", lambda e: e.memset(ones_col[:, :], 1.0))
    ph.do("g", lambda e: e.memset(ones_row[:, :], 1.0))
    ph.do("g", lambda e: e.memset(ones_col_bf[:, :], 1.0))
    ph.do("g", lambda e: e.memset(ones_row_bf[:, :], 1.0))
    ph.do("g", lambda e: e.memset(idn[:, :], 0.0))
    ph.do("g", lambda e: e.affine_select(
        out=idn[:, :], in_=idn[:, :], compare_op=ALU.not_equal, fill=1.0,
        base=0, pattern=[[-1, 128]], channel_multiplier=1))
    for nm, dst in (("rotperm", rotp), ("rollperm", rollp), ("iota8", iota8v)):
        dt = BF16 if nm in ("rotperm", "rollperm") else F32
        d = P.din(nm, list(dst.shape), dt)
        ph.do("y", lambda e, d=d, dst=dst: e.dma_start(dst[:, :], d[:, :]), inc="al", n=16)


def emit_input_convert(P, ph):
    nc = P.nc
    x_fm = P.sbuf("x_fm", [128, NCH * T], F32)
    x_in = P.din("x_in", [NB, NL, D], F32)
    idn = P.sb["idn"]
    with nc.sbuf_tensor("xtok_i", [128, 16 * D], F32) as xtok, \
         nc.psum_tensor("ptr_i", [128, 4 * 512], F32) as ptr:
        ap_in = x_in.ap().rearrange("b (j p) d -> p (b j) d", p=128)
        ap_out = xtok.ap().rearrange("p (j d) -> p j d", j=16)
        c_al = ph.do("y", lambda e: e.dma_start(ap_out, ap_in), inc="al", n=16)
        base_pe = P.cnt["pe"]; base_ve = P.cnt["ve"]
        i = 0
        for j in range(16):
            for dc in range(NCH):
                bank = i % 4
                ph.wait("t", "al", c_al)
                if i >= 4:
                    ph.wait("t", "ve", base_ve + i - 3)
                ph.do("t", lambda e, j=j, dc=dc, bank=bank: e.transpose(
                    ptr[:, bank * 512:bank * 512 + 128],
                    xtok[:, j * D + dc * 128: j * D + (dc + 1) * 128],
                    idn[:, :]))
                ph.wait("v", "pe", base_pe + i + 1)
                ph.do("v", lambda e, j=j, dc=dc, bank=bank: e.tensor_copy(
                    x_fm[:, dc * T + j * 128: dc * T + (j + 1) * 128],
                    ptr[:, bank * 512:bank * 512 + 128]))
                i += 1
        ph.flush()


def emit_output_convert(P, ph):
    nc = P.nc
    x_fm = P.sb["x_fm"]
    x_out = P.dout("x_out", [NB, NL, D], F32)
    idn = P.sb["idn"]
    with nc.sbuf_tensor("xtok_o", [128, 16 * D], F32) as xtok, \
         nc.psum_tensor("ptr_o", [128, 4 * 512], F32) as ptr:
        base_pe = P.cnt["pe"]; base_ve = P.cnt["ve"]
        i = 0
        for j in range(16):
            for dc in range(NCH):
                bank = i % 4
                if i >= 4:
                    ph.wait("t", "ve", base_ve + i - 3)
                ph.do("t", lambda e, j=j, dc=dc, bank=bank: e.transpose(
                    ptr[:, bank * 512:bank * 512 + 128],
                    x_fm[:, dc * T + j * 128: dc * T + (j + 1) * 128],
                    idn[:, :]))
                ph.wait("v", "pe", base_pe + i + 1)
                ph.do("v", lambda e, j=j, dc=dc, bank=bank: e.tensor_copy(
                    xtok[:, j * D + dc * 128: j * D + (dc + 1) * 128],
                    ptr[:, bank * 512:bank * 512 + 128]))
                i += 1
        c_ve = P.cnt["ve"]
        ph.wait("y", "ve", c_ve)
        ap_out = x_out.ap().rearrange("b (j p) d -> p (b j) d", p=128)
        ap_in = xtok.ap().rearrange("p (j d) -> p j d", j=16)
        c_st = ph.do("y", lambda e: e.dma_start(ap_out, ap_in), inc="st", n=16)
        ph.wait("y", "st", c_st)
        ph.flush()


def emit_fm_load(P, ph):
    x_fm = P.sbuf("x_fm", [128, NCH * T], F32)
    d = P.din("xfm_in", [128, NCH * T], F32)
    c = ph.do("y", lambda e: e.dma_start(x_fm[:, :], d[:, :]), inc="al", n=16)
    ph.wait("y", "al", c)
    ph.flush()


def emit_fm_store(P, ph):
    x_fm = P.sb["x_fm"]
    d = P.dout("xfm_out", [128, NCH * T], F32)
    c = ph.do("y", lambda e: e.dma_start(d[:, :], x_fm[:, :]), inc="st", n=16)
    ph.wait("y", "st", c)
    ph.flush()



def emit_norm_fm(P, ph, srct, dstt, kind, eps, w_sb=None, uid="", tts=None,
                 dst_stride=T, dst_base=0, src_stride=T):
    nc = P.nc
    assert not (kind == "ln" and w_sb is not None)
    if tts is None:
        tts = list(range(NTT))
    ones_src = P.sb["ones_col"] if srct.dtype == F32 else P.sb["ones_col_bf"]
    ones_col = P.sb["ones_col"]
    ones_row = P.sb["ones_row"]
    src = lambda cc, t0, tw: srct[:, cc * src_stride + t0: cc * src_stride + t0 + tw]
    dst = lambda cc, t0, tw: dstt[:, dst_base + cc * dst_stride + t0 - tts[0] * TTW:
                                  dst_base + cc * dst_stride + t0 - tts[0] * TTW + tw]
    with nc.sbuf_tensor(f"nsq{uid}", [128, NCH * TTW // 4], F32) as sq, \
         nc.sbuf_tensor(f"nrw{uid}", [1, 3 * TTW], F32) as rows, \
         nc.psum_tensor(f"nmps{uid}", [1, 512], F32) as ps_sum, \
         nc.psum_tensor(f"nmpq{uid}", [1, 512], F32) as ps_sq, \
         nc.psum_tensor(f"nmpa{uid}", [128, 512], F32) as rep_a, \
         nc.psum_tensor(f"nmpb{uid}", [128, 512], F32) as rep_b:
        sq3 = sq.ap().rearrange("p (c w) -> p c w", c=NCH // 4)
        prev_ve = P.cnt["ve"]
        for tt in tts:
            t0 = tt * TTW
            src3 = srct.ap().rearrange("p (c t) -> p c t", c=NCH)[:, :, t0:t0 + TTW]
            ph.wait("s", "ve", prev_ve)
            if kind == "ln":
                ph.wait("t", "ve", prev_ve)
                for cc in range(NCH):
                    ph.do("t", lambda e, cc=cc, t0=t0: e.matmul(
                        ps_sum[:, :], ones_src[:, :], src(cc, t0, TTW),
                        start=(cc == 0), stop=(cc == NCH - 1)),
                        inc=(None if cc < NCH - 1 else "pe"))
            for half in range(4):
                if half >= 1:
                    ph.wait("s", "pe", P.cnt["pe"])
                c_sq = ph.do("s", lambda e, src3=src3, half=half: e.activation(
                    sq3, src3[:, half * (NCH // 4):(half + 1) * (NCH // 4), :],
                    AF.Square))
                ph.wait("t", "se", c_sq)
                for cc4 in range(NCH // 4):
                    cc = half * (NCH // 4) + cc4
                    ph.do("t", lambda e, cc4=cc4, cc=cc: e.matmul(
                        ps_sq[:, :], ones_col[:, :], sq[:, cc4 * TTW:(cc4 + 1) * TTW],
                        start=(cc == 0), stop=(cc == NCH - 1)),
                        inc=(None if cc < NCH - 1 else "pe"))
            c_pe = P.cnt["pe"]
            mu = rows[:, 0:TTW]; var = rows[:, TTW:2 * TTW]
            a = rows[:, 2 * TTW:3 * TTW]; b = rows[:, 0:TTW]
            s_ = Seq(ph, deps=[("pe", c_pe)])
            if kind == "ln":
                s_.do("v", lambda e: e.tensor_scalar(mu, ps_sum[0:1, :], 1.0 / D, None, ALU.mult))
                s_.do("v", lambda e: e.tensor_scalar(var, ps_sq[0:1, :], 1.0 / D, None, ALU.mult))
                s_.do("s", lambda e: e.activation(a, mu, AF.Square))
                s_.do("v", lambda e: e.tensor_sub(var, var, a))
            else:
                s_.do("v", lambda e: e.tensor_scalar(var, ps_sq[0:1, :], 1.0 / D, None, ALU.mult))
            s_.do("v", lambda e: e.tensor_scalar(var, var, float(eps), None, ALU.add))
            s_.do("s", lambda e: e.activation(var, var, AF.Sqrt))
            s_.do("v", lambda e: e.reciprocal(a, var))
            if kind == "ln":
                s_.do("v", lambda e: e.scalar_tensor_tensor(b, mu, -1.0, a, ALU.mult, ALU.mult))
            s_.do("t", lambda e: e.matmul(rep_a[:, :], ones_row[:, :], a, start=True, stop=True))
            if kind == "ln":
                s_.do("t", lambda e: e.matmul(rep_b[:, :], ones_row[:, :], b, start=True, stop=True))
            c_rep = s_.deps[0][1]
            ph.wait("v", "pe", c_rep)
            for cc in range(NCH):
                d_ap = dst(cc, t0, TTW)
                if kind == "ln":
                    ph.do("v", lambda e, cc=cc, t0=t0: e.tensor_mul(
                        sq[:, (cc % 2) * TTW:(cc % 2 + 1) * TTW], src(cc, t0, TTW),
                        rep_a[:, :]))
                    ph.do("v", lambda e, cc=cc, d_ap=d_ap: e.tensor_add(
                        d_ap, sq[:, (cc % 2) * TTW:(cc % 2 + 1) * TTW], rep_b[:, :]))
                else:
                    if w_sb is None:
                        ph.do("v", lambda e, cc=cc, t0=t0, d_ap=d_ap: e.tensor_mul(
                            d_ap, src(cc, t0, TTW), rep_a[:, :]))
                    else:
                        ph.do("v", lambda e, cc=cc, t0=t0, d_ap=d_ap: e.scalar_tensor_tensor(
                            d_ap, src(cc, t0, TTW), w_sb[:, cc:cc + 1], rep_a[:, :],
                            ALU.mult, ALU.mult))
            prev_ve = P.cnt["ve"]
    ph.flush()



def emit_proj_fm(P, ph, wsb, c_w, zt, outt, prj_ps, scale=1.0, bias_sb=None,
                 deps=(), hcs=None, outofs=None):
    if hcs is None:
        hcs = list(range(NCH))
    if outofs is None:
        outofs = lambda hc: hc * T
    base_se = P.cnt["se"]
    i = 0
    for hc in hcs:
        oofs = outofs(hc)
        for tt in range(NTT):
            t0 = tt * TTW
            bank = i % 2
            if i == 0:
                ph.wait("t", "wl", c_w)
                for sem, v in deps:
                    ph.wait("t", sem, v)
            if i >= 2:
                ph.wait("t", "se", base_se + i - 1)
            for cc in range(NCH):
                ph.do("t", lambda e, cc=cc, hc=hc, t0=t0, bank=bank: e.matmul(
                    prj_ps[:, bank * 512:(bank + 1) * 512],
                    wsb[:, cc * D + hc * 128: cc * D + (hc + 1) * 128],
                    zt[:, cc * T + t0: cc * T + t0 + TTW],
                    start=(cc == 0), stop=(cc == NCH - 1)),
                    inc=(None if cc < NCH - 1 else "pe"))
            c_pe = P.cnt["pe"]
            ph.wait("s", "pe", c_pe)
            bb = None if bias_sb is None else bias_sb[:, hc:hc + 1]
            def ev(e, oofs=oofs, t0=t0, bank=bank, bb=bb):
                kw = dict(scale=float(scale))
                if bb is not None:
                    kw["bias"] = bb
                return e.activation(outt[:, oofs + t0: oofs + t0 + TTW],
                                    prj_ps[:, bank * 512:(bank + 1) * 512],
                                    AF.Identity, **kw)
            ph.do("s", ev)
            i += 1
    return P.cnt["se"]



def emit_rope(P, ph, buf, rp_ps, tmp, c_ready_se, cs0, sn0):
    c = P.cnt["ve"]
    for hc in range(NCH):
        c = _emit_rope_tiles(
            P, ph, lambda tt, hc=hc: buf[:, hc * T + tt * TTW: hc * T + (tt + 1) * TTW],
            rp_ps, tmp, c_ready_se, cs0, sn0)
    return c



def _emit_rope_tiles(P, ph, get_sl, rp_ps, rtmp, c_ready_se, cs0, sn0):
    rotp = P.sb["rotp"]
    base_ve = P.cnt["ve"]
    for tt in range(NTT):
        t0 = tt * TTW
        bank = tt % 2
        sl = get_sl(tt)
        if tt == 0:
            ph.wait("t", "se", c_ready_se)
            ph.wait("t", "ve", base_ve)
        if tt >= 2:
            ph.wait("t", "ve", base_ve + (tt - 2) * 3 + 3)
        ph.do("t", lambda e, sl=sl, bank=bank: e.matmul(
            rp_ps[:, bank * 512:(bank + 1) * 512], rotp[:, :], sl,
            start=True, stop=True))
        c_pe = P.cnt["pe"]
        ph.wait("v", "pe", c_pe)
        ph.do("v", lambda e, bank=bank, t0=t0: e.tensor_mul(
            rtmp[:, 0:TTW], rp_ps[:, bank * 512:(bank + 1) * 512], sn0[:, t0:t0 + TTW]))
        ph.do("v", lambda e, sl=sl, t0=t0: e.tensor_mul(
            rtmp[:, TTW:2 * TTW], sl, cs0[:, t0:t0 + TTW]))
        ph.do("v", lambda e, sl=sl: e.tensor_add(sl, rtmp[:, TTW:2 * TTW], rtmp[:, 0:TTW]))
    return P.cnt["ve"]


def _emit_m2(P, ph, get_chunk, hcs, row_out, nps, sqbuf, dep_of=None):
    ones_col = P.sb["ones_col"]
    red = {}
    k = 0
    for hc in hcs:
        ch = get_chunk(hc)
        deps = [("pe", P.cnt["pe"])]
        if dep_of is not None:
            deps.append(dep_of(hc))
        for sem, v in deps:
            ph.wait("v", sem, v)
        c_sq = ph.do("v", lambda e, ch=ch: e.tensor_mul(sqbuf[:, :], ch, ch))
        ph.wait("t", "ve", c_sq)
        for j in range(2):
            hb = j * 64
            for b in range(NB):
                idx = (hc * 2 + j) * 4 + b
                bank = k % 2
                if k >= 2:
                    ph.wait("t", "ve", red[k - 2])
                c_mm = ph.do("t", lambda e, hb=hb, b=b, bank=bank: e.matmul(
                    nps[0:1, bank * 512:(bank + 1) * 512],
                    ones_col[hb:hb + 64, :], sqbuf[hb:hb + 64, b * 512:(b + 1) * 512],
                    start=True, stop=True))
                ph.wait("v", "pe", c_mm)
                red[k] = ph.do("v", lambda e, idx=idx, bank=bank: e.reduce_max(
                    row_out[0:1, idx:idx + 1], nps[0:1, bank * 512:(bank + 1) * 512],
                    axis=AXX))
                k += 1
    return P.cnt["ve"]



def emit_attn(P, ph, cfg):
    from contextlib import ExitStack
    nc = P.nc
    uid = cfg["uid"]
    x_fm = P.sb["x_fm"]
    ones_col = P.sb["ones_col"]; ones_row = P.sb["ones_row"]
    ones_col_bf = P.sb["ones_col_bf"]; ones_row_bf = P.sb["ones_row_bf"]; ones_row_bf = P.sb["ones_row_bf"]
    ginK = P.dint("ginK", [D, T], BF16)
    goutK = P.dint("goutK", [NCORES * D, T], BF16)
    ginV = P.dint("ginV", [T, D], BF16)
    goutV = P.dint("goutV", [NCORES * T, D], BF16)
    gin_s = P.dint("gin_s", [1, 64], F32)
    gout_s = P.dint("gout_s", [1, 64], F32)

    dwq = P.din(cfg["wq"], [128, NCH * D], F32)
    dwk = P.din(cfg["wk"], [128, NCH * D], F32)
    dwv = P.din(cfg["wv"], [128, NCH * D], F32)
    dwo = P.din(cfg["wo"], [128, NCH * D], F32)
    dbout = P.din(cfg["bout"], [128, NCH], F32)

    with ExitStack() as outer:
        qbuf = outer.enter_context(nc.sbuf_tensor(f"q{uid}", [128, NCH * T], BF16))
        rowbuf = outer.enter_context(nc.sbuf_tensor(f"rw{uid}", [1, 256], F32))
        negc_rep = outer.enter_context(nc.sbuf_tensor(f"ncr{uid}", [128, 64], F32))
        bout_sb = outer.enter_context(nc.sbuf_tensor(f"bo{uid}", [128, NCH], F32))
        qm2 = rowbuf[0:1, 0:64]
        km2 = rowbuf[0:1, 64:128]
        km2g = rowbuf[0:1, 128:192]
        t1 = rowbuf[0:1, 192:256]
        ph.do("y", lambda e: e.dma_start(bout_sb[:, :], dbout[:, :]), inc="al", n=16)

        with ExitStack() as stA:
            zbuf = stA.enter_context(nc.sbuf_tensor(f"z{uid}", [128, NCH * T], BF16))
            wsb0 = stA.enter_context(nc.sbuf_tensor(f"w0{uid}", [128, NCH * D], BF16))
            wsb1 = stA.enter_context(nc.sbuf_tensor(f"w1{uid}", [128, NCH * D], BF16))
            gain_sb = bias_q = bias_k = vb_sb = None
            if cfg.get("gain"):
                gain_d = P.din(cfg["gain"], [128, NCH], F32)
                gain_sb = stA.enter_context(nc.sbuf_tensor(f"gn{uid}", [128, NCH], F32))
                ph.do("y", lambda e: e.dma_start(gain_sb[:, :], gain_d[:, :]), inc="al", n=16)
            if cfg.get("qb"):
                dqb = P.din(cfg["qb"], [128, NCH], F32)
                dkb = P.din(cfg["kb"], [128, NCH], F32)
                bias_q = stA.enter_context(nc.sbuf_tensor(f"qb{uid}", [128, NCH], F32))
                bias_k = stA.enter_context(nc.sbuf_tensor(f"kb{uid}", [128, NCH], F32))
                ph.do("y", lambda e: e.dma_start(bias_q[:, :], dqb[:, :]), inc="al", n=16)
                ph.do("y", lambda e: e.dma_start(bias_k[:, :], dkb[:, :]), inc="al", n=16)
            if cfg.get("vb"):
                dvb = P.din(cfg["vb"], [1, D], F32)
                vb_sb = stA.enter_context(nc.sbuf_tensor(f"vb{uid}", [1, D], BF16))
                ph.do("g", lambda e: e.dma_start(vb_sb[:, :], dvb[:, :]), inc="al", n=16)
            cs0 = sn0 = None
            if cfg["rope"]:
                dcs = P.din("ropecs", [128, T], F32)
                dsn = P.din("ropesn", [128, T], F32)
                cs0 = stA.enter_context(nc.sbuf_tensor(f"cs{uid}", [128, T], F32))
                sn0 = stA.enter_context(nc.sbuf_tensor(f"sn{uid}", [128, T], F32))
                ph.do("y", lambda e: e.dma_start(cs0[:, :], dcs[:, :]), inc="al", n=16)
                ph.do("y", lambda e: e.dma_start(sn0[:, :], dsn[:, :]), inc="al", n=16)
            c_misc = P.cnt["al"]
            scale = 1.0 if cfg["rope"] else 64.0 ** -0.25

            ph.wait("v", "al", c_misc)
            ph.wait("s", "al", c_misc)
            emit_norm_fm(P, ph, x_fm, zbuf, cfg["kind"], cfg["eps"], w_sb=gain_sb, uid=uid)
            c_z = P.cnt["ve"]

            with nc.psum_tensor(f"prj{uid}", [128, 1024], F32) as prj_ps, \
                 nc.psum_tensor(f"rp{uid}", [128, 1024], F32) as rp_ps, \
                 nc.psum_tensor(f"nps{uid}", [1, 1024], F32) as nps, \
                 nc.psum_tensor(f"sps{uid}", [128, 64], F32) as sps, \
                 nc.sbuf_tensor(f"kch{uid}", [128, 2 * T], BF16) as kch, \
                 nc.sbuf_tensor(f"ksq{uid}", [128, T], F32) as ksq, \
                 nc.sbuf_tensor(f"rt{uid}", [128, 2 * TTW], F32) as rtmp, \
                 nc.sbuf_tensor(f"vst{uid}", [128, 2 * D], BF16) as vstage:

                c_wq = ph.do("g", lambda e: e.dma_start(
                    wsb0.ap().rearrange("p (c m) -> p c m", c=NCH),
                    dwq.ap().rearrange("p (c m) -> p c m", c=NCH)), inc="wl", n=16)
                c_qse = emit_proj_fm(P, ph, wsb0, c_wq, zbuf, qbuf, prj_ps,
                                     scale=scale, bias_sb=bias_q, deps=[("ve", c_z)])
                c_wk = ph.do("g", lambda e: e.dma_start(
                    wsb1.ap().rearrange("p (c m) -> p c m", c=NCH),
                    dwk.ap().rearrange("p (c m) -> p c m", c=NCH)), inc="wl", n=16)
                if cfg["rope"]:
                    emit_rope(P, ph, qbuf, rp_ps, rtmp, c_qse, cs0, sn0)
                    qdep = None
                else:
                    qdep = lambda _hc: ("se", c_qse)
                _emit_m2(P, ph, lambda hc: qbuf[:, hc * T:(hc + 1) * T],
                         range(NCH), qm2, nps, ksq, dep_of=qdep)

                kdma = {}
                kv_done = {}
                for hc in range(NCH):
                    half = hc % 2
                    buf = kch[:, half * T:(half + 1) * T]
                    deps = []
                    if hc >= 2:
                        deps = [("st", kdma[hc - 2]), ("ve", kv_done[hc - 2])]
                    c_se = emit_proj_fm(P, ph, wsb1, c_wk, zbuf, buf, prj_ps,
                                        scale=scale, bias_sb=bias_k, deps=deps,
                                        hcs=[hc], outofs=lambda _hc: 0)
                    if cfg["rope"]:
                        c_rdy = _emit_rope_tiles(
                            P, ph, lambda tt, buf=buf: buf[:, tt * TTW:(tt + 1) * TTW],
                            rp_ps, rtmp, c_se, cs0, sn0)
                        kdep = ("ve", c_rdy)
                    else:
                        kdep = ("se", c_se)
                    c_m2 = _emit_m2(P, ph, lambda _hc, buf=buf: buf, [hc], km2,
                                    nps, ksq, dep_of=lambda _hc, kdep=kdep: kdep)
                    kv_done[hc] = c_m2
                    ph.wait("y", kdep[0], kdep[1])
                    kdma[hc] = ph.do("y", lambda e, hc=hc, buf=buf: e.dma_start(
                        ginK.ap()[hc * 128:(hc + 1) * 128, :], buf), inc="st", n=16)
                c_kcontrib = P.cnt["st"]

                ph.wait("g", "se", c_qse)
                c_wv = ph.do("g", lambda e: e.dma_start(
                    wsb0.ap().rearrange("p (c m) -> p c m", c=NCH),
                    dwv.ap().rearrange("p (c m) -> p c m", c=NCH)), inc="wl", n=16)
                vdma = {}
                base_vse = P.cnt["se"]
                for tk in range(16):
                    for hh in range(2):
                        i = tk * 2 + hh
                        bank = i % 2
                        if i == 0:
                            ph.wait("t", "wl", c_wv)
                        if i >= 2:
                            ph.wait("t", "se", base_vse + i - 1)
                        if tk >= 2 and hh == 0:
                            ph.wait("t", "st", vdma[tk - 2])
                        for cc in range(NCH):
                            last = (cc == NCH - 1 and vb_sb is None)
                            ph.do("t", lambda e, cc=cc, tk=tk, hh=hh, bank=bank, last=last:
                                  e.matmul(prj_ps[:, bank * 512:(bank + 1) * 512],
                                           zbuf[:, cc * T + tk * 128: cc * T + (tk + 1) * 128],
                                           wsb0[:, cc * D + hh * 512: cc * D + hh * 512 + 512],
                                           start=(cc == 0), stop=last),
                                  inc=(None if not last else "pe"))
                        if vb_sb is not None:
                            ph.do("t", lambda e, hh=hh, bank=bank: e.matmul(
                                prj_ps[:, bank * 512:(bank + 1) * 512],
                                ones_row_bf[:, :],
                                vb_sb[0:1, hh * 512:hh * 512 + 512],
                                start=False, stop=True, skip_group_check=True), inc="pe")
                        c_pe = P.cnt["pe"]
                        ph.wait("s", "pe", c_pe)
                        ph.do("s", lambda e, hh=hh, bank=bank, tk=tk: e.activation(
                            vstage[:, (tk % 2) * D + hh * 512:(tk % 2) * D + hh * 512 + 512],
                            prj_ps[:, bank * 512:(bank + 1) * 512], AF.Identity))
                        if hh == 1:
                            c_se = P.cnt["se"]
                            ph.wait("y", "se", c_se)
                            vdma[tk] = ph.do("y", lambda e, tk=tk: e.dma_start(
                                ginV.ap()[tk * 128:(tk + 1) * 128, :],
                                vstage[:, (tk % 2) * D:(tk % 2) * D + D]), inc="st", n=16)
                c_vcontrib = P.cnt["st"]

                c_km = P.cnt["ve"]
                ph.wait("y", "ve", c_km)
                c_kms = ph.do("y", lambda e: e.dma_start(gin_s[:, :], km2), inc="st", n=16)
                ph.wait("g", "st", c_kms)
                c_cc1 = ph.do("g", lambda e: e.collective_compute(
                    "AllReduce", ALU.max, replica_groups=[list(range(NCORES))],
                    ins=[gin_s.ap().opt()], outs=[gout_s.ap().opt()]), inc="cc")
                ph.wait("g", "st", c_kcontrib)
                c_cc2 = ph.do("g", lambda e: e.collective_compute(
                    "AllGather", ALU.bypass, replica_groups=[list(range(NCORES))],
                    ins=[ginK.ap().opt()], outs=[goutK.ap().opt()]), inc="cc")
                ph.wait("g", "st", c_vcontrib)
                c_cc3 = ph.do("g", lambda e: e.collective_compute(
                    "AllGather", ALU.bypass, replica_groups=[list(range(NCORES))],
                    ins=[ginV.ap().opt()], outs=[goutV.ap().opt()]), inc="cc")

                ph.wait("y", "cc", c_cc1)
                c_al = ph.do("y", lambda e: e.dma_start(km2g, gout_s[:, :]), inc="al", n=16)
                sh = Seq(ph, deps=[("al", c_al)])
                sh.do("v", lambda e: e.tensor_mul(t1, qm2, km2g))
                sh.do("s", lambda e: e.activation(t1, t1, AF.Sqrt))
                sh.do("v", lambda e: e.tensor_scalar(t1, t1, -1.0, None, ALU.mult))
                sh.do("t", lambda e: e.matmul(sps[:, :], ones_row[:, :], t1,
                                              start=True, stop=True))
                sh.do("v", lambda e: e.tensor_copy(negc_rep[:, :], sps[:, :]))
                ph.flush()
                if cfg.get("dump_a"):
                    dz = P.dout("dump_z", [128, NCH * T], BF16)
                    dq = P.dout("dump_q", [128, NCH * T], BF16)
                    dnc = P.dout("dump_negc", [128, 64], F32)
                    dgk = P.dout("dump_ginK", [D, T], BF16)
                    dgv = P.dout("dump_ginV", [T, D], BF16)
                    c = ph.do("y", lambda e: e.dma_start(dz[:, :], zbuf[:, :]), inc="st", n=16)
                    c = ph.do("y", lambda e: e.dma_start(dq[:, :], qbuf[:, :]), inc="st", n=16)
                    c = ph.do("y", lambda e: e.dma_start(dnc[:, :], negc_rep[:, :]), inc="st", n=16)
                    c = ph.do("y", lambda e: e.dma_start(dgk[:, :], ginK[:, :]), inc="st", n=16)
                    c = ph.do("y", lambda e: e.dma_start(dgv[:, :], ginV[:, :]), inc="st", n=16)
                    ph.wait("y", "st", c)
                    ph.flush()
                    return

        with nc.sbuf_tensor(f"ks0{uid}", [128, 2 * 4096], BF16) as ksb0, \
             nc.sbuf_tensor(f"ks1{uid}", [128, 2 * 4096], BF16) as ksb1, \
             nc.sbuf_tensor(f"vs0{uid}", [128, 32 * 256], BF16) as vsb0, \
             nc.sbuf_tensor(f"vs1{uid}", [128, 32 * 256], BF16) as vsb1, \
             nc.sbuf_tensor(f"pb{uid}", [128, 2 * 1024], BF16) as psb, \
             nc.sbuf_tensor(f"abf{uid}", [128, NCH * 512], BF16) as abuf, \
             nc.sbuf_tensor(f"rr{uid}", [1, 512], F32) as rrow, \
             nc.sbuf_tensor(f"rs{uid}", [64, 512], F32) as rep_sb, \
             nc.sbuf_tensor(f"wo{uid}", [128, NCH * D], BF16) as wo_sb, \
             nc.psum_tensor(f"scp{uid}", [128, 2048], F32) as sc_ps, \
             nc.psum_tensor(f"avp{uid}", [128, 2048], F32) as av_ps:
            c_wo = ph.do("g", lambda e: e.dma_start(
                wo_sb.ap().rearrange("p (c m) -> p c m", c=NCH),
                dwo.ap().rearrange("p (c m) -> p c m", c=NCH)), inc="wl", n=16)
            ksbs = (ksb0, ksb1)
            vsbs = (vsb0, vsb1)
            c_al_iter = [0] * 16
            c_iter_done = [0] * 16

            def issue_loads(i):
                b, hg = divmod(i, 4)
                if i >= 2:
                    ph.wait("y", "pe", c_iter_done[i - 2])
                if i == 0:
                    ph.wait("y", "cc", c_cc3)
                kap = goutK.ap().rearrange("(c d) t -> d c t", c=NCORES)
                for pr in range(2):
                    h0 = hg * 4 + 2 * pr
                    src = kap[h0 * 64: h0 * 64 + 128, :, b * 512:(b + 1) * 512]
                    dst = ksbs[i % 2].ap().rearrange("p (r t) -> p r t", r=2)[
                        :, pr:pr + 1, :].rearrange(
                        "p o (c t) -> p (o c) t", c=NCORES)
                    ph.do("y", lambda e, src=src, dst=dst: e.dma_start(dst, src),
                          inc="al", n=16)
                vap = goutV.ap().rearrange("(c t) h -> c t h", c=NCORES)
                for c in range(NCORES):
                    src = vap[c:c + 1, b * 512:(b + 1) * 512,
                              hg * 256:(hg + 1) * 256].rearrange(
                                  "o (tl p) h -> p (o tl) h", p=128)
                    dst = vsbs[i % 2].ap().rearrange("p (g h) -> p g h", g=32)[
                        :, c * 4:(c + 1) * 4, :]
                    ph.do("y", lambda e, src=src, dst=dst: e.dma_start(dst, src),
                          inc="al", n=16)
                c_al_iter[i] = P.cnt["al"]

            issue_loads(0)
            issue_loads(1)
            for i in range(16):
                b, hg = divmod(i, 4)
                ksb = ksbs[i % 2]
                vsb = vsbs[i % 2]
                c_prev_norm_ve = P.cnt["ve"]
                for j in range(4):
                    h = hg * 4 + j
                    idx = h * 4 + b
                    hb = (j % 2) * 64
                    pair = j // 2
                    qap = qbuf[(h % 2) * 64:(h % 2) * 64 + 64,
                               (h // 2) * T + b * 512:(h // 2) * T + b * 512 + 512]
                    if j == 0:
                        ph.wait("t", "al", c_al_iter[i])
                    ph.wait("t", "ve", P.cnt["ve"])
                    c_exp = {}
                    c_av = {}
                    for gg in range(16):
                        bank = gg % 2
                        if gg >= 2:
                            ph.wait("t", "se", c_exp[gg - 2])
                        for u in range(2):
                            g = 2 * gg + u
                            c_sc = ph.do("t", lambda e, g=g, bank=bank, u=u, pair=pair,
                                         hb=hb, qap=qap, ksb=ksb:
                                         e.matmul(sc_ps[:, (2 * bank + u) * 512:(2 * bank + u + 1) * 512],
                                                  ksb[hb:hb + 64,
                                                      pair * 4096 + g * 128: pair * 4096 + (g + 1) * 128],
                                                  qap, start=True, stop=True))
                        ph.wait("s", "pe", c_sc)
                        c_exp[gg] = ph.do("s", lambda e, bank=bank, idx=idx: e.activation(
                            psb[:, bank * 1024:(bank + 1) * 1024],
                            sc_ps[:, bank * 1024:(bank + 1) * 1024], AF.Exp,
                            bias=negc_rep[:, idx:idx + 1]))
                        ph.wait("t", "se", c_exp[gg])
                        for u in range(2):
                            g = 2 * gg + u
                            ph.do("t", lambda e, g=g, j=j, bank=bank, u=u, vsb=vsb: e.matmul(
                                av_ps[0:64, j * 512:(j + 1) * 512],
                                vsb[:, g * 256 + j * 64: g * 256 + j * 64 + 64],
                                psb[:, (2 * bank + u) * 512:(2 * bank + u + 1) * 512],
                                start=(g == 0), stop=False), inc=None)
                            c_av[gg] = ph.do("t", lambda e, g=g, j=j, bank=bank, u=u: e.matmul(
                                av_ps[64:65, j * 512:(j + 1) * 512],
                                ones_col_bf[:, :],
                                psb[:, (2 * bank + u) * 512:(2 * bank + u + 1) * 512],
                                start=(g == 0), stop=(g == 31), skip_group_check=True))
                    ph.wait("v", "pe", c_av[15])
                    c_rc = ph.do("v", lambda e, j=j: e.reciprocal(
                        rrow[:, :], av_ps[64:65, j * 512:(j + 1) * 512]))
                    ph.wait("t", "ve", c_rc)
                    c_rp = ph.do("t", lambda e: e.matmul(
                        sc_ps[0:64, 0:512], ones_row[0:1, 0:64], rrow[:, :],
                        start=True, stop=True))
                    ph.wait("v", "pe", c_rp)
                    ph.do("v", lambda e: e.tensor_copy(rep_sb[:, :], sc_ps[0:64, 0:512]))
                    hcd = h // 2
                    hbd = (h % 2) * 64
                    ph.do("v", lambda e, j=j, hcd=hcd, hbd=hbd: e.tensor_mul(
                        abuf[hbd:hbd + 64, hcd * 512:(hcd + 1) * 512],
                        av_ps[0:64, j * 512:(j + 1) * 512], rep_sb[:, :]))
                c_iter_done[i] = P.cnt["pe"]
                if i + 2 < 16:
                    issue_loads(i + 2)
                if hg == 3:
                    c_a = P.cnt["ve"]
                    c_evac = {}
                    for hc in range(NCH):
                        bank = hc % 2
                        if hc == 0:
                            ph.wait("t", "ve", c_a)
                            ph.wait("t", "wl", c_wo)
                        if hc >= 2:
                            ph.wait("t", "ve", c_evac[hc - 2])
                        for cc in range(NCH):
                            ph.do("t", lambda e, cc=cc, hc=hc, bank=bank: e.matmul(
                                sc_ps[:, bank * 512:(bank + 1) * 512],
                                wo_sb[:, cc * D + hc * 128: cc * D + (hc + 1) * 128],
                                abuf[:, cc * 512:(cc + 1) * 512],
                                start=(cc == 0), stop=(cc == NCH - 1)),
                                inc=(None if cc < NCH - 1 else "pe"))
                        c_pe = P.cnt["pe"]
                        ph.wait("v", "pe", c_pe)
                        c_evac[hc] = ph.do("v", lambda e, hc=hc, b=b, bank=bank:
                                           e.scalar_tensor_tensor(
                            x_fm[:, hc * T + b * 512: hc * T + b * 512 + 512],
                            sc_ps[:, bank * 512:(bank + 1) * 512],
                            bout_sb[:, hc:hc + 1],
                            x_fm[:, hc * T + b * 512: hc * T + b * 512 + 512],
                            ALU.add, ALU.add))
                    c_iter_done[i] = P.cnt["pe"]
            ph.flush()


CFG_L0A = dict(uid="A", kind="ln", eps=1e-5, gain=None, rope=False,
               wq="l0a_wqt", wk="l0a_wkt", wv="l0a_wvt", wo="l0a_wot",
               bout="l0a_bout", qb=None, kb=None, vb=None)
CFG_L0F_ATT = dict(uid="F", kind="rms", eps=1e-6, gain="l0f_n1w", rope=True,
                   wq="l0f_wqt", wk="l0f_wkt", wv="l0f_wvt", wo="l0f_wot",
                   bout="l0f_bout", qb="l0f_qb", kb="l0f_kb", vb="l0f_vb")


def emit_l0f_outer(P, ph):
    nc = P.nc
    x_fm = P.sb["x_fm"]
    xsp = P.dint("xspill", [128, NCH * T], F32)
    c = ph.do("y", lambda e: e.dma_start(xsp[:, :], x_fm[:, :]), inc="st", n=16)
    ph.wait("y", "st", c)
    ph.flush()
    emit_norm_fm(P, ph, x_fm, x_fm, "ln", 1e-5, uid="L0FO")
    ph.flush()


def emit_l0f_close(P, ph):
    nc = P.nc
    x_fm = P.sb["x_fm"]
    xsp = P.dram["xspill"]
    with nc.sbuf_tensor("xrel", [128, NCH * T], F32) as xr:
        c = ph.do("y", lambda e: e.dma_start(xr[:, :], xsp[:, :]), inc="al", n=16)
        ph.wait("v", "al", c)
        for cc in range(NCH):
            for tt in range(NTT):
                ph.do("v", lambda e, cc=cc, tt=tt: e.tensor_add(
                    x_fm[:, cc * T + tt * TTW: cc * T + (tt + 1) * TTW],
                    x_fm[:, cc * T + tt * TTW: cc * T + (tt + 1) * TTW],
                    xr[:, cc * T + tt * TTW: cc * T + (tt + 1) * TTW]))
        ph.flush()


def build_launch(phases):
    nc = bass.Bass("TRN2", target_bir_lowering=False)
    P = Prog(nc)
    ph = P.phase()
    emit_consts(P, ph)
    for p in phases:
        if p == "in":
            emit_input_convert(P, ph)
        elif p == "fmload":
            emit_fm_load(P, ph)
        elif p == "l0a":
            emit_attn(P, ph, CFG_L0A)
        elif p == "l0f_outer":
            emit_l0f_outer(P, ph)
        elif p == "l0f_att":
            emit_attn(P, ph, CFG_L0F_ATT)
        elif p == "l0f_moe":
            emit_moe_l0(P, ph)
        elif p == "l0f_close":
            emit_l0f_close(P, ph)
        elif p == "l1a":
            emit_l1_attn(P, ph)
        elif p == "l1f":
            emit_moe_l1(P, ph)
        elif p == "out":
            emit_output_convert(P, ph)
        elif p == "fmstore":
            emit_fm_store(P, ph)
        else:
            raise ValueError(p)
    return P


L0F = ["l0f_outer", "l0f_att", "l0f_moe", "l0f_close"]
PLAN = [["in", "l0a"] + L0F + ["l1a", "l1f", "out"]]


def kernel(**inputs):
    g, percore = host_prep(inputs)
    hand = None
    final = [None] * NCORES
    for phases in PLAN:
        P = build_launch(phases)
        innames = P.innames
        in_maps = []
        for c in range(NCORES):
            m = {}
            for n in innames:
                if n in percore[c]:
                    m[n] = percore[c][n]
                elif n == "xfm_in":
                    m[n] = hand[c]
                else:
                    m[n] = g[n]
            in_maps.append(m)
        res = run_bass_kernel_spmd(P.nc, in_maps, core_ids=list(range(NCORES)))
        r0 = res.results
        if "xfm_out" in r0[0]:
            hand = [r0[c]["xfm_out"] for c in range(NCORES)]
        if "x_out" in r0[0]:
            final = [r0[c]["x_out"] for c in range(NCORES)]
    return np.concatenate(final, axis=1)


def emit_moe_l0(P, ph):
    raise NotImplementedError


def emit_moe_l1(P, ph):
    raise NotImplementedError


def emit_l1_attn(P, ph):
    raise NotImplementedError


def emit_gating(P, ph, zget, gname, comb_fm, uid, tks, alpha_sb=None, tk0=0):
    nc = P.nc
    iota8v = P.sb["iota8v"]
    idn = P.sb["idn"]
    dg = P.din(gname, [128, NCH * 8], F32)
    with nc.sbuf_tensor(f"gw{uid}", [128, NCH * 8], BF16) as gsb, \
         nc.sbuf_tensor(f"gt{uid}", [128, 40], F32) as gt, \
         nc.sbuf_tensor(f"i100{uid}", [128, 8], F32) as i100, \
         nc.psum_tensor(f"gp{uid}", [128, 512], F32) as gps, \
         nc.psum_tensor(f"tp{uid}", [8, 1024], F32) as tps:
        c_g = ph.do("g", lambda e: e.dma_start(gsb[:, :], dg[:, :]), inc="wl", n=16)
        lg = gt[:, 0:8]; e1 = gt[:, 8:16]; t8 = gt[:, 16:24]
        m1 = gt[:, 24:25]; m2 = gt[:, 25:26]; p1 = gt[:, 26:27]; p2 = gt[:, 27:28]
        nm1 = gt[:, 28:29]; tx = gt[:, 29:30]; g1 = gt[:, 30:31]; g2 = gt[:, 31:32]
        mk1 = gt[:, 32:40]
        c_i = ph.do("v", lambda e: e.tensor_scalar(i100[:, :], iota8v[:, :], 100.0,
                                                   None, ALU.add))
        prev = {}
        for it, tk in enumerate(tks):
            deps = [("wl", c_g), ("ve", c_i)]
            if it >= 1:
                deps.append(("ve", prev[it - 1]))
            s_ = Seq(ph, deps=deps)
            for sem, v in deps:
                ph.wait("t", sem, v)
            for cc in range(NCH):
                ph.do("t", lambda e, cc=cc, tk=tk: e.matmul(
                    gps[:, 0:8], zget(cc, tk),
                    gsb[:, cc * 8:(cc + 1) * 8],
                    start=(cc == 0), stop=(cc == NCH - 1)),
                    inc=(None if cc < NCH - 1 else "pe"))
            s_.deps = [("pe", P.cnt["pe"])]
            s_.do("v", lambda e: e.tensor_copy(lg, gps[:, 0:8]))
            s_.do("v", lambda e: e.reduce_max(m1, lg, axis=AXX))
            s_.do("v", lambda e: e.tensor_scalar(e1, lg, m1, None, ALU.is_equal))
            s_.do("v", lambda e: e.scalar_tensor_tensor(t8, e1, -100.0, i100[:, :],
                                                        ALU.mult, ALU.add))
            s_.do("v", lambda e: e.tensor_reduce(p1, t8, AXX, ALU.min))
            s_.do("v", lambda e: e.tensor_scalar(mk1, iota8v[:, :], p1, None, ALU.is_equal))
            s_.do("v", lambda e: e.scalar_tensor_tensor(t8, mk1, -1e30, lg,
                                                        ALU.mult, ALU.add))
            s_.do("v", lambda e: e.reduce_max(m2, t8, axis=AXX))
            s_.do("v", lambda e: e.tensor_scalar(e1, t8, m2, None, ALU.is_equal))
            s_.do("v", lambda e: e.scalar_tensor_tensor(t8, e1, -100.0, i100[:, :],
                                                        ALU.mult, ALU.add))
            s_.do("v", lambda e: e.tensor_reduce(p2, t8, AXX, ALU.min))
            s_.do("v", lambda e: e.tensor_scalar(nm1, m1, -1.0, None, ALU.mult))
            s_.do("s", lambda e: e.activation(tx, m2, AF.Exp, bias=nm1))
            s_.do("v", lambda e: e.tensor_scalar(tx, tx, 1.0, None, ALU.add))
            s_.do("v", lambda e: e.reciprocal(g1, tx))
            s_.do("v", lambda e: e.tensor_scalar(g2, g1, -1.0, 1.0, ALU.mult, ALU.add))
            s_.do("v", lambda e: e.tensor_scalar(mk1, mk1, g1, None, ALU.mult))
            s_.do("v", lambda e: e.tensor_scalar(e1, iota8v[:, :], p2, None, ALU.is_equal))
            s_.do("v", lambda e: e.scalar_tensor_tensor(lg, e1, g2, mk1,
                                                        ALU.mult, ALU.add))
            if alpha_sb is not None:
                s_.do("v", lambda e: e.tensor_mul(lg, lg, alpha_sb[:, :]))
            bank = it % 2
            s_.do("t", lambda e, bank=bank: e.transpose(
                tps[:, bank * 512:bank * 512 + 128], lg, idn[:, :]))
            prev[it] = s_.do("v", lambda e, tk=tk, bank=bank: e.tensor_copy(
                comb_fm[0:8, (tk - tk0) * 128:(tk - tk0 + 1) * 128],
                tps[0:8, bank * 512:bank * 512 + 128]))
        ph.flush()
    return P.cnt["ve"]



def _moe_bufs(P, st, uid, NF, ntt, mode, blk1):
    nc = P.nc
    B = {}
    B["h"] = st.enter_context(nc.sbuf_tensor(f"h{uid}", [128, NF * ntt * TTW], BF16))
    B["w1"] = st.enter_context(nc.sbuf_tensor(f"w1{uid}", [128, 2 * blk1], BF16))
    B["w2"] = st.enter_context(nc.sbuf_tensor(f"w2{uid}", [128, 2 * blk1], BF16)) \
        if mode == "l0" else None
    B["w3"] = st.enter_context(nc.sbuf_tensor(f"w3{uid}", [128, 2 * NF * 128], BF16))
    B["b1"] = st.enter_context(nc.sbuf_tensor(f"b1{uid}", [128, NF], F32))
    B["b2"] = st.enter_context(nc.sbuf_tensor(f"b2{uid}", [128, NF], F32)) \
        if mode == "l0" else None
    B["b3"] = st.enter_context(nc.sbuf_tensor(f"b3{uid}", [1, D], BF16))
    B["cr"] = st.enter_context(nc.sbuf_tensor(f"cr{uid}", [1, ntt * TTW], BF16))
    B["cp"] = st.enter_context(nc.sbuf_tensor(f"cp{uid}", [128, ntt * TTW], F32))
    B["s1"] = st.enter_context(nc.sbuf_tensor(f"s1{uid}", [128, 2 * TTW], BF16))
    B["a2"] = st.enter_context(nc.sbuf_tensor(f"a2{uid}", [128, 2 * TTW], BF16)) \
        if mode == "l0" else None
    B["ze"] = st.enter_context(nc.sbuf_tensor(f"ze{uid}", [128, NCH * ntt * TTW], BF16)) \
        if mode == "l1" else None
    return B


def _emit_moe_experts(P, ph, B, zget, comb, x_fm, uid, NF, blk1,
                      dw1, dw2, dw3, db1, db2, db3, mode, tbase, ntt,
                      lnw=None, lnb=None, comb_base=None):
    nc = P.nc
    ones_row_bf = P.sb["ones_row_bf"]
    TGW = ntt * TTW
    h_sb = B["h"]; w1s = B["w1"]; w2s = B["w2"]; w3s = B["w3"]
    b1s = B["b1"]; b2s = B["b2"]; b3r = B["b3"]
    crow = B["cr"]; crep = B["cp"]; s1t = B["s1"]; a2t = B["a2"]; zes = B["ze"]
    with nc.psum_tensor(f"ap{uid}{tbase}", [128, 2048], F32) as aps, \
         nc.psum_tensor(f"yp{uid}{tbase}", [128, 1024], F32) as yps, \
         nc.psum_tensor(f"rp{uid}{tbase}", [128, 1024], F32) as rps:
        c_prev_y = 0
        c_prev_fc1 = 0
        c_prev_b = (0, 0)
        for e in range(MOE_NEXP):
            ph.wait("y", "se", c_prev_b[0])
            ph.wait("g", "pe", c_prev_b[1])
            c_b1 = ph.do("y", lambda e_, e=e: e_.dma_start(b1s[:, :], db1.ap()[e, :, :]),
                         inc="al", n=16)
            if mode == "l0":
                c_b1 = ph.do("y", lambda e_, e=e: e_.dma_start(b2s[:, :], db2.ap()[e, :, :]),
                             inc="al", n=16)
            c_b3 = ph.do("g", lambda e_, e=e: e_.dma_start(b3r[:, :], db3.ap()[e, :, :]),
                         inc="al", n=16)
            cb = tbase if comb_base is None else comb_base
            c_cr = ph.do("g", lambda e_, e=e, cb=cb: e_.dma_start(
                crow[:, :], comb[e:e + 1, cb:cb + TGW]), inc="al", n=16)
            ph.wait("t", "al", c_cr)
            ph.wait("t", "ve", P.cnt["ve"])
            c_crepq = {}
            for q in range(ntt):
                if q >= 2:
                    ph.wait("t", "ve", c_crepq[q - 2])
                ph.do("t", lambda e_, q=q: e_.matmul(
                    rps[:, (q % 2) * 512:(q % 2) * 512 + 512], ones_row_bf[:, :],
                    crow[0:1, q * TTW:(q + 1) * TTW], start=True, stop=True))
                c_pe = P.cnt["pe"]
                ph.wait("v", "pe", c_pe)
                c_crepq[q] = ph.do("v", lambda e_, q=q: e_.tensor_copy(
                    crep[:, q * TTW:(q + 1) * TTW],
                    rps[:, (q % 2) * 512:(q % 2) * 512 + 512]))
            c_creps = P.cnt["ve"]
            if mode == "l1":
                ph.wait("s", "pe", c_prev_fc1)
                ph.wait("s", "al", c_b1)
                for cc in range(NCH):
                    for q in range(ntt):
                        ph.do("s", lambda e_, cc=cc, q=q, e=e: e_.activation(
                            zes[:, cc * TGW + q * TTW: cc * TGW + (q + 1) * TTW],
                            zget(cc, q),
                            AF.Identity, bias=lnb[:, e * NCH + cc:e * NCH + cc + 1],
                            scale=lnw[:, e * NCH + cc:e * NCH + cc + 1]))
                c_ze = P.cnt["se"]
                zin = lambda cc, q: zes[:, cc * TGW + q * TTW: cc * TGW + (q + 1) * TTW]
            else:
                c_ze = 0
                zin = zget
            wdmas = {}

            def load_blk(fc, e=e):
                dep = wdmas.get(fc - 2)
                if dep is not None:
                    ph.wait("g", "pe", dep[1])
                c1 = ph.do("g", lambda e_, fc=fc, e=e: e_.dma_start(
                    w1s.ap().rearrange("p (h w) -> p h w", h=2)[:, fc % 2:fc % 2 + 1, :],
                    dw1.ap()[e, fc:fc + 1, :, :].rearrange("o p w -> p o w")),
                    inc="wl", n=16)
                if mode == "l0":
                    c1 = ph.do("g", lambda e_, fc=fc, e=e: e_.dma_start(
                        w2s.ap().rearrange("p (h w) -> p h w", h=2)[:, fc % 2:fc % 2 + 1, :],
                        dw2.ap()[e, fc:fc + 1, :, :].rearrange("o p w -> p o w")),
                        inc="wl", n=16)
                wdmas[fc] = (c1, 0)

            load_blk(0)
            load_blk(1)
            c_sil = {}
            c_a2c = {}
            c_hh = {}
            for fc in range(NF):
                for q in range(ntt):
                    i = fc * ntt + q
                    bank = i % 2
                    if i == 0:
                        ph.wait("t", "wl", wdmas[fc][0])
                        ph.wait("t", "al", c_b1)
                        ph.wait("t", "se", c_ze)
                        ph.wait("t", "ve", c_creps)
                    elif q == 0:
                        ph.wait("t", "wl", wdmas[fc][0])
                    if i >= 2:
                        ph.wait("t", "se", c_sil[i - 2])
                    for cc in range(NCH):
                        ph.do("t", lambda e_, cc=cc, fc=fc, q=q, bank=bank: e_.matmul(
                            aps[:, bank * 512:bank * 512 + 512],
                            w1s[:, (fc % 2) * blk1 + cc * 128:(fc % 2) * blk1 + (cc + 1) * 128],
                            zin(cc, q),
                            start=(cc == 0), stop=(cc == NCH - 1)),
                            inc=(None if cc < NCH - 1 else "pe"))
                    c_a1 = P.cnt["pe"]
                    if mode == "l0":
                        if i >= 2:
                            ph.wait("t", "ve", c_a2c[i - 2])
                        for cc in range(NCH):
                            ph.do("t", lambda e_, cc=cc, fc=fc, q=q, bank=bank: e_.matmul(
                                aps[:, 1024 + bank * 512:1024 + bank * 512 + 512],
                                w2s[:, (fc % 2) * blk1 + cc * 128:(fc % 2) * blk1 + (cc + 1) * 128],
                                zin(cc, q),
                                start=(cc == 0), stop=(cc == NCH - 1)),
                                inc=(None if cc < NCH - 1 else "pe"))
                        c_a2 = P.cnt["pe"]
                    hofs = fc * TGW + q * TTW
                    ph.wait("s", "pe", c_a1)
                    if i >= 2:
                        ph.wait("s", "ve", c_hh[i - 2])
                    if mode == "l0":
                        c_sil[i] = ph.do("s", lambda e_, fc=fc, bank=bank, i=i: e_.activation(
                            s1t[:, (i % 2) * TTW:(i % 2 + 1) * TTW],
                            aps[:, bank * 512:bank * 512 + 512], AF.Silu,
                            bias=b1s[:, fc:fc + 1]))
                        ph.wait("v", "pe", c_a2)
                        c_a2c[i] = ph.do("v", lambda e_, fc=fc, q=q, bank=bank, i=i:
                                         e_.scalar_tensor_tensor(
                            a2t[:, (i % 2) * TTW:(i % 2 + 1) * TTW],
                            aps[:, 1024 + bank * 512:1024 + bank * 512 + 512],
                            b2s[:, fc:fc + 1],
                            crep[:, q * TTW:(q + 1) * TTW], ALU.add, ALU.mult))
                        ph.wait("v", "se", c_sil[i])
                        c_hh[i] = ph.do("v", lambda e_, hofs=hofs, i=i: e_.tensor_mul(
                            h_sb[:, hofs:hofs + TTW],
                            s1t[:, (i % 2) * TTW:(i % 2 + 1) * TTW],
                            a2t[:, (i % 2) * TTW:(i % 2 + 1) * TTW]))
                    else:
                        c_sil[i] = ph.do("s", lambda e_, fc=fc, bank=bank, i=i: e_.activation(
                            s1t[:, (i % 2) * TTW:(i % 2 + 1) * TTW],
                            aps[:, bank * 512:bank * 512 + 512], AF.Gelu_apprx_tanh,
                            bias=b1s[:, fc:fc + 1]))
                        ph.wait("v", "se", c_sil[i])
                        c_hh[i] = ph.do("v", lambda e_, hofs=hofs, q=q, i=i: e_.tensor_mul(
                            h_sb[:, hofs:hofs + TTW],
                            s1t[:, (i % 2) * TTW:(i % 2 + 1) * TTW],
                            crep[:, q * TTW:(q + 1) * TTW]))
                if fc + 2 < NF:
                    wdmas[fc] = (wdmas[fc][0], P.cnt["pe"])
                    load_blk(fc + 2)
            c_prev_fc1 = P.cnt["pe"]
            c_hall = P.cnt["ve"]
            if MOE_SKIP_FC3:
                c_prev_y = P.cnt["pe"]
                c_prev_b = (P.cnt["se"], P.cnt["pe"])
                continue
            w3dmas = {}

            def load_w3(dc, e=e):
                dep = w3dmas.get(dc - 2)
                if dep is not None:
                    ph.wait("g", "pe", dep[1])
                c3 = ph.do("g", lambda e_, dc=dc, e=e: e_.dma_start(
                    w3s.ap().rearrange("p (h w) -> p h w", h=2)[:, dc % 2:dc % 2 + 1, :],
                    dw3.ap()[e, dc:dc + 1, :, :].rearrange("o p w -> p o w")),
                    inc="wl", n=16)
                w3dmas[dc] = (c3, 0)

            load_w3(0)
            load_w3(1)
            c_ev = {}
            for dc in range(NCH):
                for q in range(ntt):
                    i2 = dc * ntt + q
                    bank = i2 % 2
                    if q == 0:
                        ph.wait("t", "wl", w3dmas[dc][0])
                    if i2 == 0:
                        ph.wait("t", "ve", c_hall)
                        ph.wait("t", "al", c_b3)
                    if i2 >= 2:
                        ph.wait("t", "ve", c_ev[i2 - 2])
                    for fcc in range(NF):
                        ph.do("t", lambda e_, fcc=fcc, dc=dc, q=q, bank=bank: e_.matmul(
                            yps[:, bank * 512:bank * 512 + 512],
                            w3s[:, (dc % 2) * NF * 128 + fcc * 128:(dc % 2) * NF * 128 + (fcc + 1) * 128],
                            h_sb[:, fcc * TGW + q * TTW: fcc * TGW + (q + 1) * TTW],
                            start=(fcc == 0), stop=False), inc=None)
                    ph.do("t", lambda e_, dc=dc, q=q, bank=bank: e_.matmul(
                        yps[:, bank * 512:bank * 512 + 512],
                        b3r[0:1, dc * 128:(dc + 1) * 128],
                        crow[0:1, q * TTW:(q + 1) * TTW],
                        start=False, stop=True, skip_group_check=True), inc="pe")
                    c_y = P.cnt["pe"]
                    ph.wait("v", "pe", c_y)
                    c_ev[i2] = ph.do("v", lambda e_, dc=dc, q=q, bank=bank, tbase=tbase:
                                     e_.scalar_tensor_tensor(
                        x_fm[:, dc * T + tbase + q * TTW: dc * T + tbase + (q + 1) * TTW],
                        yps[:, bank * 512:bank * 512 + 512], 1.0,
                        x_fm[:, dc * T + tbase + q * TTW: dc * T + tbase + (q + 1) * TTW],
                        ALU.mult, ALU.add))
                if dc + 2 < NCH:
                    w3dmas[dc] = (w3dmas[dc][0], P.cnt["pe"])
                    load_w3(dc + 2)
            c_prev_y = P.cnt["pe"]
            c_prev_b = (P.cnt["se"], P.cnt["pe"])


def emit_moe_l0(P, ph):
    from contextlib import ExitStack
    nc = P.nc
    x_fm = P.sb["x_fm"]
    NF = 16
    dn2 = P.din("l0f_n2w", [128, NCH], F32)
    dw1 = P.din("l0f_w1b", [8, NF, 128, NCH * 128], F32)
    dw2 = P.din("l0f_w2b", [8, NF, 128, NCH * 128], F32)
    dw3 = P.din("l0f_w3b", [8, NCH, 128, NF * 128], F32)
    db1 = P.din("l0f_b1", [8, 128, NF], F32)
    db2 = P.din("l0f_b2", [8, 128, NF], F32)
    db3 = P.din("l0f_b3", [8, 1, D], F32)
    with ExitStack() as st:
        zp = st.enter_context(nc.sbuf_tensor("l0m_z", [128, NCH * T], BF16))
        comb = st.enter_context(nc.sbuf_tensor("l0m_c", [8, T], F32))
        n2w = st.enter_context(nc.sbuf_tensor("l0m_n2", [128, NCH], F32))
        c_n2 = ph.do("y", lambda e: e.dma_start(n2w[:, :], dn2[:, :]), inc="al", n=16)
        ph.wait("v", "al", c_n2)
        emit_norm_fm(P, ph, x_fm, zp, "rms", 1e-6, w_sb=n2w, uid="l0m")
        emit_gating(P, ph,
                    lambda cc, tk: zp[:, cc * T + tk * 128: cc * T + (tk + 1) * 128],
                    "l0f_gatet", comb, "l0m", tks=range(16))
        if MOE_DBG:
            dz = P.dout("dump_mz", [128, NCH * T], BF16)
            dc_ = P.dout("dump_comb", [8, T], F32)
            c = ph.do("y", lambda e: e.dma_start(dz[:, :], zp[:, :]), inc="st", n=16)
            c = ph.do("y", lambda e: e.dma_start(dc_[:, :], comb[:, :]), inc="st", n=16)
            ph.wait("y", "st", c)
            ph.flush()
            return
        B = _moe_bufs(P, st, "l0m", NF, NTT, "l0", NCH * 128)
        _emit_moe_experts(P, ph, B,
                          lambda cc, q: zp[:, cc * T + q * TTW: cc * T + (q + 1) * TTW],
                          comb, x_fm, "l0m", NF, NCH * 128,
                          dw1, dw2, dw3, db1, db2, db3, "l0", tbase=0, ntt=NTT)
        ph.flush()


def emit_moe_l1(P, ph):
    from contextlib import ExitStack
    nc = P.nc
    x_fm = P.sb["x_fm"]
    ones_row = P.sb["ones_row"]
    NF = 32
    NTG = 2
    NTTG = 2
    dw1 = P.din("l1f_fc1b", [8, NF, 128, NCH * 128], F32)
    dw3 = P.din("l1f_fc2b", [8, NCH, 128, NF * 128], F32)
    db1 = P.din("l1f_b1", [8, 128, NF], F32)
    db3 = P.din("l1f_b2", [8, 1, D], F32)
    dlnw = P.din("l1f_lnw", [8, 128, NCH], F32)
    dlnb = P.din("l1f_lnb", [8, 128, NCH], F32)
    dalpha = P.din("l1f_alpha", [1, 8], F32)
    with ExitStack() as st:
        zp = st.enter_context(nc.sbuf_tensor("l1m_z", [128, NCH * NTTG * TTW], BF16))
        comb = st.enter_context(nc.sbuf_tensor("l1m_c", [8, NTTG * TTW], F32))
        lnw = st.enter_context(nc.sbuf_tensor("l1m_lw", [128, 8 * NCH], F32))
        lnb = st.enter_context(nc.sbuf_tensor("l1m_lb", [128, 8 * NCH], F32))
        arow = st.enter_context(nc.sbuf_tensor("l1m_ar", [1, 8], F32))
        alph = st.enter_context(nc.sbuf_tensor("l1m_al", [128, 8], F32))
        ph.do("y", lambda e: e.dma_start(
            lnw.ap().rearrange("p (e c) -> p e c", e=8),
            dlnw.ap().rearrange("e p c -> p e c")), inc="al", n=16)
        ph.do("y", lambda e: e.dma_start(
            lnb.ap().rearrange("p (e c) -> p e c", e=8),
            dlnb.ap().rearrange("e p c -> p e c")), inc="al", n=16)
        c_a = ph.do("y", lambda e: e.dma_start(arow[:, :], dalpha[:, :]), inc="al", n=16)
        with nc.psum_tensor("l1m_alp", [128, 64], F32) as alps:
            sq = Seq(ph, deps=[("al", c_a)])
            sq.do("t", lambda e: e.matmul(alps[:, 0:8], ones_row[:, :], arow[:, :],
                                          start=True, stop=True))
            c_alph = sq.do("v", lambda e: e.tensor_copy(alph[:, :], alps[:, 0:8]))
        if True:
            B = _moe_bufs(P, st, "l1m", NF, NTTG, "l1", NCH * 128)
            for tg in range(NTG):
                tbase = tg * NTTG * TTW
                tts = [tg * NTTG + k for k in range(NTTG)]
                emit_norm_fm(P, ph, x_fm, zp, "ln", 1e-5, uid=f"l1o{tg}",
                             tts=tts, dst_stride=NTTG * TTW)
                emit_gating(P, ph,
                            lambda cc, tk, tbase=tbase: zp[:, cc * NTTG * TTW + tk * 128 - tbase:
                                                           cc * NTTG * TTW + (tk + 1) * 128 - tbase],
                            "l1f_wg", comb, f"l1m{tg}",
                            tks=range(tg * NTTG * 4, (tg + 1) * NTTG * 4),
                            alpha_sb=alph, tk0=tg * NTTG * 4)
                emit_norm_fm(P, ph, zp, zp, "ln", 1e-5, uid=f"l1i{tg}",
                             tts=list(range(NTTG)), dst_stride=NTTG * TTW,
                             src_stride=NTTG * TTW)
                _emit_moe_experts(P, ph, B,
                                  lambda cc, q: zp[:, cc * NTTG * TTW + q * TTW:
                                                   cc * NTTG * TTW + (q + 1) * TTW],
                                  comb, x_fm, f"l1m{tg}", NF, NCH * 128,
                                  dw1, None, dw3, db1, None, db3, "l1",
                                  tbase=tbase, ntt=NTTG, lnw=lnw, lnb=lnb, comb_base=0)
        ph.flush()


KK_PAIRS = [(0, 1), (0, 2), (1, 2), (0, 3), (1, 3), (2, 3)]
KQ_PAIRS = [(0, 0), (0, 1), (1, 1), (0, 2), (1, 2), (2, 2),
            (0, 3), (1, 3), (2, 3), (3, 3)]


def emit_l1_attn(P, ph):
    from contextlib import ExitStack
    nc = P.nc
    x_fm = P.sb["x_fm"]
    ones_col = P.sb["ones_col"]; ones_row = P.sb["ones_row"]
    ones_col_bf = P.sb["ones_col_bf"]; ones_row_bf = P.sb["ones_row_bf"]
    rollp = P.sb["rollp"]
    dwq = P.din("l1a_wq", [128, NCH * D], F32)
    dwk = P.din("l1a_wk", [128, NCH * D], F32)
    dwv = P.din("l1a_wv", [128, NCH * D], F32)
    dwb = P.din("l1a_wbm", [128, NCH * 16], F32)
    dwo = P.din("l1a_wo", [128, NCH * D], F32)
    dbo = P.din("l1a_bo", [128, NCH], F32)

    dy = P.dint("l1a_y", [D, T], BF16)
    with ExitStack() as st:
        zbuf = st.enter_context(nc.sbuf_tensor("l1a_z", [128, NCH * T], BF16))
        beta_fm = st.enter_context(nc.sbuf_tensor("l1a_bfs", [16, T], BF16))
        brow = st.enter_context(nc.sbuf_tensor("l1a_br", [1, T], F32))
        rows = st.enter_context(nc.sbuf_tensor("l1a_rw", [1, 5 * TTW], F32))
        rowsb = st.enter_context(nc.sbuf_tensor("l1a_rwb", [1, 16 * TTW], BF16))
        bo_sb = st.enter_context(nc.sbuf_tensor("l1a_bos", [128, NCH], F32))
        c_bo = ph.do("y", lambda e: e.dma_start(bo_sb[:, :], dbo[:, :]), inc="al", n=16)

        emit_norm_fm(P, ph, x_fm, zbuf, "ln", 1e-5, uid="l1a")
        c_z = P.cnt["ve"]

        with nc.sbuf_tensor("l1a_wbs", [128, NCH * 16], BF16) as wbs, \
             nc.psum_tensor("l1a_bp", [16, 512], F32) as bps:
            c_wb = ph.do("g", lambda e: e.dma_start(wbs[:, :], dwb[:, :]), inc="wl", n=16)
            base_se = P.cnt["se"]
            for tt in range(NTT):
                if tt == 0:
                    ph.wait("t", "wl", c_wb)
                    ph.wait("t", "ve", c_z)
                if tt >= 1:
                    ph.wait("t", "se", base_se + tt)
                for cc in range(NCH):
                    ph.do("t", lambda e, cc=cc, tt=tt: e.matmul(
                        bps[:, :], wbs[:, cc * 16:(cc + 1) * 16],
                        zbuf[:, cc * T + tt * TTW: cc * T + (tt + 1) * TTW],
                        start=(cc == 0), stop=(cc == NCH - 1)),
                        inc=(None if cc < NCH - 1 else "pe"))
                c_pe = P.cnt["pe"]
                ph.wait("s", "pe", c_pe)
                ph.do("s", lambda e, tt=tt: e.activation(
                    beta_fm[:, tt * TTW:(tt + 1) * TTW], bps[:, :], AF.Sigmoid))
            c_brow = P.cnt["se"]
            ph.flush()

        with nc.sbuf_tensor("l1a_wp", [128, 2 * 3 * NCH * 128], BF16) as wpair, \
             nc.sbuf_tensor("l1a_qkv", [128, 3 * T], BF16) as qkv, \
             nc.sbuf_tensor("l1a_cat", [128, 2 * T], BF16) as cat, \
             nc.sbuf_tensor("l1a_phi", [128, T], BF16) as phif, \
             nc.sbuf_tensor("l1a_kphi", [128, T], BF16) as kphi, \
             nc.sbuf_tensor("l1a_qphi", [128, T], BF16) as qphi, \
             nc.sbuf_tensor("l1a_dp", [128, TTW], F32) as dprod, \
             nc.sbuf_tensor("l1a_u", [64, 6 * TTW], F32) as ubuf, \
             nc.sbuf_tensor("l1a_yt", [64, TTW], BF16) as ytile, \
             nc.sbuf_tensor("l1a_vt", [64, TTW], BF16) as vtile, \
             nc.psum_tensor("l1a_pj", [128, 1024], F32) as prj_ps, \
             nc.psum_tensor("l1a_rp", [128, 1024], F32) as rp_ps, \
             nc.psum_tensor("l1a_np", [1, 1024], F32) as nps, \
             nc.psum_tensor("l1a_rr", [128, 1024], F32) as rep_ps:
            arow = lambda i: rowsb[0:1, KK_PAIRS.index(i) * TTW:
                                   (KK_PAIRS.index(i) + 1) * TTW]
            crow_ = lambda i: rowsb[0:1, (6 + KQ_PAIRS.index(i)) * TTW:
                                    (6 + KQ_PAIRS.index(i) + 1) * TTW]
            qrrow = lambda s: rows[0:1, s * TTW:(s + 1) * TTW]
            mrow = rows[0:1, 4 * TTW:5 * TTW]
            PW = NCH * 128

            wdma = {}

            def load_pair(pr):
                half = pr % 2
                dep = wdma.get(pr - 2)
                if dep is not None:
                    ph.wait("g", "pe", dep[1])
                c = 0
                for wi, dw in enumerate((dwq, dwk, dwv)):
                    c = ph.do("g", lambda e, dw=dw, half=half, wi=wi, pr=pr: e.dma_start(
                        wpair.ap().rearrange("p (h w c) -> p h w c", h=2, w=3)[
                            :, half, wi, :].rearrange("p (cc m) -> p cc m", cc=NCH),
                        dw.ap().rearrange("p (cc m) -> p cc m", cc=NCH)[
                            :, :, pr * 128:(pr + 1) * 128]), inc="wl", n=16)
                wdma[pr] = (c, 0)

            load_pair(0)
            load_pair(1)
            c_prev_heads = c_z
            for pr in range(8):
                half = pr % 2
                base_se = P.cnt["se"]
                i = 0
                for wi in range(3):
                    for tt in range(NTT):
                        bank = i % 2
                        if i == 0:
                            ph.wait("t", "wl", wdma[pr][0])
                            ph.wait("t", "ve", c_prev_heads)
                        if i >= 2:
                            ph.wait("t", "se", base_se + i - 1)
                        for cc in range(NCH):
                            ph.do("t", lambda e, cc=cc, wi=wi, tt=tt, bank=bank, half=half:
                                  e.matmul(
                                      prj_ps[:, bank * 512:(bank + 1) * 512],
                                      wpair[:, half * 3 * PW + wi * PW + cc * 128:
                                            half * 3 * PW + wi * PW + (cc + 1) * 128],
                                      zbuf[:, cc * T + tt * TTW: cc * T + (tt + 1) * TTW],
                                      start=(cc == 0), stop=(cc == NCH - 1)),
                                  inc=(None if cc < NCH - 1 else "pe"))
                        c_pe = P.cnt["pe"]
                        ph.wait("s", "pe", c_pe)
                        ph.do("s", lambda e, wi=wi, tt=tt, bank=bank: e.activation(
                            qkv[:, wi * T + tt * TTW: wi * T + (tt + 1) * TTW],
                            prj_ps[:, bank * 512:(bank + 1) * 512], AF.Identity))
                        i += 1
                wdma[pr] = (wdma[pr][0], P.cnt["pe"])
                if pr + 2 < 8:
                    load_pair(pr + 2)
                c_proj = P.cnt["se"]

                sq = Seq(ph, deps=[("se", c_proj)])
                for hj in range(2):
                    h = pr * 2 + hj
                    hb = hj * 64
                    sq.do("g", lambda e, h=h: e.dma_start(
                        brow[:, :], beta_fm[h:h + 1, :]), inc="al", n=16)
                    for which, (src_off, phidst) in enumerate(
                            ((T, kphi), (0, qphi))):
                        sq.do("s", lambda e, src_off=src_off, hb=hb: e.activation(
                            cat[0:64, 0:T], qkv[src_off + 0:src_off + 0][0:0, 0:0]
                            if False else qkv[hb:hb + 64, src_off:src_off + T],
                            AF.Relu))
                        sq.do("s", lambda e, src_off=src_off, hb=hb: e.activation(
                            cat[64:128, 0:T], qkv[hb:hb + 64, src_off:src_off + T],
                            AF.Relu, scale=-1.0))
                        for tt in range(NTT):
                            bank = tt % 2
                            sq.do("t", lambda e, tt=tt, bank=bank: e.matmul(
                                rp_ps[:, bank * 512:(bank + 1) * 512], rollp[:, :],
                                cat[:, tt * TTW:(tt + 1) * TTW], start=True, stop=True))
                            sq.do("v", lambda e, tt=tt, bank=bank: e.tensor_mul(
                                phif[:, tt * TTW:(tt + 1) * TTW],
                                cat[:, tt * TTW:(tt + 1) * TTW],
                                rp_ps[:, bank * 512:(bank + 1) * 512]))
                            sq.do("t", lambda e, tt=tt, bank=bank: e.matmul(
                                nps[:, bank * 512:(bank + 1) * 512], ones_col_bf[:, :],
                                phif[:, tt * TTW:(tt + 1) * TTW], start=True, stop=True))
                            sq.do("v", lambda e, tt=tt, bank=bank: e.tensor_scalar(
                                mrow[:, tt * 128: tt * 128 + 512][0:1, 0:512]
                                if False else mrow,
                                nps[0:1, bank * 512:(bank + 1) * 512], 1e-6, None,
                                ALU.add))
                            sq.do("v", lambda e, which=which, tt=tt: e.reciprocal(
                                (qrrow(tt) if which == 1 else mrow), mrow))
                            if which == 0:
                                sq.do("t", lambda e, tt=tt, bank=bank: e.matmul(
                                    rep_ps[:, bank * 512:(bank + 1) * 512][0:64, :]
                                    if False else rep_ps[0:64, bank * 512:(bank + 1) * 512],
                                    ones_row[0:1, 0:64], mrow, start=True, stop=True))
                                sq.do("t", lambda e, tt=tt, bank=bank: e.matmul(
                                    rep_ps[64:128, bank * 512:(bank + 1) * 512],
                                    ones_row[0:1, 64:128], mrow, start=True, stop=True))
                                sq.do("v", lambda e, tt=tt, bank=bank: e.tensor_mul(
                                    kphi[:, tt * TTW:(tt + 1) * TTW],
                                    phif[:, tt * TTW:(tt + 1) * TTW],
                                    rp_ps[:, bank * 512:(bank + 1) * 512] if False
                                    else rep_ps[:, bank * 512:(bank + 1) * 512]))
                            else:
                                sq.do("v", lambda e, tt=tt: e.tensor_copy(
                                    qphi[:, tt * TTW:(tt + 1) * TTW],
                                    phif[:, tt * TTW:(tt + 1) * TTW]))
                    for (i_, s_) in KK_PAIRS:
                        sq.do("v", lambda e, i_=i_, s_=s_: e.tensor_mul(
                            dprod[:, :], kphi[:, i_ * TTW:(i_ + 1) * TTW],
                            kphi[:, s_ * TTW:(s_ + 1) * TTW]))
                        sq.do("t", lambda e: e.matmul(
                            nps[:, 0:512], ones_col[:, :], dprod[:, :],
                            start=True, stop=True))
                        sq.do("v", lambda e, i_=i_, s_=s_: e.tensor_copy(
                            arow((i_, s_)), nps[0:1, 0:512]))
                    for (i_, s_) in KQ_PAIRS:
                        sq.do("v", lambda e, i_=i_, s_=s_: e.tensor_mul(
                            dprod[:, :], kphi[:, i_ * TTW:(i_ + 1) * TTW],
                            qphi[:, s_ * TTW:(s_ + 1) * TTW]))
                        sq.do("t", lambda e: e.matmul(
                            nps[:, 0:512], ones_col[:, :], dprod[:, :],
                            start=True, stop=True))
                        sq.do("v", lambda e, i_=i_, s_=s_: e.tensor_copy(
                            crow_((i_, s_)), nps[0:1, 0:512]))

                    u = lambda i: ubuf[0:64, i * TTW:(i + 1) * TTW]
                    tmp = ubuf[0:64, 4 * TTW:5 * TTW]
                    yac = ubuf[0:64, 5 * TTW:6 * TTW]
                    vsl = lambda s: qkv[hb:hb + 64, 2 * T + s * TTW: 2 * T + (s + 1) * TTW]

                    def rep64(row_ap, bank, bf=False):
                        orow = ones_row_bf if bf else ones_row
                        sq.do("t", lambda e, row_ap=row_ap, bank=bank, orow=orow:
                              e.matmul(rep_ps[0:64, bank * 512:(bank + 1) * 512],
                                       orow[0:1, 0:64], row_ap, start=True, stop=True))
                        return rep_ps[0:64, bank * 512:(bank + 1) * 512]

                    def betarow(s):
                        return brow[0:1, s * TTW:(s + 1) * TTW]

                    for s_ in range(NB):
                        sq.do("v", lambda e, s_=s_: e.tensor_copy(vtile[:, :], vsl(s_)))
                        if s_ == 0:
                            r = rep64(betarow(0), 0)
                            sq.do("v", lambda e, r=r: e.tensor_mul(u(0), vtile[:, :], r))
                        else:
                            for ii in range(s_):
                                r = rep64(arow((ii, s_)), ii % 2, bf=True)
                                if ii == 0:
                                    sq.do("v", lambda e, r=r, ii=ii: e.tensor_mul(
                                        tmp, u(ii), r))
                                else:
                                    sq.do("v", lambda e, r=r, ii=ii: e.tensor_mul(
                                        dprod[0:64, :], u(ii), r))
                                    sq.do("v", lambda e: e.tensor_add(
                                        tmp, tmp, dprod[0:64, :]))
                            sq.do("v", lambda e: e.tensor_sub(tmp, vtile[:, :], tmp))
                            r = rep64(betarow(s_), s_ % 2)
                            sq.do("v", lambda e, r=r, s_=s_: e.tensor_mul(u(s_), tmp, r))
                        for ii in range(s_ + 1):
                            r = rep64(crow_((ii, s_)), ii % 2, bf=True)
                            if ii == 0:
                                sq.do("v", lambda e, r=r, ii=ii: e.tensor_mul(
                                    yac, u(ii), r))
                            else:
                                sq.do("v", lambda e, r=r, ii=ii: e.tensor_mul(
                                    dprod[0:64, :], u(ii), r))
                                sq.do("v", lambda e: e.tensor_add(
                                    yac, yac, dprod[0:64, :]))
                        r = rep64(qrrow(s_), s_ % 2)
                        sq.do("v", lambda e, r=r: e.tensor_mul(ytile[:, :], yac, r))
                        sq.do("y", lambda e, h=h, s_=s_: e.dma_start(
                            dy.ap()[h * 64:(h + 1) * 64, s_ * TTW:(s_ + 1) * TTW],
                            ytile[:, :]), inc="st", n=16)
                c_prev_heads = P.cnt["ve"]
            ph.flush()

        with nc.sbuf_tensor("l1a_wos", [128, NCH * D], BF16) as wo_sb, \
             nc.sbuf_tensor("l1a_ab", [128, 2 * NCH * 512], BF16) as absl, \
             nc.psum_tensor("l1a_op", [128, 1024], F32) as op_ps:
            c_wo = ph.do("g", lambda e: e.dma_start(
                wo_sb.ap().rearrange("p (c m) -> p c m", c=NCH),
                dwo.ap().rearrange("p (c m) -> p c m", c=NCH)), inc="wl", n=16)
            c_y = P.cnt["st"]
            c_ldt = {}
            c_tt_pe = {}

            def load_a(tt):
                ph.wait("y", "st", c_y)
                if tt >= 2:
                    ph.wait("y", "pe", c_tt_pe[tt - 2])
                c_ldt[tt] = ph.do("y", lambda e, tt=tt: e.dma_start(
                    absl.ap().rearrange("p (h c w) -> p h c w", h=2, c=NCH)[
                        :, tt % 2, :, :],
                    dy.ap().rearrange("(c p) t -> p c t", p=128)[
                        :, :, tt * TTW:(tt + 1) * TTW]), inc="al", n=16)

            load_a(0)
            load_a(1)
            c_ev = {}
            for tt in range(NTT):
                for hc in range(NCH):
                    i = tt * NCH + hc
                    bank = i % 2
                    if hc == 0:
                        ph.wait("t", "al", c_ldt[tt])
                        if tt == 0:
                            ph.wait("t", "wl", c_wo)
                            ph.wait("t", "al", c_bo)
                    if i >= 2:
                        ph.wait("t", "ve", c_ev[i - 2])
                    for cc in range(NCH):
                        ph.do("t", lambda e, cc=cc, hc=hc, tt=tt, bank=bank: e.matmul(
                            op_ps[:, bank * 512:(bank + 1) * 512],
                            wo_sb[:, cc * D + hc * 128: cc * D + (hc + 1) * 128],
                            absl[:, (tt % 2) * NCH * 512 + cc * 512:
                                 (tt % 2) * NCH * 512 + (cc + 1) * 512],
                            start=(cc == 0), stop=(cc == NCH - 1)),
                            inc=(None if cc < NCH - 1 else "pe"))
                    c_pe = P.cnt["pe"]
                    ph.wait("v", "pe", c_pe)
                    c_ev[i] = ph.do("v", lambda e, hc=hc, tt=tt, bank=bank:
                                    e.scalar_tensor_tensor(
                        x_fm[:, hc * T + tt * TTW: hc * T + (tt + 1) * TTW],
                        op_ps[:, bank * 512:(bank + 1) * 512],
                        bo_sb[:, hc:hc + 1],
                        x_fm[:, hc * T + tt * TTW: hc * T + (tt + 1) * TTW],
                        ALU.add, ALU.add))
                c_tt_pe[tt] = P.cnt["pe"]
                if tt + 2 < NTT:
                    load_a(tt + 2)
            ph.flush()


# version 03
# speedup vs baseline: 1.2785x; 1.0155x over previous
import numpy as np
import ml_dtypes
import concourse.bass as bass
import concourse.mybir as mybir
from concourse.bass_utils import run_bass_kernel_spmd

F32 = mybir.dt.float32
BF16 = mybir.dt.bfloat16
AF = mybir.ActivationFunctionType
ALU = mybir.AluOpType
AXX = mybir.AxisListType.X

NCORES = 8
NB = 4
NL = 512
T = NB * NL
D = 1024
NCH = D // 128
NTT = 4
TTW = 512
SEQ = 4096
NH = 16
HD = 64
VDIM = NH * 65

BF = ml_dtypes.bfloat16
MOE_DBG = False
MOE_NEXP = 8
MOE_SKIP_FC3 = False
ESEM = {"t": "pe", "v": "ve", "s": "se", "g": "ge"}



def _blocks(wT, mblk=128):
    K, M = wT.shape
    kc = K // 128
    nb = M // mblk
    return np.ascontiguousarray(
        wT.reshape(kc, 128, nb, mblk).transpose(2, 1, 0, 3).reshape(nb, 128, kc * mblk))


def _chunkmajor(wT):
    K, M = wT.shape
    kc = K // 128
    return np.ascontiguousarray(wT.reshape(kc, 128, M).transpose(1, 0, 2).reshape(128, kc * M))


def _rope_tables(core):
    n = core * NL + np.arange(NL, dtype=np.float64)
    inv = 1.0 / (10000.0 ** (np.arange(0, 64, 2, dtype=np.float64) / 64.0))
    fr = n[:, None] * inv[None, :]
    emb = np.concatenate([fr, fr], axis=1)
    s = 64.0 ** -0.25
    cs = (np.cos(emb) * s).astype(np.float32).T
    sn = (np.sin(emb) * s).astype(np.float32).T
    return (np.ascontiguousarray(np.tile(cs, (2, NB))),
            np.ascontiguousarray(np.tile(sn, (2, NB))))


def _rotperm():
    m = np.zeros((128, 128), np.float32)
    for blk in (0, 64):
        for i in range(32):
            m[blk + 32 + i, blk + i] = -1.0
            m[blk + i, blk + 32 + i] = 1.0
    return m.astype(BF)


def _rollperm():
    m = np.zeros((128, 128), np.float32)
    for i in range(128):
        m[(i - 1) % 128, i] = 1.0
    return m.astype(BF)


def _pp(vec, n=NCH):
    return np.ascontiguousarray(np.asarray(vec).reshape(n, 128).T)


def host_prep(inputs):
    g = {}
    T_ = lambda a: np.ascontiguousarray(np.asarray(a).T)
    A_ = lambda a: np.ascontiguousarray(np.asarray(a))

    g["l0a_wqt"] = _chunkmajor(T_(inputs["L0_attn_Wq"]))
    g["l0a_wkt"] = _chunkmajor(T_(inputs["L0_attn_Wk"]))
    g["l0a_wvt"] = _chunkmajor(T_(inputs["L0_attn_Wv"]))
    g["l0a_wot"] = _chunkmajor(T_(inputs["L0_attn_Wout"]))
    g["l0a_bout"] = _pp(inputs["L0_attn_bout"])

    qkvT = T_(inputs["L0_ffn_qkv_w"])
    g["l0f_wqt"] = _chunkmajor(qkvT[:, 0:1024])
    g["l0f_wkt"] = _chunkmajor(qkvT[:, 1024:2048])
    g["l0f_wvt"] = _chunkmajor(qkvT[:, 2048:3072])
    qb = A_(inputs["L0_ffn_qkv_b"])
    g["l0f_qb"] = _pp(qb[0:1024])
    g["l0f_kb"] = _pp(qb[1024:2048])
    g["l0f_vb"] = qb[2048:3072].reshape(1, 1024).copy()
    g["l0f_wot"] = _chunkmajor(T_(inputs["L0_ffn_out_w"]))
    g["l0f_bout"] = _pp(inputs["L0_ffn_out_b"])
    g["l0f_n1w"] = _pp(inputs["L0_ffn_norm1_w"])
    g["l0f_n2w"] = _pp(inputs["L0_ffn_norm2_w"])

    w1 = np.asarray(inputs["L0_ffn_w1"]); w2 = np.asarray(inputs["L0_ffn_w2"])
    w3 = np.asarray(inputs["L0_ffn_w3"])
    g["l0f_w1b"] = np.stack([_blocks(T_(w1[e])) for e in range(8)])
    g["l0f_w2b"] = np.stack([_blocks(T_(w2[e])) for e in range(8)])
    g["l0f_w3b"] = np.stack([_blocks(T_(w3[e])) for e in range(8)])
    g["l0f_b1"] = np.stack([_pp(np.asarray(inputs["L0_ffn_b1"])[e], 16) for e in range(8)])
    g["l0f_b2"] = np.stack([_pp(np.asarray(inputs["L0_ffn_b2"])[e], 16) for e in range(8)])
    g["l0f_b3"] = A_(inputs["L0_ffn_b3"]).reshape(8, 1, 1024).copy()
    g["l0f_gatet"] = _chunkmajor(T_(inputs["L0_ffn_gate_w"]))

    g["l1a_wq"] = _chunkmajor(A_(inputs["L1_attn_Wq"]))
    g["l1a_wk"] = _chunkmajor(A_(inputs["L1_attn_Wk"]))
    g["l1a_wv"] = _chunkmajor(A_(inputs["L1_attn_Wv"]))
    g["l1a_wbm"] = _chunkmajor(A_(inputs["L1_attn_Wb"]))
    g["l1a_wo"] = _chunkmajor(A_(inputs["L1_attn_Wo"]))
    g["l1a_bo"] = _pp(inputs["L1_attn_bo"])

    g["l1f_wg"] = _chunkmajor(A_(inputs["L1_ffn_Wg"]))
    g["l1f_alpha"] = A_(inputs["L1_ffn_alpha"]).reshape(1, 8).copy()
    g["l1f_lnw"] = np.stack([_pp(np.asarray(inputs["L1_ffn_ln_w"])[e]) for e in range(8)])
    g["l1f_lnb"] = np.stack([_pp(np.asarray(inputs["L1_ffn_ln_b"])[e]) for e in range(8)])
    fc1 = np.asarray(inputs["L1_ffn_fc1_w"]); fc2 = np.asarray(inputs["L1_ffn_fc2_w"])
    g["l1f_fc1b"] = np.stack([_blocks(A_(fc1[e])) for e in range(8)])
    g["l1f_fc2b"] = np.stack([_blocks(A_(fc2[e])) for e in range(8)])
    g["l1f_b1"] = np.stack([_pp(np.asarray(inputs["L1_ffn_fc1_b"])[e], 32) for e in range(8)])
    g["l1f_b2"] = A_(inputs["L1_ffn_fc2_b"]).reshape(8, 1, 1024).copy()

    g["rotperm"] = _rotperm()
    g["rollperm"] = _rollperm()
    g["iota8"] = np.tile(np.arange(8, dtype=np.float32), (128, 1)).copy()

    x = np.asarray(inputs["L0_attn_x"])
    percore = []
    for c in range(NCORES):
        d = {"x_in": np.ascontiguousarray(x[:, c * NL:(c + 1) * NL, :])}
        cs, sn = _rope_tables(c)
        d["ropecs"] = cs
        d["ropesn"] = sn
        percore.append(d)
    return g, percore



class Ph:
    ENGS = ("t", "v", "s", "g", "y")

    def __init__(self, P):
        self.P = P
        self.nc = P.nc
        self.ops = {e: [] for e in self.ENGS}

    def wait(self, eng, sem, val):
        if val <= 0:
            return
        h = self.P.sems[sem]
        self.ops[eng].append(lambda e, h=h, v=val: e.wait_ge(h, v))

    def raw(self, eng, fn):
        self.ops[eng].append(fn)

    def do(self, eng, fn, inc="auto", n=1):
        if inc == "auto":
            inc = ESEM.get(eng)
        if inc is None:
            self.ops[eng].append(fn)
            return None
        h = self.P.sems[inc]
        self.ops[eng].append(lambda e, fn=fn, h=h, n=n: fn(e).then_inc(h, n))
        self.P.cnt[inc] += n
        return self.P.cnt[inc]

    def flush(self):
        with self.nc.Block() as block:
            for eng, sect in (("y", block.sync), ("g", block.gpsimd),
                              ("t", block.tensor), ("s", block.scalar),
                              ("v", block.vector)):
                ops = self.ops[eng]
                if ops:
                    sect(lambda e, ops=ops: [f(e) for f in ops])
        self.ops = {e: [] for e in self.ENGS}


class Seq:
    def __init__(self, ph, deps=()):
        self.ph = ph
        self.deps = list(deps)

    def do(self, eng, fn, inc="auto", n=1):
        for sem, cntv in self.deps:
            self.ph.wait(eng, sem, cntv)
        c = self.ph.do(eng, fn, inc=inc, n=n)
        if inc == "auto":
            inc = ESEM.get(eng)
        if inc is not None:
            self.deps = [(inc, c)]
        return c

    def also(self, eng, fn, inc="auto", n=1):
        c = self.ph.do(eng, fn, inc=inc, n=n)
        if inc == "auto":
            inc = ESEM.get(eng)
        if inc is not None:
            self.deps.append((inc, c))
        return c


class Prog:
    def __init__(self, nc):
        self.nc = nc
        self.cnt = {}
        self.sems = {}
        for name in ("wl", "al", "st", "pe", "ve", "se", "ge", "cc"):
            self.sems[name] = nc.alloc_semaphore(name)
            self.cnt[name] = 0
        self.dram = {}
        self.sb = {}
        self.innames = []

    def phase(self):
        return Ph(self)

    def din(self, name, shape, dtype=F32):
        if name not in self.dram:
            self.dram[name] = self.nc.dram_tensor(name, list(shape), dtype, kind="ExternalInput")
            self.innames.append(name)
        return self.dram[name]

    def dout(self, name, shape, dtype=F32):
        self.dram[name] = self.nc.dram_tensor(name, list(shape), dtype, kind="ExternalOutput")
        return self.dram[name]

    def dint(self, name, shape, dtype=F32):
        if name not in self.dram:
            self.dram[name] = self.nc.dram_tensor(name, list(shape), dtype, kind="Internal")
        return self.dram[name]

    def sbuf(self, name, shape, dtype=F32):
        if name not in self.sb:
            self.sb[name] = self.nc.alloc_sbuf_tensor(name, list(shape), dtype)
        return self.sb[name]



def emit_consts(P, ph):
    idn = P.sbuf("idn", [128, 128], F32)
    ones_col = P.sbuf("ones_col", [128, 1], F32)
    ones_row = P.sbuf("ones_row", [1, 128], F32)
    ones_col_bf = P.sbuf("ones_col_bf", [128, 1], BF16)
    ones_row_bf = P.sbuf("ones_row_bf", [1, 128], BF16)
    rotp = P.sbuf("rotp", [128, 128], BF16)
    rollp = P.sbuf("rollp", [128, 128], BF16)
    iota8v = P.sbuf("iota8v", [128, 8], F32)

    ph.do("g", lambda e: e.memset(ones_col[:, :], 1.0))
    ph.do("g", lambda e: e.memset(ones_row[:, :], 1.0))
    ph.do("g", lambda e: e.memset(ones_col_bf[:, :], 1.0))
    ph.do("g", lambda e: e.memset(ones_row_bf[:, :], 1.0))
    ph.do("g", lambda e: e.memset(idn[:, :], 0.0))
    ph.do("g", lambda e: e.affine_select(
        out=idn[:, :], in_=idn[:, :], compare_op=ALU.not_equal, fill=1.0,
        base=0, pattern=[[-1, 128]], channel_multiplier=1))
    for nm, dst in (("rotperm", rotp), ("rollperm", rollp), ("iota8", iota8v)):
        dt = BF16 if nm in ("rotperm", "rollperm") else F32
        d = P.din(nm, list(dst.shape), dt)
        ph.do("y", lambda e, d=d, dst=dst: e.dma_start(dst[:, :], d[:, :]), inc="al", n=16)


def emit_input_convert(P, ph):
    nc = P.nc
    x_fm = P.sbuf("x_fm", [128, NCH * T], F32)
    x_in = P.din("x_in", [NB, NL, D], F32)
    idn = P.sb["idn"]
    with nc.sbuf_tensor("xtok_i", [128, 16 * D], F32) as xtok, \
         nc.psum_tensor("ptr_i", [128, 4 * 512], F32) as ptr:
        ap_in = x_in.ap().rearrange("b (j p) d -> p (b j) d", p=128)
        ap_out = xtok.ap().rearrange("p (j d) -> p j d", j=16)
        c_al = ph.do("y", lambda e: e.dma_start(ap_out, ap_in), inc="al", n=16)
        base_pe = P.cnt["pe"]; base_ve = P.cnt["ve"]
        i = 0
        for j in range(16):
            for dc in range(NCH):
                bank = i % 4
                ph.wait("t", "al", c_al)
                if i >= 4:
                    ph.wait("t", "ve", base_ve + i - 3)
                ph.do("t", lambda e, j=j, dc=dc, bank=bank: e.transpose(
                    ptr[:, bank * 512:bank * 512 + 128],
                    xtok[:, j * D + dc * 128: j * D + (dc + 1) * 128],
                    idn[:, :]))
                ph.wait("v", "pe", base_pe + i + 1)
                ph.do("v", lambda e, j=j, dc=dc, bank=bank: e.tensor_copy(
                    x_fm[:, dc * T + j * 128: dc * T + (j + 1) * 128],
                    ptr[:, bank * 512:bank * 512 + 128]))
                i += 1
        ph.flush()


def emit_output_convert(P, ph):
    nc = P.nc
    x_fm = P.sb["x_fm"]
    x_out = P.dout("x_out", [NB, NL, D], F32)
    idn = P.sb["idn"]
    with nc.sbuf_tensor("xtok_o", [128, 16 * D], F32) as xtok, \
         nc.psum_tensor("ptr_o", [128, 4 * 512], F32) as ptr:
        base_pe = P.cnt["pe"]; base_ve = P.cnt["ve"]
        i = 0
        for j in range(16):
            for dc in range(NCH):
                bank = i % 4
                if i >= 4:
                    ph.wait("t", "ve", base_ve + i - 3)
                ph.do("t", lambda e, j=j, dc=dc, bank=bank: e.transpose(
                    ptr[:, bank * 512:bank * 512 + 128],
                    x_fm[:, dc * T + j * 128: dc * T + (j + 1) * 128],
                    idn[:, :]))
                ph.wait("v", "pe", base_pe + i + 1)
                ph.do("v", lambda e, j=j, dc=dc, bank=bank: e.tensor_copy(
                    xtok[:, j * D + dc * 128: j * D + (dc + 1) * 128],
                    ptr[:, bank * 512:bank * 512 + 128]))
                i += 1
        c_ve = P.cnt["ve"]
        ph.wait("y", "ve", c_ve)
        ap_out = x_out.ap().rearrange("b (j p) d -> p (b j) d", p=128)
        ap_in = xtok.ap().rearrange("p (j d) -> p j d", j=16)
        c_st = ph.do("y", lambda e: e.dma_start(ap_out, ap_in), inc="st", n=16)
        ph.wait("y", "st", c_st)
        ph.flush()


def emit_fm_load(P, ph):
    x_fm = P.sbuf("x_fm", [128, NCH * T], F32)
    d = P.din("xfm_in", [128, NCH * T], F32)
    c = ph.do("y", lambda e: e.dma_start(x_fm[:, :], d[:, :]), inc="al", n=16)
    ph.wait("y", "al", c)
    ph.flush()


def emit_fm_store(P, ph):
    x_fm = P.sb["x_fm"]
    d = P.dout("xfm_out", [128, NCH * T], F32)
    c = ph.do("y", lambda e: e.dma_start(d[:, :], x_fm[:, :]), inc="st", n=16)
    ph.wait("y", "st", c)
    ph.flush()



def emit_norm_fm(P, ph, srct, dstt, kind, eps, w_sb=None, uid="", tts=None,
                 dst_stride=T, dst_base=0, src_stride=T):
    nc = P.nc
    assert not (kind == "ln" and w_sb is not None)
    if tts is None:
        tts = list(range(NTT))
    ones_src = P.sb["ones_col"] if srct.dtype == F32 else P.sb["ones_col_bf"]
    ones_col = P.sb["ones_col"]
    ones_row = P.sb["ones_row"]
    src = lambda cc, t0, tw: srct[:, cc * src_stride + t0: cc * src_stride + t0 + tw]
    dst = lambda cc, t0, tw: dstt[:, dst_base + cc * dst_stride + t0 - tts[0] * TTW:
                                  dst_base + cc * dst_stride + t0 - tts[0] * TTW + tw]
    with nc.sbuf_tensor(f"nsq{uid}", [128, NCH * TTW // 4], F32) as sq, \
         nc.sbuf_tensor(f"nrw{uid}", [1, 3 * TTW], F32) as rows, \
         nc.psum_tensor(f"nmps{uid}", [1, 512], F32) as ps_sum, \
         nc.psum_tensor(f"nmpq{uid}", [1, 512], F32) as ps_sq, \
         nc.psum_tensor(f"nmpa{uid}", [128, 512], F32) as rep_a, \
         nc.psum_tensor(f"nmpb{uid}", [128, 512], F32) as rep_b:
        sq3 = sq.ap().rearrange("p (c w) -> p c w", c=NCH // 4)
        prev_ve = P.cnt["ve"]
        for tt in tts:
            t0 = tt * TTW
            src3 = srct.ap().rearrange("p (c t) -> p c t", c=NCH)[:, :, t0:t0 + TTW]
            ph.wait("s", "ve", prev_ve)
            if kind == "ln":
                ph.wait("t", "ve", prev_ve)
                for cc in range(NCH):
                    ph.do("t", lambda e, cc=cc, t0=t0: e.matmul(
                        ps_sum[:, :], ones_src[:, :], src(cc, t0, TTW),
                        start=(cc == 0), stop=(cc == NCH - 1)),
                        inc=(None if cc < NCH - 1 else "pe"))
            for half in range(4):
                if half >= 1:
                    ph.wait("s", "pe", P.cnt["pe"])
                c_sq = ph.do("s", lambda e, src3=src3, half=half: e.activation(
                    sq3, src3[:, half * (NCH // 4):(half + 1) * (NCH // 4), :],
                    AF.Square))
                ph.wait("t", "se", c_sq)
                for cc4 in range(NCH // 4):
                    cc = half * (NCH // 4) + cc4
                    ph.do("t", lambda e, cc4=cc4, cc=cc: e.matmul(
                        ps_sq[:, :], ones_col[:, :], sq[:, cc4 * TTW:(cc4 + 1) * TTW],
                        start=(cc == 0), stop=(cc == NCH - 1)),
                        inc=(None if cc < NCH - 1 else "pe"))
            c_pe = P.cnt["pe"]
            mu = rows[:, 0:TTW]; var = rows[:, TTW:2 * TTW]
            a = rows[:, 2 * TTW:3 * TTW]; b = rows[:, 0:TTW]
            s_ = Seq(ph, deps=[("pe", c_pe)])
            if kind == "ln":
                s_.do("v", lambda e: e.tensor_scalar(mu, ps_sum[0:1, :], 1.0 / D, None, ALU.mult))
                s_.do("v", lambda e: e.tensor_scalar(var, ps_sq[0:1, :], 1.0 / D, None, ALU.mult))
                s_.do("s", lambda e: e.activation(a, mu, AF.Square))
                s_.do("v", lambda e: e.tensor_sub(var, var, a))
            else:
                s_.do("v", lambda e: e.tensor_scalar(var, ps_sq[0:1, :], 1.0 / D, None, ALU.mult))
            s_.do("v", lambda e: e.tensor_scalar(var, var, float(eps), None, ALU.add))
            s_.do("s", lambda e: e.activation(var, var, AF.Sqrt))
            s_.do("v", lambda e: e.reciprocal(a, var))
            if kind == "ln":
                s_.do("v", lambda e: e.scalar_tensor_tensor(b, mu, -1.0, a, ALU.mult, ALU.mult))
            s_.do("t", lambda e: e.matmul(rep_a[:, :], ones_row[:, :], a, start=True, stop=True))
            if kind == "ln":
                s_.do("t", lambda e: e.matmul(rep_b[:, :], ones_row[:, :], b, start=True, stop=True))
            c_rep = s_.deps[0][1]
            ph.wait("v", "pe", c_rep)
            for cc in range(NCH):
                d_ap = dst(cc, t0, TTW)
                if kind == "ln":
                    ph.do("v", lambda e, cc=cc, t0=t0: e.tensor_mul(
                        sq[:, (cc % 2) * TTW:(cc % 2 + 1) * TTW], src(cc, t0, TTW),
                        rep_a[:, :]))
                    ph.do("v", lambda e, cc=cc, d_ap=d_ap: e.tensor_add(
                        d_ap, sq[:, (cc % 2) * TTW:(cc % 2 + 1) * TTW], rep_b[:, :]))
                else:
                    if w_sb is None:
                        ph.do("v", lambda e, cc=cc, t0=t0, d_ap=d_ap: e.tensor_mul(
                            d_ap, src(cc, t0, TTW), rep_a[:, :]))
                    else:
                        ph.do("v", lambda e, cc=cc, t0=t0, d_ap=d_ap: e.scalar_tensor_tensor(
                            d_ap, src(cc, t0, TTW), w_sb[:, cc:cc + 1], rep_a[:, :],
                            ALU.mult, ALU.mult))
            prev_ve = P.cnt["ve"]
    ph.flush()



def emit_proj_fm(P, ph, wsb, c_w, zt, outt, prj_ps, scale=1.0, bias_sb=None,
                 deps=(), hcs=None, outofs=None):
    if hcs is None:
        hcs = list(range(NCH))
    if outofs is None:
        outofs = lambda hc: hc * T
    base_se = P.cnt["se"]
    i = 0
    for hc in hcs:
        oofs = outofs(hc)
        for tt in range(NTT):
            t0 = tt * TTW
            bank = i % 2
            if i == 0:
                ph.wait("t", "wl", c_w)
                for sem, v in deps:
                    ph.wait("t", sem, v)
            if i >= 2:
                ph.wait("t", "se", base_se + i - 1)
            for cc in range(NCH):
                ph.do("t", lambda e, cc=cc, hc=hc, t0=t0, bank=bank: e.matmul(
                    prj_ps[:, bank * 512:(bank + 1) * 512],
                    wsb[:, cc * D + hc * 128: cc * D + (hc + 1) * 128],
                    zt[:, cc * T + t0: cc * T + t0 + TTW],
                    start=(cc == 0), stop=(cc == NCH - 1)),
                    inc=(None if cc < NCH - 1 else "pe"))
            c_pe = P.cnt["pe"]
            ph.wait("s", "pe", c_pe)
            bb = None if bias_sb is None else bias_sb[:, hc:hc + 1]
            def ev(e, oofs=oofs, t0=t0, bank=bank, bb=bb):
                kw = dict(scale=float(scale))
                if bb is not None:
                    kw["bias"] = bb
                return e.activation(outt[:, oofs + t0: oofs + t0 + TTW],
                                    prj_ps[:, bank * 512:(bank + 1) * 512],
                                    AF.Identity, **kw)
            ph.do("s", ev)
            i += 1
    return P.cnt["se"]



def emit_rope(P, ph, buf, rp_ps, tmp, c_ready_se, cs0, sn0):
    c = P.cnt["ve"]
    for hc in range(NCH):
        c = _emit_rope_tiles(
            P, ph, lambda tt, hc=hc: buf[:, hc * T + tt * TTW: hc * T + (tt + 1) * TTW],
            rp_ps, tmp, c_ready_se, cs0, sn0)
    return c



def _emit_rope_tiles(P, ph, get_sl, rp_ps, rtmp, c_ready_se, cs0, sn0):
    rotp = P.sb["rotp"]
    base_ve = P.cnt["ve"]
    for tt in range(NTT):
        t0 = tt * TTW
        bank = tt % 2
        sl = get_sl(tt)
        if tt == 0:
            ph.wait("t", "se", c_ready_se)
            ph.wait("t", "ve", base_ve)
        if tt >= 2:
            ph.wait("t", "ve", base_ve + (tt - 2) * 3 + 3)
        ph.do("t", lambda e, sl=sl, bank=bank: e.matmul(
            rp_ps[:, bank * 512:(bank + 1) * 512], rotp[:, :], sl,
            start=True, stop=True))
        c_pe = P.cnt["pe"]
        ph.wait("v", "pe", c_pe)
        ph.do("v", lambda e, bank=bank, t0=t0: e.tensor_mul(
            rtmp[:, 0:TTW], rp_ps[:, bank * 512:(bank + 1) * 512], sn0[:, t0:t0 + TTW]))
        ph.do("v", lambda e, sl=sl, t0=t0: e.tensor_mul(
            rtmp[:, TTW:2 * TTW], sl, cs0[:, t0:t0 + TTW]))
        ph.do("v", lambda e, sl=sl: e.tensor_add(sl, rtmp[:, TTW:2 * TTW], rtmp[:, 0:TTW]))
    return P.cnt["ve"]


def _emit_m2(P, ph, get_chunk, hcs, row_out, nps, sqbuf, dep_of=None):
    ones_col = P.sb["ones_col"]
    red = {}
    k = 0
    for hc in hcs:
        ch = get_chunk(hc)
        deps = [("pe", P.cnt["pe"])]
        if dep_of is not None:
            deps.append(dep_of(hc))
        for sem, v in deps:
            ph.wait("v", sem, v)
        c_sq = ph.do("v", lambda e, ch=ch: e.tensor_mul(sqbuf[:, :], ch, ch))
        ph.wait("t", "ve", c_sq)
        for j in range(2):
            hb = j * 64
            for b in range(NB):
                idx = (hc * 2 + j) * 4 + b
                bank = k % 2
                if k >= 2:
                    ph.wait("t", "ve", red[k - 2])
                c_mm = ph.do("t", lambda e, hb=hb, b=b, bank=bank: e.matmul(
                    nps[0:1, bank * 512:(bank + 1) * 512],
                    ones_col[hb:hb + 64, :], sqbuf[hb:hb + 64, b * 512:(b + 1) * 512],
                    start=True, stop=True))
                ph.wait("v", "pe", c_mm)
                red[k] = ph.do("v", lambda e, idx=idx, bank=bank: e.reduce_max(
                    row_out[0:1, idx:idx + 1], nps[0:1, bank * 512:(bank + 1) * 512],
                    axis=AXX))
                k += 1
    return P.cnt["ve"]



def emit_attn(P, ph, cfg):
    from contextlib import ExitStack
    nc = P.nc
    uid = cfg["uid"]
    x_fm = P.sb["x_fm"]
    ones_col = P.sb["ones_col"]; ones_row = P.sb["ones_row"]
    ones_col_bf = P.sb["ones_col_bf"]; ones_row_bf = P.sb["ones_row_bf"]; ones_row_bf = P.sb["ones_row_bf"]
    ginK = P.dint("ginK", [D, T], BF16)
    goutK = P.dint("goutK", [NCORES * D, T], BF16)
    ginV = P.dint("ginV", [T, D], BF16)
    goutV = P.dint("goutV", [NCORES * T, D], BF16)
    gin_s = P.dint("gin_s", [1, 64], F32)
    gout_s = P.dint("gout_s", [1, 64], F32)

    dwq = P.din(cfg["wq"], [128, NCH * D], F32)
    dwk = P.din(cfg["wk"], [128, NCH * D], F32)
    dwv = P.din(cfg["wv"], [128, NCH * D], F32)
    dwo = P.din(cfg["wo"], [128, NCH * D], F32)
    dbout = P.din(cfg["bout"], [128, NCH], F32)

    with ExitStack() as outer:
        qbuf = outer.enter_context(nc.sbuf_tensor(f"q{uid}", [128, NCH * T], BF16))
        rowbuf = outer.enter_context(nc.sbuf_tensor(f"rw{uid}", [1, 256], F32))
        negc_rep = outer.enter_context(nc.sbuf_tensor(f"ncr{uid}", [128, 64], F32))
        bout_sb = outer.enter_context(nc.sbuf_tensor(f"bo{uid}", [128, NCH], F32))
        qm2 = rowbuf[0:1, 0:64]
        km2 = rowbuf[0:1, 64:128]
        km2g = rowbuf[0:1, 128:192]
        t1 = rowbuf[0:1, 192:256]
        ph.do("y", lambda e: e.dma_start(bout_sb[:, :], dbout[:, :]), inc="al", n=16)

        with ExitStack() as stA:
            zbuf = stA.enter_context(nc.sbuf_tensor(f"z{uid}", [128, NCH * T], BF16))
            wsb0 = stA.enter_context(nc.sbuf_tensor(f"w0{uid}", [128, NCH * D], BF16))
            wsb1 = stA.enter_context(nc.sbuf_tensor(f"w1{uid}", [128, NCH * D], BF16))
            gain_sb = bias_q = bias_k = vb_sb = None
            if cfg.get("gain"):
                gain_d = P.din(cfg["gain"], [128, NCH], F32)
                gain_sb = stA.enter_context(nc.sbuf_tensor(f"gn{uid}", [128, NCH], F32))
                ph.do("y", lambda e: e.dma_start(gain_sb[:, :], gain_d[:, :]), inc="al", n=16)
            if cfg.get("qb"):
                dqb = P.din(cfg["qb"], [128, NCH], F32)
                dkb = P.din(cfg["kb"], [128, NCH], F32)
                bias_q = stA.enter_context(nc.sbuf_tensor(f"qb{uid}", [128, NCH], F32))
                bias_k = stA.enter_context(nc.sbuf_tensor(f"kb{uid}", [128, NCH], F32))
                ph.do("y", lambda e: e.dma_start(bias_q[:, :], dqb[:, :]), inc="al", n=16)
                ph.do("y", lambda e: e.dma_start(bias_k[:, :], dkb[:, :]), inc="al", n=16)
            if cfg.get("vb"):
                dvb = P.din(cfg["vb"], [1, D], F32)
                vb_sb = stA.enter_context(nc.sbuf_tensor(f"vb{uid}", [1, D], BF16))
                ph.do("g", lambda e: e.dma_start(vb_sb[:, :], dvb[:, :]), inc="al", n=16)
            cs0 = sn0 = None
            if cfg["rope"]:
                dcs = P.din("ropecs", [128, T], F32)
                dsn = P.din("ropesn", [128, T], F32)
                cs0 = stA.enter_context(nc.sbuf_tensor(f"cs{uid}", [128, T], F32))
                sn0 = stA.enter_context(nc.sbuf_tensor(f"sn{uid}", [128, T], F32))
                ph.do("y", lambda e: e.dma_start(cs0[:, :], dcs[:, :]), inc="al", n=16)
                ph.do("y", lambda e: e.dma_start(sn0[:, :], dsn[:, :]), inc="al", n=16)
            c_misc = P.cnt["al"]
            scale = 1.0 if cfg["rope"] else 64.0 ** -0.25

            ph.wait("v", "al", c_misc)
            ph.wait("s", "al", c_misc)
            emit_norm_fm(P, ph, x_fm, zbuf, cfg["kind"], cfg["eps"], w_sb=gain_sb, uid=uid)
            c_z = P.cnt["ve"]

            with nc.psum_tensor(f"prj{uid}", [128, 1024], F32) as prj_ps, \
                 nc.psum_tensor(f"rp{uid}", [128, 1024], F32) as rp_ps, \
                 nc.psum_tensor(f"nps{uid}", [1, 1024], F32) as nps, \
                 nc.psum_tensor(f"sps{uid}", [128, 64], F32) as sps, \
                 nc.sbuf_tensor(f"kch{uid}", [128, 2 * T], BF16) as kch, \
                 nc.sbuf_tensor(f"ksq{uid}", [128, T], F32) as ksq, \
                 nc.sbuf_tensor(f"rt{uid}", [128, 2 * TTW], F32) as rtmp, \
                 nc.sbuf_tensor(f"vst{uid}", [128, 2 * D], BF16) as vstage:

                c_wk = ph.do("g", lambda e: e.dma_start(
                    wsb1.ap().rearrange("p (c m) -> p c m", c=NCH),
                    dwk.ap().rearrange("p (c m) -> p c m", c=NCH)), inc="wl", n=16)
                c_wv = ph.do("g", lambda e: e.dma_start(
                    wsb0.ap().rearrange("p (c m) -> p c m", c=NCH),
                    dwv.ap().rearrange("p (c m) -> p c m", c=NCH)), inc="wl", n=16)

                kdma = {}
                kv_done = {}
                for hc in range(NCH):
                    half = hc % 2
                    buf = kch[:, half * T:(half + 1) * T]
                    deps = [("ve", c_z)]
                    if hc >= 2:
                        deps = [("st", kdma[hc - 2]), ("ve", kv_done[hc - 2])]
                    c_se = emit_proj_fm(P, ph, wsb1, c_wk, zbuf, buf, prj_ps,
                                        scale=scale, bias_sb=bias_k, deps=deps,
                                        hcs=[hc], outofs=lambda _hc: 0)
                    if cfg["rope"]:
                        c_rdy = _emit_rope_tiles(
                            P, ph, lambda tt, buf=buf: buf[:, tt * TTW:(tt + 1) * TTW],
                            rp_ps, rtmp, c_se, cs0, sn0)
                        kdep = ("ve", c_rdy)
                    else:
                        kdep = ("se", c_se)
                    c_m2 = _emit_m2(P, ph, lambda _hc, buf=buf: buf, [hc], km2,
                                    nps, ksq, dep_of=lambda _hc, kdep=kdep: kdep)
                    kv_done[hc] = c_m2
                    ph.wait("y", kdep[0], kdep[1])
                    kdma[hc] = ph.do("y", lambda e, hc=hc, buf=buf: e.dma_start(
                        ginK.ap()[hc * 128:(hc + 1) * 128, :], buf), inc="st", n=16)
                c_kcontrib = P.cnt["st"]
                c_kpe = P.cnt["pe"]

                c_km = P.cnt["ve"]
                ph.wait("y", "ve", c_km)
                c_kms = ph.do("y", lambda e: e.dma_start(gin_s[:, :], km2), inc="st", n=16)
                ph.wait("g", "st", c_kms)
                c_cc1 = ph.do("g", lambda e: e.collective_compute(
                    "AllReduce", ALU.max, replica_groups=[list(range(NCORES))],
                    ins=[gin_s.ap().opt()], outs=[gout_s.ap().opt()]), inc="cc")
                ph.wait("g", "st", c_kcontrib)
                c_cc2 = ph.do("g", lambda e: e.collective_compute(
                    "AllGather", ALU.bypass, replica_groups=[list(range(NCORES))],
                    ins=[ginK.ap().opt()], outs=[goutK.ap().opt()]), inc="cc")

                vdma = {}
                base_vse = P.cnt["se"]
                for tk in range(16):
                    for hh in range(2):
                        i = tk * 2 + hh
                        bank = i % 2
                        if i == 0:
                            ph.wait("t", "wl", c_wv)
                        if i >= 2:
                            ph.wait("t", "se", base_vse + i - 1)
                        if tk >= 2 and hh == 0:
                            ph.wait("t", "st", vdma[tk - 2])
                        for cc in range(NCH):
                            last = (cc == NCH - 1 and vb_sb is None)
                            ph.do("t", lambda e, cc=cc, tk=tk, hh=hh, bank=bank, last=last:
                                  e.matmul(prj_ps[:, bank * 512:(bank + 1) * 512],
                                           zbuf[:, cc * T + tk * 128: cc * T + (tk + 1) * 128],
                                           wsb0[:, cc * D + hh * 512: cc * D + hh * 512 + 512],
                                           start=(cc == 0), stop=last),
                                  inc=(None if not last else "pe"))
                        if vb_sb is not None:
                            ph.do("t", lambda e, hh=hh, bank=bank: e.matmul(
                                prj_ps[:, bank * 512:(bank + 1) * 512],
                                ones_row_bf[:, :],
                                vb_sb[0:1, hh * 512:hh * 512 + 512],
                                start=False, stop=True, skip_group_check=True), inc="pe")
                        c_pe = P.cnt["pe"]
                        ph.wait("s", "pe", c_pe)
                        ph.do("s", lambda e, hh=hh, bank=bank, tk=tk: e.activation(
                            vstage[:, (tk % 2) * D + hh * 512:(tk % 2) * D + hh * 512 + 512],
                            prj_ps[:, bank * 512:(bank + 1) * 512], AF.Identity))
                        if hh == 1:
                            c_se = P.cnt["se"]
                            ph.wait("y", "se", c_se)
                            vdma[tk] = ph.do("y", lambda e, tk=tk: e.dma_start(
                                ginV.ap()[tk * 128:(tk + 1) * 128, :],
                                vstage[:, (tk % 2) * D:(tk % 2) * D + D]), inc="st", n=16)
                c_vcontrib = P.cnt["st"]
                ph.wait("g", "st", c_vcontrib)
                c_cc3 = ph.do("g", lambda e: e.collective_compute(
                    "AllGather", ALU.bypass, replica_groups=[list(range(NCORES))],
                    ins=[ginV.ap().opt()], outs=[goutV.ap().opt()]), inc="cc")

                ph.wait("g", "pe", c_kpe)
                c_wq = ph.do("g", lambda e: e.dma_start(
                    wsb1.ap().rearrange("p (c m) -> p c m", c=NCH),
                    dwq.ap().rearrange("p (c m) -> p c m", c=NCH)), inc="wl", n=16)
                c_qse = emit_proj_fm(P, ph, wsb1, c_wq, zbuf, qbuf, prj_ps,
                                     scale=scale, bias_sb=bias_q,
                                     deps=[("ve", c_z), ("se", P.cnt["se"])])
                if cfg["rope"]:
                    emit_rope(P, ph, qbuf, rp_ps, rtmp, c_qse, cs0, sn0)
                    qdep = None
                else:
                    qdep = lambda _hc: ("se", c_qse)
                _emit_m2(P, ph, lambda hc: qbuf[:, hc * T:(hc + 1) * T],
                         range(NCH), qm2, nps, ksq, dep_of=qdep)

                ph.wait("y", "cc", c_cc1)
                c_al = ph.do("y", lambda e: e.dma_start(km2g, gout_s[:, :]), inc="al", n=16)
                sh = Seq(ph, deps=[("al", c_al)])
                sh.do("v", lambda e: e.tensor_mul(t1, qm2, km2g))
                sh.do("s", lambda e: e.activation(t1, t1, AF.Sqrt))
                sh.do("v", lambda e: e.tensor_scalar(t1, t1, -1.0, None, ALU.mult))
                sh.do("t", lambda e: e.matmul(sps[:, :], ones_row[:, :], t1,
                                              start=True, stop=True))
                sh.do("v", lambda e: e.tensor_copy(negc_rep[:, :], sps[:, :]))
                ph.flush()
                if cfg.get("dump_a"):
                    dz = P.dout("dump_z", [128, NCH * T], BF16)
                    dq = P.dout("dump_q", [128, NCH * T], BF16)
                    dnc = P.dout("dump_negc", [128, 64], F32)
                    dgk = P.dout("dump_ginK", [D, T], BF16)
                    dgv = P.dout("dump_ginV", [T, D], BF16)
                    c = ph.do("y", lambda e: e.dma_start(dz[:, :], zbuf[:, :]), inc="st", n=16)
                    c = ph.do("y", lambda e: e.dma_start(dq[:, :], qbuf[:, :]), inc="st", n=16)
                    c = ph.do("y", lambda e: e.dma_start(dnc[:, :], negc_rep[:, :]), inc="st", n=16)
                    c = ph.do("y", lambda e: e.dma_start(dgk[:, :], ginK[:, :]), inc="st", n=16)
                    c = ph.do("y", lambda e: e.dma_start(dgv[:, :], ginV[:, :]), inc="st", n=16)
                    ph.wait("y", "st", c)
                    ph.flush()
                    return

        with nc.sbuf_tensor(f"ks0{uid}", [128, 2 * 4096], BF16) as ksb0, \
             nc.sbuf_tensor(f"ks1{uid}", [128, 2 * 4096], BF16) as ksb1, \
             nc.sbuf_tensor(f"vs0{uid}", [128, 32 * 256], BF16) as vsb0, \
             nc.sbuf_tensor(f"vs1{uid}", [128, 32 * 256], BF16) as vsb1, \
             nc.sbuf_tensor(f"pb{uid}", [128, 2 * 1024], BF16) as psb, \
             nc.sbuf_tensor(f"abf{uid}", [128, NCH * 512], BF16) as abuf, \
             nc.sbuf_tensor(f"rr{uid}", [1, 512], F32) as rrow, \
             nc.sbuf_tensor(f"rs{uid}", [64, 512], F32) as rep_sb, \
             nc.sbuf_tensor(f"wo{uid}", [128, NCH * D], BF16) as wo_sb, \
             nc.psum_tensor(f"scp{uid}", [128, 2048], F32) as sc_ps, \
             nc.psum_tensor(f"avp{uid}", [128, 2048], F32) as av_ps:
            c_wo = ph.do("g", lambda e: e.dma_start(
                wo_sb.ap().rearrange("p (c m) -> p c m", c=NCH),
                dwo.ap().rearrange("p (c m) -> p c m", c=NCH)), inc="wl", n=16)
            ksbs = (ksb0, ksb1)
            vsbs = (vsb0, vsb1)
            c_al_iter = [0] * 16
            c_iter_done = [0] * 16

            def issue_loads(i):
                b, hg = divmod(i, 4)
                if i >= 2:
                    ph.wait("y", "pe", c_iter_done[i - 2])
                if i == 0:
                    ph.wait("y", "cc", c_cc3)
                kap = goutK.ap().rearrange("(c d) t -> d c t", c=NCORES)
                for pr in range(2):
                    h0 = hg * 4 + 2 * pr
                    src = kap[h0 * 64: h0 * 64 + 128, :, b * 512:(b + 1) * 512]
                    dst = ksbs[i % 2].ap().rearrange("p (r t) -> p r t", r=2)[
                        :, pr:pr + 1, :].rearrange(
                        "p o (c t) -> p (o c) t", c=NCORES)
                    ph.do("y", lambda e, src=src, dst=dst: e.dma_start(dst, src),
                          inc="al", n=16)
                vap = goutV.ap().rearrange("(c t) h -> c t h", c=NCORES)
                for c in range(NCORES):
                    src = vap[c:c + 1, b * 512:(b + 1) * 512,
                              hg * 256:(hg + 1) * 256].rearrange(
                                  "o (tl p) h -> p (o tl) h", p=128)
                    dst = vsbs[i % 2].ap().rearrange("p (g h) -> p g h", g=32)[
                        :, c * 4:(c + 1) * 4, :]
                    ph.do("y", lambda e, src=src, dst=dst: e.dma_start(dst, src),
                          inc="al", n=16)
                c_al_iter[i] = P.cnt["al"]

            issue_loads(0)
            issue_loads(1)
            for i in range(16):
                b, hg = divmod(i, 4)
                ksb = ksbs[i % 2]
                vsb = vsbs[i % 2]
                c_prev_norm_ve = P.cnt["ve"]
                for j in range(4):
                    h = hg * 4 + j
                    idx = h * 4 + b
                    hb = (j % 2) * 64
                    pair = j // 2
                    qap = qbuf[(h % 2) * 64:(h % 2) * 64 + 64,
                               (h // 2) * T + b * 512:(h // 2) * T + b * 512 + 512]
                    if j == 0:
                        ph.wait("t", "al", c_al_iter[i])
                    ph.wait("t", "ve", P.cnt["ve"])
                    c_exp = {}
                    c_av = {}
                    for gg in range(16):
                        bank = gg % 2
                        if gg >= 2:
                            ph.wait("t", "se", c_exp[gg - 2])
                        for u in range(2):
                            g = 2 * gg + u
                            c_sc = ph.do("t", lambda e, g=g, bank=bank, u=u, pair=pair,
                                         hb=hb, qap=qap, ksb=ksb:
                                         e.matmul(sc_ps[:, (2 * bank + u) * 512:(2 * bank + u + 1) * 512],
                                                  ksb[hb:hb + 64,
                                                      pair * 4096 + g * 128: pair * 4096 + (g + 1) * 128],
                                                  qap, start=True, stop=True))
                        ph.wait("s", "pe", c_sc)
                        c_exp[gg] = ph.do("s", lambda e, bank=bank, idx=idx: e.activation(
                            psb[:, bank * 1024:(bank + 1) * 1024],
                            sc_ps[:, bank * 1024:(bank + 1) * 1024], AF.Exp,
                            bias=negc_rep[:, idx:idx + 1]))
                        ph.wait("t", "se", c_exp[gg])
                        for u in range(2):
                            g = 2 * gg + u
                            ph.do("t", lambda e, g=g, j=j, bank=bank, u=u, vsb=vsb: e.matmul(
                                av_ps[0:64, j * 512:(j + 1) * 512],
                                vsb[:, g * 256 + j * 64: g * 256 + j * 64 + 64],
                                psb[:, (2 * bank + u) * 512:(2 * bank + u + 1) * 512],
                                start=(g == 0), stop=False), inc=None)
                            c_av[gg] = ph.do("t", lambda e, g=g, j=j, bank=bank, u=u: e.matmul(
                                av_ps[64:65, j * 512:(j + 1) * 512],
                                ones_col_bf[:, :],
                                psb[:, (2 * bank + u) * 512:(2 * bank + u + 1) * 512],
                                start=(g == 0), stop=(g == 31), skip_group_check=True))
                    ph.wait("v", "pe", c_av[15])
                    c_rc = ph.do("v", lambda e, j=j: e.reciprocal(
                        rrow[:, :], av_ps[64:65, j * 512:(j + 1) * 512]))
                    ph.wait("t", "ve", c_rc)
                    c_rp = ph.do("t", lambda e: e.matmul(
                        sc_ps[0:64, 0:512], ones_row[0:1, 0:64], rrow[:, :],
                        start=True, stop=True))
                    ph.wait("v", "pe", c_rp)
                    ph.do("v", lambda e: e.tensor_copy(rep_sb[:, :], sc_ps[0:64, 0:512]))
                    hcd = h // 2
                    hbd = (h % 2) * 64
                    ph.do("v", lambda e, j=j, hcd=hcd, hbd=hbd: e.tensor_mul(
                        abuf[hbd:hbd + 64, hcd * 512:(hcd + 1) * 512],
                        av_ps[0:64, j * 512:(j + 1) * 512], rep_sb[:, :]))
                c_iter_done[i] = P.cnt["pe"]
                if i + 2 < 16:
                    issue_loads(i + 2)
                if hg == 3:
                    c_a = P.cnt["ve"]
                    c_evac = {}
                    for hc in range(NCH):
                        bank = hc % 2
                        if hc == 0:
                            ph.wait("t", "ve", c_a)
                            ph.wait("t", "wl", c_wo)
                        if hc >= 2:
                            ph.wait("t", "ve", c_evac[hc - 2])
                        for cc in range(NCH):
                            ph.do("t", lambda e, cc=cc, hc=hc, bank=bank: e.matmul(
                                sc_ps[:, bank * 512:(bank + 1) * 512],
                                wo_sb[:, cc * D + hc * 128: cc * D + (hc + 1) * 128],
                                abuf[:, cc * 512:(cc + 1) * 512],
                                start=(cc == 0), stop=(cc == NCH - 1)),
                                inc=(None if cc < NCH - 1 else "pe"))
                        c_pe = P.cnt["pe"]
                        ph.wait("v", "pe", c_pe)
                        c_evac[hc] = ph.do("v", lambda e, hc=hc, b=b, bank=bank:
                                           e.scalar_tensor_tensor(
                            x_fm[:, hc * T + b * 512: hc * T + b * 512 + 512],
                            sc_ps[:, bank * 512:(bank + 1) * 512],
                            bout_sb[:, hc:hc + 1],
                            x_fm[:, hc * T + b * 512: hc * T + b * 512 + 512],
                            ALU.add, ALU.add))
                    c_iter_done[i] = P.cnt["pe"]
            ph.flush()


CFG_L0A = dict(uid="A", kind="ln", eps=1e-5, gain=None, rope=False,
               wq="l0a_wqt", wk="l0a_wkt", wv="l0a_wvt", wo="l0a_wot",
               bout="l0a_bout", qb=None, kb=None, vb=None)
CFG_L0F_ATT = dict(uid="F", kind="rms", eps=1e-6, gain="l0f_n1w", rope=True,
                   wq="l0f_wqt", wk="l0f_wkt", wv="l0f_wvt", wo="l0f_wot",
                   bout="l0f_bout", qb="l0f_qb", kb="l0f_kb", vb="l0f_vb")


def emit_l0f_outer(P, ph):
    nc = P.nc
    x_fm = P.sb["x_fm"]
    xsp = P.dint("xspill", [128, NCH * T], F32)
    c = ph.do("y", lambda e: e.dma_start(xsp[:, :], x_fm[:, :]), inc="st", n=16)
    ph.wait("y", "st", c)
    ph.flush()
    emit_norm_fm(P, ph, x_fm, x_fm, "ln", 1e-5, uid="L0FO")
    ph.flush()


def emit_l0f_close(P, ph):
    nc = P.nc
    x_fm = P.sb["x_fm"]
    xsp = P.dram["xspill"]
    with nc.sbuf_tensor("xrel", [128, NCH * T], F32) as xr:
        c = ph.do("y", lambda e: e.dma_start(xr[:, :], xsp[:, :]), inc="al", n=16)
        ph.wait("v", "al", c)
        for cc in range(NCH):
            for tt in range(NTT):
                ph.do("v", lambda e, cc=cc, tt=tt: e.tensor_add(
                    x_fm[:, cc * T + tt * TTW: cc * T + (tt + 1) * TTW],
                    x_fm[:, cc * T + tt * TTW: cc * T + (tt + 1) * TTW],
                    xr[:, cc * T + tt * TTW: cc * T + (tt + 1) * TTW]))
        ph.flush()


def build_launch(phases):
    nc = bass.Bass("TRN2", target_bir_lowering=False)
    P = Prog(nc)
    ph = P.phase()
    emit_consts(P, ph)
    for p in phases:
        if p == "in":
            emit_input_convert(P, ph)
        elif p == "fmload":
            emit_fm_load(P, ph)
        elif p == "l0a":
            emit_attn(P, ph, CFG_L0A)
        elif p == "l0f_outer":
            emit_l0f_outer(P, ph)
        elif p == "l0f_att":
            emit_attn(P, ph, CFG_L0F_ATT)
        elif p == "l0f_moe":
            emit_moe_l0(P, ph)
        elif p == "l0f_close":
            emit_l0f_close(P, ph)
        elif p == "l1a":
            emit_l1_attn(P, ph)
        elif p == "l1f":
            emit_moe_l1(P, ph)
        elif p == "out":
            emit_output_convert(P, ph)
        elif p == "fmstore":
            emit_fm_store(P, ph)
        else:
            raise ValueError(p)
    return P


L0F = ["l0f_outer", "l0f_att", "l0f_moe", "l0f_close"]
PLAN = [["in", "l0a"] + L0F + ["l1a", "l1f", "out"]]


def kernel(**inputs):
    g, percore = host_prep(inputs)
    hand = None
    final = [None] * NCORES
    for phases in PLAN:
        P = build_launch(phases)
        innames = P.innames
        in_maps = []
        for c in range(NCORES):
            m = {}
            for n in innames:
                if n in percore[c]:
                    m[n] = percore[c][n]
                elif n == "xfm_in":
                    m[n] = hand[c]
                else:
                    m[n] = g[n]
            in_maps.append(m)
        res = run_bass_kernel_spmd(P.nc, in_maps, core_ids=list(range(NCORES)))
        r0 = res.results
        if "xfm_out" in r0[0]:
            hand = [r0[c]["xfm_out"] for c in range(NCORES)]
        if "x_out" in r0[0]:
            final = [r0[c]["x_out"] for c in range(NCORES)]
    return np.concatenate(final, axis=1)


def emit_moe_l0(P, ph):
    raise NotImplementedError


def emit_moe_l1(P, ph):
    raise NotImplementedError


def emit_l1_attn(P, ph):
    raise NotImplementedError


def emit_gating(P, ph, zget, gname, comb_fm, uid, tks, alpha_sb=None, tk0=0):
    nc = P.nc
    iota8v = P.sb["iota8v"]
    idn = P.sb["idn"]
    dg = P.din(gname, [128, NCH * 8], F32)
    with nc.sbuf_tensor(f"gw{uid}", [128, NCH * 8], BF16) as gsb, \
         nc.sbuf_tensor(f"gt{uid}", [128, 40], F32) as gt, \
         nc.sbuf_tensor(f"i100{uid}", [128, 8], F32) as i100, \
         nc.psum_tensor(f"gp{uid}", [128, 512], F32) as gps, \
         nc.psum_tensor(f"tp{uid}", [8, 1024], F32) as tps:
        c_g = ph.do("g", lambda e: e.dma_start(gsb[:, :], dg[:, :]), inc="wl", n=16)
        lg = gt[:, 0:8]; e1 = gt[:, 8:16]; t8 = gt[:, 16:24]
        m1 = gt[:, 24:25]; m2 = gt[:, 25:26]; p1 = gt[:, 26:27]; p2 = gt[:, 27:28]
        nm1 = gt[:, 28:29]; tx = gt[:, 29:30]; g1 = gt[:, 30:31]; g2 = gt[:, 31:32]
        mk1 = gt[:, 32:40]
        c_i = ph.do("v", lambda e: e.tensor_scalar(i100[:, :], iota8v[:, :], 100.0,
                                                   None, ALU.add))
        prev = {}
        for it, tk in enumerate(tks):
            deps = [("wl", c_g), ("ve", c_i)]
            if it >= 1:
                deps.append(("ve", prev[it - 1]))
            s_ = Seq(ph, deps=deps)
            for sem, v in deps:
                ph.wait("t", sem, v)
            for cc in range(NCH):
                ph.do("t", lambda e, cc=cc, tk=tk: e.matmul(
                    gps[:, 0:8], zget(cc, tk),
                    gsb[:, cc * 8:(cc + 1) * 8],
                    start=(cc == 0), stop=(cc == NCH - 1)),
                    inc=(None if cc < NCH - 1 else "pe"))
            s_.deps = [("pe", P.cnt["pe"])]
            s_.do("v", lambda e: e.tensor_copy(lg, gps[:, 0:8]))
            s_.do("v", lambda e: e.reduce_max(m1, lg, axis=AXX))
            s_.do("v", lambda e: e.tensor_scalar(e1, lg, m1, None, ALU.is_equal))
            s_.do("v", lambda e: e.scalar_tensor_tensor(t8, e1, -100.0, i100[:, :],
                                                        ALU.mult, ALU.add))
            s_.do("v", lambda e: e.tensor_reduce(p1, t8, AXX, ALU.min))
            s_.do("v", lambda e: e.tensor_scalar(mk1, iota8v[:, :], p1, None, ALU.is_equal))
            s_.do("v", lambda e: e.scalar_tensor_tensor(t8, mk1, -1e30, lg,
                                                        ALU.mult, ALU.add))
            s_.do("v", lambda e: e.reduce_max(m2, t8, axis=AXX))
            s_.do("v", lambda e: e.tensor_scalar(e1, t8, m2, None, ALU.is_equal))
            s_.do("v", lambda e: e.scalar_tensor_tensor(t8, e1, -100.0, i100[:, :],
                                                        ALU.mult, ALU.add))
            s_.do("v", lambda e: e.tensor_reduce(p2, t8, AXX, ALU.min))
            s_.do("v", lambda e: e.tensor_scalar(nm1, m1, -1.0, None, ALU.mult))
            s_.do("s", lambda e: e.activation(tx, m2, AF.Exp, bias=nm1))
            s_.do("v", lambda e: e.tensor_scalar(tx, tx, 1.0, None, ALU.add))
            s_.do("v", lambda e: e.reciprocal(g1, tx))
            s_.do("v", lambda e: e.tensor_scalar(g2, g1, -1.0, 1.0, ALU.mult, ALU.add))
            s_.do("v", lambda e: e.tensor_scalar(mk1, mk1, g1, None, ALU.mult))
            s_.do("v", lambda e: e.tensor_scalar(e1, iota8v[:, :], p2, None, ALU.is_equal))
            s_.do("v", lambda e: e.scalar_tensor_tensor(lg, e1, g2, mk1,
                                                        ALU.mult, ALU.add))
            if alpha_sb is not None:
                s_.do("v", lambda e: e.tensor_mul(lg, lg, alpha_sb[:, :]))
            bank = it % 2
            s_.do("t", lambda e, bank=bank: e.transpose(
                tps[:, bank * 512:bank * 512 + 128], lg, idn[:, :]))
            prev[it] = s_.do("v", lambda e, tk=tk, bank=bank: e.tensor_copy(
                comb_fm[0:8, (tk - tk0) * 128:(tk - tk0 + 1) * 128],
                tps[0:8, bank * 512:bank * 512 + 128]))
        ph.flush()
    return P.cnt["ve"]



def _moe_bufs(P, st, uid, NF, ntt, mode, blk1):
    nc = P.nc
    B = {}
    B["h"] = st.enter_context(nc.sbuf_tensor(f"h{uid}", [128, NF * ntt * TTW], BF16))
    B["w1"] = st.enter_context(nc.sbuf_tensor(f"w1{uid}", [128, 2 * blk1], BF16))
    B["w2"] = st.enter_context(nc.sbuf_tensor(f"w2{uid}", [128, 2 * blk1], BF16)) \
        if mode == "l0" else None
    B["w3"] = st.enter_context(nc.sbuf_tensor(f"w3{uid}", [128, 2 * NF * 128], BF16))
    B["b1"] = st.enter_context(nc.sbuf_tensor(f"b1{uid}", [128, NF], F32))
    B["b2"] = st.enter_context(nc.sbuf_tensor(f"b2{uid}", [128, NF], F32)) \
        if mode == "l0" else None
    B["b3"] = st.enter_context(nc.sbuf_tensor(f"b3{uid}", [1, D], BF16))
    B["cr"] = st.enter_context(nc.sbuf_tensor(f"cr{uid}", [1, ntt * TTW], BF16))
    B["cp"] = st.enter_context(nc.sbuf_tensor(f"cp{uid}", [128, ntt * TTW], F32))
    B["s1"] = st.enter_context(nc.sbuf_tensor(f"s1{uid}", [128, 2 * TTW], BF16))
    B["a2"] = st.enter_context(nc.sbuf_tensor(f"a2{uid}", [128, 2 * TTW], BF16)) \
        if mode == "l0" else None
    B["ze"] = st.enter_context(nc.sbuf_tensor(f"ze{uid}", [128, NCH * ntt * TTW], BF16)) \
        if mode == "l1" else None
    return B


def _emit_moe_experts(P, ph, B, zget, comb, x_fm, uid, NF, blk1,
                      dw1, dw2, dw3, db1, db2, db3, mode, tbase, ntt,
                      lnw=None, lnb=None, comb_base=None):
    nc = P.nc
    ones_row_bf = P.sb["ones_row_bf"]
    TGW = ntt * TTW
    h_sb = B["h"]; w1s = B["w1"]; w2s = B["w2"]; w3s = B["w3"]
    b1s = B["b1"]; b2s = B["b2"]; b3r = B["b3"]
    crow = B["cr"]; crep = B["cp"]; s1t = B["s1"]; a2t = B["a2"]; zes = B["ze"]
    with nc.psum_tensor(f"ap{uid}{tbase}", [128, 2048], F32) as aps, \
         nc.psum_tensor(f"yp{uid}{tbase}", [128, 1024], F32) as yps, \
         nc.psum_tensor(f"rp{uid}{tbase}", [128, 1024], F32) as rps:
        c_prev_y = 0
        c_prev_fc1 = 0
        c_prev_b = (0, 0)
        for e in range(MOE_NEXP):
            ph.wait("y", "se", c_prev_b[0])
            ph.wait("g", "pe", c_prev_b[1])
            c_b1 = ph.do("y", lambda e_, e=e: e_.dma_start(b1s[:, :], db1.ap()[e, :, :]),
                         inc="al", n=16)
            if mode == "l0":
                c_b1 = ph.do("y", lambda e_, e=e: e_.dma_start(b2s[:, :], db2.ap()[e, :, :]),
                             inc="al", n=16)
            c_b3 = ph.do("g", lambda e_, e=e: e_.dma_start(b3r[:, :], db3.ap()[e, :, :]),
                         inc="al", n=16)
            cb = tbase if comb_base is None else comb_base
            c_cr = ph.do("g", lambda e_, e=e, cb=cb: e_.dma_start(
                crow[:, :], comb[e:e + 1, cb:cb + TGW]), inc="al", n=16)
            ph.wait("t", "al", c_cr)
            ph.wait("t", "ve", P.cnt["ve"])
            c_crepq = {}
            for q in range(ntt):
                if q >= 2:
                    ph.wait("t", "ve", c_crepq[q - 2])
                ph.do("t", lambda e_, q=q: e_.matmul(
                    rps[:, (q % 2) * 512:(q % 2) * 512 + 512], ones_row_bf[:, :],
                    crow[0:1, q * TTW:(q + 1) * TTW], start=True, stop=True))
                c_pe = P.cnt["pe"]
                ph.wait("v", "pe", c_pe)
                c_crepq[q] = ph.do("v", lambda e_, q=q: e_.tensor_copy(
                    crep[:, q * TTW:(q + 1) * TTW],
                    rps[:, (q % 2) * 512:(q % 2) * 512 + 512]))
            c_creps = P.cnt["ve"]
            if mode == "l1":
                ph.wait("s", "pe", c_prev_fc1)
                ph.wait("s", "al", c_b1)
                for cc in range(NCH):
                    for q in range(ntt):
                        ph.do("s", lambda e_, cc=cc, q=q, e=e: e_.activation(
                            zes[:, cc * TGW + q * TTW: cc * TGW + (q + 1) * TTW],
                            zget(cc, q),
                            AF.Identity, bias=lnb[:, e * NCH + cc:e * NCH + cc + 1],
                            scale=lnw[:, e * NCH + cc:e * NCH + cc + 1]))
                c_ze = P.cnt["se"]
                zin = lambda cc, q: zes[:, cc * TGW + q * TTW: cc * TGW + (q + 1) * TTW]
            else:
                c_ze = 0
                zin = zget
            wdmas = {}

            def load_blk(fc, e=e):
                dep = wdmas.get(fc - 2)
                if dep is not None:
                    ph.wait("g", "pe", dep[1])
                c1 = ph.do("g", lambda e_, fc=fc, e=e: e_.dma_start(
                    w1s.ap().rearrange("p (h w) -> p h w", h=2)[:, fc % 2:fc % 2 + 1, :],
                    dw1.ap()[e, fc:fc + 1, :, :].rearrange("o p w -> p o w")),
                    inc="wl", n=16)
                if mode == "l0":
                    c1 = ph.do("g", lambda e_, fc=fc, e=e: e_.dma_start(
                        w2s.ap().rearrange("p (h w) -> p h w", h=2)[:, fc % 2:fc % 2 + 1, :],
                        dw2.ap()[e, fc:fc + 1, :, :].rearrange("o p w -> p o w")),
                        inc="wl", n=16)
                wdmas[fc] = (c1, 0)

            load_blk(0)
            load_blk(1)
            c_sil = {}
            c_a2c = {}
            c_hh = {}
            for fc in range(NF):
                for q in range(ntt):
                    i = fc * ntt + q
                    bank = i % 2
                    if i == 0:
                        ph.wait("t", "wl", wdmas[fc][0])
                        ph.wait("t", "al", c_b1)
                        ph.wait("t", "se", c_ze)
                        ph.wait("t", "ve", c_creps)
                    elif q == 0:
                        ph.wait("t", "wl", wdmas[fc][0])
                    if i >= 2:
                        ph.wait("t", "se", c_sil[i - 2])
                    for cc in range(NCH):
                        ph.do("t", lambda e_, cc=cc, fc=fc, q=q, bank=bank: e_.matmul(
                            aps[:, bank * 512:bank * 512 + 512],
                            w1s[:, (fc % 2) * blk1 + cc * 128:(fc % 2) * blk1 + (cc + 1) * 128],
                            zin(cc, q),
                            start=(cc == 0), stop=(cc == NCH - 1)),
                            inc=(None if cc < NCH - 1 else "pe"))
                    c_a1 = P.cnt["pe"]
                    if mode == "l0":
                        if i >= 2:
                            ph.wait("t", "ve", c_a2c[i - 2])
                        for cc in range(NCH):
                            ph.do("t", lambda e_, cc=cc, fc=fc, q=q, bank=bank: e_.matmul(
                                aps[:, 1024 + bank * 512:1024 + bank * 512 + 512],
                                w2s[:, (fc % 2) * blk1 + cc * 128:(fc % 2) * blk1 + (cc + 1) * 128],
                                zin(cc, q),
                                start=(cc == 0), stop=(cc == NCH - 1)),
                                inc=(None if cc < NCH - 1 else "pe"))
                        c_a2 = P.cnt["pe"]
                    hofs = fc * TGW + q * TTW
                    ph.wait("s", "pe", c_a1)
                    if i >= 2:
                        ph.wait("s", "ve", c_hh[i - 2])
                    if mode == "l0":
                        c_sil[i] = ph.do("s", lambda e_, fc=fc, bank=bank, i=i: e_.activation(
                            s1t[:, (i % 2) * TTW:(i % 2 + 1) * TTW],
                            aps[:, bank * 512:bank * 512 + 512], AF.Silu,
                            bias=b1s[:, fc:fc + 1]))
                        ph.wait("v", "pe", c_a2)
                        c_a2c[i] = ph.do("v", lambda e_, fc=fc, q=q, bank=bank, i=i:
                                         e_.scalar_tensor_tensor(
                            a2t[:, (i % 2) * TTW:(i % 2 + 1) * TTW],
                            aps[:, 1024 + bank * 512:1024 + bank * 512 + 512],
                            b2s[:, fc:fc + 1],
                            crep[:, q * TTW:(q + 1) * TTW], ALU.add, ALU.mult))
                        ph.wait("v", "se", c_sil[i])
                        c_hh[i] = ph.do("v", lambda e_, hofs=hofs, i=i: e_.tensor_mul(
                            h_sb[:, hofs:hofs + TTW],
                            s1t[:, (i % 2) * TTW:(i % 2 + 1) * TTW],
                            a2t[:, (i % 2) * TTW:(i % 2 + 1) * TTW]))
                    else:
                        c_sil[i] = ph.do("s", lambda e_, fc=fc, bank=bank, i=i: e_.activation(
                            s1t[:, (i % 2) * TTW:(i % 2 + 1) * TTW],
                            aps[:, bank * 512:bank * 512 + 512], AF.Gelu_apprx_tanh,
                            bias=b1s[:, fc:fc + 1]))
                        ph.wait("v", "se", c_sil[i])
                        c_hh[i] = ph.do("v", lambda e_, hofs=hofs, q=q, i=i: e_.tensor_mul(
                            h_sb[:, hofs:hofs + TTW],
                            s1t[:, (i % 2) * TTW:(i % 2 + 1) * TTW],
                            crep[:, q * TTW:(q + 1) * TTW]))
                if fc + 2 < NF:
                    wdmas[fc] = (wdmas[fc][0], P.cnt["pe"])
                    load_blk(fc + 2)
            c_prev_fc1 = P.cnt["pe"]
            c_hall = P.cnt["ve"]
            if MOE_SKIP_FC3:
                c_prev_y = P.cnt["pe"]
                c_prev_b = (P.cnt["se"], P.cnt["pe"])
                continue
            w3dmas = {}

            def load_w3(dc, e=e):
                dep = w3dmas.get(dc - 2)
                if dep is not None:
                    ph.wait("g", "pe", dep[1])
                c3 = ph.do("g", lambda e_, dc=dc, e=e: e_.dma_start(
                    w3s.ap().rearrange("p (h w) -> p h w", h=2)[:, dc % 2:dc % 2 + 1, :],
                    dw3.ap()[e, dc:dc + 1, :, :].rearrange("o p w -> p o w")),
                    inc="wl", n=16)
                w3dmas[dc] = (c3, 0)

            load_w3(0)
            load_w3(1)
            c_ev = {}
            for dc in range(NCH):
                for q in range(ntt):
                    i2 = dc * ntt + q
                    bank = i2 % 2
                    if q == 0:
                        ph.wait("t", "wl", w3dmas[dc][0])
                    if i2 == 0:
                        ph.wait("t", "ve", c_hall)
                        ph.wait("t", "al", c_b3)
                    if i2 >= 2:
                        ph.wait("t", "ve", c_ev[i2 - 2])
                    for fcc in range(NF):
                        ph.do("t", lambda e_, fcc=fcc, dc=dc, q=q, bank=bank: e_.matmul(
                            yps[:, bank * 512:bank * 512 + 512],
                            w3s[:, (dc % 2) * NF * 128 + fcc * 128:(dc % 2) * NF * 128 + (fcc + 1) * 128],
                            h_sb[:, fcc * TGW + q * TTW: fcc * TGW + (q + 1) * TTW],
                            start=(fcc == 0), stop=False), inc=None)
                    ph.do("t", lambda e_, dc=dc, q=q, bank=bank: e_.matmul(
                        yps[:, bank * 512:bank * 512 + 512],
                        b3r[0:1, dc * 128:(dc + 1) * 128],
                        crow[0:1, q * TTW:(q + 1) * TTW],
                        start=False, stop=True, skip_group_check=True), inc="pe")
                    c_y = P.cnt["pe"]
                    ph.wait("v", "pe", c_y)
                    c_ev[i2] = ph.do("v", lambda e_, dc=dc, q=q, bank=bank, tbase=tbase:
                                     e_.scalar_tensor_tensor(
                        x_fm[:, dc * T + tbase + q * TTW: dc * T + tbase + (q + 1) * TTW],
                        yps[:, bank * 512:bank * 512 + 512], 1.0,
                        x_fm[:, dc * T + tbase + q * TTW: dc * T + tbase + (q + 1) * TTW],
                        ALU.mult, ALU.add))
                if dc + 2 < NCH:
                    w3dmas[dc] = (w3dmas[dc][0], P.cnt["pe"])
                    load_w3(dc + 2)
            c_prev_y = P.cnt["pe"]
            c_prev_b = (P.cnt["se"], P.cnt["pe"])


def emit_moe_l0(P, ph):
    from contextlib import ExitStack
    nc = P.nc
    x_fm = P.sb["x_fm"]
    NF = 16
    dn2 = P.din("l0f_n2w", [128, NCH], F32)
    dw1 = P.din("l0f_w1b", [8, NF, 128, NCH * 128], F32)
    dw2 = P.din("l0f_w2b", [8, NF, 128, NCH * 128], F32)
    dw3 = P.din("l0f_w3b", [8, NCH, 128, NF * 128], F32)
    db1 = P.din("l0f_b1", [8, 128, NF], F32)
    db2 = P.din("l0f_b2", [8, 128, NF], F32)
    db3 = P.din("l0f_b3", [8, 1, D], F32)
    with ExitStack() as st:
        zp = st.enter_context(nc.sbuf_tensor("l0m_z", [128, NCH * T], BF16))
        comb = st.enter_context(nc.sbuf_tensor("l0m_c", [8, T], F32))
        n2w = st.enter_context(nc.sbuf_tensor("l0m_n2", [128, NCH], F32))
        c_n2 = ph.do("y", lambda e: e.dma_start(n2w[:, :], dn2[:, :]), inc="al", n=16)
        ph.wait("v", "al", c_n2)
        emit_norm_fm(P, ph, x_fm, zp, "rms", 1e-6, w_sb=n2w, uid="l0m")
        emit_gating(P, ph,
                    lambda cc, tk: zp[:, cc * T + tk * 128: cc * T + (tk + 1) * 128],
                    "l0f_gatet", comb, "l0m", tks=range(16))
        if MOE_DBG:
            dz = P.dout("dump_mz", [128, NCH * T], BF16)
            dc_ = P.dout("dump_comb", [8, T], F32)
            c = ph.do("y", lambda e: e.dma_start(dz[:, :], zp[:, :]), inc="st", n=16)
            c = ph.do("y", lambda e: e.dma_start(dc_[:, :], comb[:, :]), inc="st", n=16)
            ph.wait("y", "st", c)
            ph.flush()
            return
        B = _moe_bufs(P, st, "l0m", NF, NTT, "l0", NCH * 128)
        _emit_moe_experts(P, ph, B,
                          lambda cc, q: zp[:, cc * T + q * TTW: cc * T + (q + 1) * TTW],
                          comb, x_fm, "l0m", NF, NCH * 128,
                          dw1, dw2, dw3, db1, db2, db3, "l0", tbase=0, ntt=NTT)
        ph.flush()


def emit_moe_l1(P, ph):
    from contextlib import ExitStack
    nc = P.nc
    x_fm = P.sb["x_fm"]
    ones_row = P.sb["ones_row"]
    NF = 32
    NTG = 2
    NTTG = 2
    dw1 = P.din("l1f_fc1b", [8, NF, 128, NCH * 128], F32)
    dw3 = P.din("l1f_fc2b", [8, NCH, 128, NF * 128], F32)
    db1 = P.din("l1f_b1", [8, 128, NF], F32)
    db3 = P.din("l1f_b2", [8, 1, D], F32)
    dlnw = P.din("l1f_lnw", [8, 128, NCH], F32)
    dlnb = P.din("l1f_lnb", [8, 128, NCH], F32)
    dalpha = P.din("l1f_alpha", [1, 8], F32)
    with ExitStack() as st:
        zp = st.enter_context(nc.sbuf_tensor("l1m_z", [128, NCH * NTTG * TTW], BF16))
        comb = st.enter_context(nc.sbuf_tensor("l1m_c", [8, NTTG * TTW], F32))
        lnw = st.enter_context(nc.sbuf_tensor("l1m_lw", [128, 8 * NCH], F32))
        lnb = st.enter_context(nc.sbuf_tensor("l1m_lb", [128, 8 * NCH], F32))
        arow = st.enter_context(nc.sbuf_tensor("l1m_ar", [1, 8], F32))
        alph = st.enter_context(nc.sbuf_tensor("l1m_al", [128, 8], F32))
        ph.do("y", lambda e: e.dma_start(
            lnw.ap().rearrange("p (e c) -> p e c", e=8),
            dlnw.ap().rearrange("e p c -> p e c")), inc="al", n=16)
        ph.do("y", lambda e: e.dma_start(
            lnb.ap().rearrange("p (e c) -> p e c", e=8),
            dlnb.ap().rearrange("e p c -> p e c")), inc="al", n=16)
        c_a = ph.do("y", lambda e: e.dma_start(arow[:, :], dalpha[:, :]), inc="al", n=16)
        with nc.psum_tensor("l1m_alp", [128, 64], F32) as alps:
            sq = Seq(ph, deps=[("al", c_a)])
            sq.do("t", lambda e: e.matmul(alps[:, 0:8], ones_row[:, :], arow[:, :],
                                          start=True, stop=True))
            c_alph = sq.do("v", lambda e: e.tensor_copy(alph[:, :], alps[:, 0:8]))
        if True:
            B = _moe_bufs(P, st, "l1m", NF, NTTG, "l1", NCH * 128)
            for tg in range(NTG):
                tbase = tg * NTTG * TTW
                tts = [tg * NTTG + k for k in range(NTTG)]
                emit_norm_fm(P, ph, x_fm, zp, "ln", 1e-5, uid=f"l1o{tg}",
                             tts=tts, dst_stride=NTTG * TTW)
                emit_gating(P, ph,
                            lambda cc, tk, tbase=tbase: zp[:, cc * NTTG * TTW + tk * 128 - tbase:
                                                           cc * NTTG * TTW + (tk + 1) * 128 - tbase],
                            "l1f_wg", comb, f"l1m{tg}",
                            tks=range(tg * NTTG * 4, (tg + 1) * NTTG * 4),
                            alpha_sb=alph, tk0=tg * NTTG * 4)
                emit_norm_fm(P, ph, zp, zp, "ln", 1e-5, uid=f"l1i{tg}",
                             tts=list(range(NTTG)), dst_stride=NTTG * TTW,
                             src_stride=NTTG * TTW)
                _emit_moe_experts(P, ph, B,
                                  lambda cc, q: zp[:, cc * NTTG * TTW + q * TTW:
                                                   cc * NTTG * TTW + (q + 1) * TTW],
                                  comb, x_fm, f"l1m{tg}", NF, NCH * 128,
                                  dw1, None, dw3, db1, None, db3, "l1",
                                  tbase=tbase, ntt=NTTG, lnw=lnw, lnb=lnb, comb_base=0)
        ph.flush()


KK_PAIRS = [(0, 1), (0, 2), (1, 2), (0, 3), (1, 3), (2, 3)]
KQ_PAIRS = [(0, 0), (0, 1), (1, 1), (0, 2), (1, 2), (2, 2),
            (0, 3), (1, 3), (2, 3), (3, 3)]


def emit_l1_attn(P, ph):
    from contextlib import ExitStack
    nc = P.nc
    x_fm = P.sb["x_fm"]
    ones_col = P.sb["ones_col"]; ones_row = P.sb["ones_row"]
    ones_col_bf = P.sb["ones_col_bf"]; ones_row_bf = P.sb["ones_row_bf"]
    rollp = P.sb["rollp"]
    dwq = P.din("l1a_wq", [128, NCH * D], F32)
    dwk = P.din("l1a_wk", [128, NCH * D], F32)
    dwv = P.din("l1a_wv", [128, NCH * D], F32)
    dwb = P.din("l1a_wbm", [128, NCH * 16], F32)
    dwo = P.din("l1a_wo", [128, NCH * D], F32)
    dbo = P.din("l1a_bo", [128, NCH], F32)

    dy = P.dint("l1a_y", [D, T], BF16)
    with ExitStack() as st:
        zbuf = st.enter_context(nc.sbuf_tensor("l1a_z", [128, NCH * T], BF16))
        beta_fm = st.enter_context(nc.sbuf_tensor("l1a_bfs", [16, T], BF16))
        brow = st.enter_context(nc.sbuf_tensor("l1a_br", [1, T], F32))
        rows = st.enter_context(nc.sbuf_tensor("l1a_rw", [1, 5 * TTW], F32))
        rowsb = st.enter_context(nc.sbuf_tensor("l1a_rwb", [1, 16 * TTW], BF16))
        bo_sb = st.enter_context(nc.sbuf_tensor("l1a_bos", [128, NCH], F32))
        c_bo = ph.do("y", lambda e: e.dma_start(bo_sb[:, :], dbo[:, :]), inc="al", n=16)

        emit_norm_fm(P, ph, x_fm, zbuf, "ln", 1e-5, uid="l1a")
        c_z = P.cnt["ve"]

        with nc.sbuf_tensor("l1a_wbs", [128, NCH * 16], BF16) as wbs, \
             nc.psum_tensor("l1a_bp", [16, 512], F32) as bps:
            c_wb = ph.do("g", lambda e: e.dma_start(wbs[:, :], dwb[:, :]), inc="wl", n=16)
            base_se = P.cnt["se"]
            for tt in range(NTT):
                if tt == 0:
                    ph.wait("t", "wl", c_wb)
                    ph.wait("t", "ve", c_z)
                if tt >= 1:
                    ph.wait("t", "se", base_se + tt)
                for cc in range(NCH):
                    ph.do("t", lambda e, cc=cc, tt=tt: e.matmul(
                        bps[:, :], wbs[:, cc * 16:(cc + 1) * 16],
                        zbuf[:, cc * T + tt * TTW: cc * T + (tt + 1) * TTW],
                        start=(cc == 0), stop=(cc == NCH - 1)),
                        inc=(None if cc < NCH - 1 else "pe"))
                c_pe = P.cnt["pe"]
                ph.wait("s", "pe", c_pe)
                ph.do("s", lambda e, tt=tt: e.activation(
                    beta_fm[:, tt * TTW:(tt + 1) * TTW], bps[:, :], AF.Sigmoid))
            c_brow = P.cnt["se"]
            ph.flush()

        with nc.sbuf_tensor("l1a_wp", [128, 2 * 3 * NCH * 128], BF16) as wpair, \
             nc.sbuf_tensor("l1a_qkv", [128, 3 * T], BF16) as qkv, \
             nc.sbuf_tensor("l1a_cat", [128, 2 * T], BF16) as cat, \
             nc.sbuf_tensor("l1a_phi", [128, T], BF16) as phif, \
             nc.sbuf_tensor("l1a_kphi", [128, T], BF16) as kphi, \
             nc.sbuf_tensor("l1a_qphi", [128, T], BF16) as qphi, \
             nc.sbuf_tensor("l1a_dp", [128, TTW], F32) as dprod, \
             nc.sbuf_tensor("l1a_u", [64, 6 * TTW], F32) as ubuf, \
             nc.sbuf_tensor("l1a_yt", [64, TTW], BF16) as ytile, \
             nc.sbuf_tensor("l1a_vt", [64, TTW], BF16) as vtile, \
             nc.psum_tensor("l1a_pj", [128, 1024], F32) as prj_ps, \
             nc.psum_tensor("l1a_rp", [128, 1024], F32) as rp_ps, \
             nc.psum_tensor("l1a_np", [1, 1024], F32) as nps, \
             nc.psum_tensor("l1a_rr", [128, 1024], F32) as rep_ps:
            arow = lambda i: rowsb[0:1, KK_PAIRS.index(i) * TTW:
                                   (KK_PAIRS.index(i) + 1) * TTW]
            crow_ = lambda i: rowsb[0:1, (6 + KQ_PAIRS.index(i)) * TTW:
                                    (6 + KQ_PAIRS.index(i) + 1) * TTW]
            qrrow = lambda s: rows[0:1, s * TTW:(s + 1) * TTW]
            mrow = rows[0:1, 4 * TTW:5 * TTW]
            PW = NCH * 128

            wdma = {}

            def load_pair(pr):
                half = pr % 2
                dep = wdma.get(pr - 2)
                if dep is not None:
                    ph.wait("g", "pe", dep[1])
                c = 0
                for wi, dw in enumerate((dwq, dwk, dwv)):
                    c = ph.do("g", lambda e, dw=dw, half=half, wi=wi, pr=pr: e.dma_start(
                        wpair.ap().rearrange("p (h w c) -> p h w c", h=2, w=3)[
                            :, half, wi, :].rearrange("p (cc m) -> p cc m", cc=NCH),
                        dw.ap().rearrange("p (cc m) -> p cc m", cc=NCH)[
                            :, :, pr * 128:(pr + 1) * 128]), inc="wl", n=16)
                wdma[pr] = (c, 0)

            load_pair(0)
            load_pair(1)
            c_prev_heads = c_z
            for pr in range(8):
                half = pr % 2
                base_se = P.cnt["se"]
                i = 0
                for wi in range(3):
                    for tt in range(NTT):
                        bank = i % 2
                        if i == 0:
                            ph.wait("t", "wl", wdma[pr][0])
                            ph.wait("t", "ve", c_prev_heads)
                        if i >= 2:
                            ph.wait("t", "se", base_se + i - 1)
                        for cc in range(NCH):
                            ph.do("t", lambda e, cc=cc, wi=wi, tt=tt, bank=bank, half=half:
                                  e.matmul(
                                      prj_ps[:, bank * 512:(bank + 1) * 512],
                                      wpair[:, half * 3 * PW + wi * PW + cc * 128:
                                            half * 3 * PW + wi * PW + (cc + 1) * 128],
                                      zbuf[:, cc * T + tt * TTW: cc * T + (tt + 1) * TTW],
                                      start=(cc == 0), stop=(cc == NCH - 1)),
                                  inc=(None if cc < NCH - 1 else "pe"))
                        c_pe = P.cnt["pe"]
                        ph.wait("s", "pe", c_pe)
                        ph.do("s", lambda e, wi=wi, tt=tt, bank=bank: e.activation(
                            qkv[:, wi * T + tt * TTW: wi * T + (tt + 1) * TTW],
                            prj_ps[:, bank * 512:(bank + 1) * 512], AF.Identity))
                        i += 1
                wdma[pr] = (wdma[pr][0], P.cnt["pe"])
                if pr + 2 < 8:
                    load_pair(pr + 2)
                c_proj = P.cnt["se"]

                sq = Seq(ph, deps=[("se", c_proj)])
                for hj in range(2):
                    h = pr * 2 + hj
                    hb = hj * 64
                    sq.do("g", lambda e, h=h: e.dma_start(
                        brow[:, :], beta_fm[h:h + 1, :]), inc="al", n=16)
                    for which, (src_off, phidst) in enumerate(
                            ((T, kphi), (0, qphi))):
                        sq.do("s", lambda e, src_off=src_off, hb=hb: e.activation(
                            cat[0:64, 0:T], qkv[src_off + 0:src_off + 0][0:0, 0:0]
                            if False else qkv[hb:hb + 64, src_off:src_off + T],
                            AF.Relu))
                        sq.do("s", lambda e, src_off=src_off, hb=hb: e.activation(
                            cat[64:128, 0:T], qkv[hb:hb + 64, src_off:src_off + T],
                            AF.Relu, scale=-1.0))
                        for tt in range(NTT):
                            bank = tt % 2
                            sq.do("t", lambda e, tt=tt, bank=bank: e.matmul(
                                rp_ps[:, bank * 512:(bank + 1) * 512], rollp[:, :],
                                cat[:, tt * TTW:(tt + 1) * TTW], start=True, stop=True))
                            sq.do("v", lambda e, tt=tt, bank=bank: e.tensor_mul(
                                phif[:, tt * TTW:(tt + 1) * TTW],
                                cat[:, tt * TTW:(tt + 1) * TTW],
                                rp_ps[:, bank * 512:(bank + 1) * 512]))
                            sq.do("t", lambda e, tt=tt, bank=bank: e.matmul(
                                nps[:, bank * 512:(bank + 1) * 512], ones_col_bf[:, :],
                                phif[:, tt * TTW:(tt + 1) * TTW], start=True, stop=True))
                            sq.do("v", lambda e, tt=tt, bank=bank: e.tensor_scalar(
                                mrow[:, tt * 128: tt * 128 + 512][0:1, 0:512]
                                if False else mrow,
                                nps[0:1, bank * 512:(bank + 1) * 512], 1e-6, None,
                                ALU.add))
                            sq.do("v", lambda e, which=which, tt=tt: e.reciprocal(
                                (qrrow(tt) if which == 1 else mrow), mrow))
                            if which == 0:
                                sq.do("t", lambda e, tt=tt, bank=bank: e.matmul(
                                    rep_ps[:, bank * 512:(bank + 1) * 512][0:64, :]
                                    if False else rep_ps[0:64, bank * 512:(bank + 1) * 512],
                                    ones_row[0:1, 0:64], mrow, start=True, stop=True))
                                sq.do("t", lambda e, tt=tt, bank=bank: e.matmul(
                                    rep_ps[64:128, bank * 512:(bank + 1) * 512],
                                    ones_row[0:1, 64:128], mrow, start=True, stop=True))
                                sq.do("v", lambda e, tt=tt, bank=bank: e.tensor_mul(
                                    kphi[:, tt * TTW:(tt + 1) * TTW],
                                    phif[:, tt * TTW:(tt + 1) * TTW],
                                    rp_ps[:, bank * 512:(bank + 1) * 512] if False
                                    else rep_ps[:, bank * 512:(bank + 1) * 512]))
                            else:
                                sq.do("v", lambda e, tt=tt: e.tensor_copy(
                                    qphi[:, tt * TTW:(tt + 1) * TTW],
                                    phif[:, tt * TTW:(tt + 1) * TTW]))
                    for (i_, s_) in KK_PAIRS:
                        sq.do("v", lambda e, i_=i_, s_=s_: e.tensor_mul(
                            dprod[:, :], kphi[:, i_ * TTW:(i_ + 1) * TTW],
                            kphi[:, s_ * TTW:(s_ + 1) * TTW]))
                        sq.do("t", lambda e: e.matmul(
                            nps[:, 0:512], ones_col[:, :], dprod[:, :],
                            start=True, stop=True))
                        sq.do("v", lambda e, i_=i_, s_=s_: e.tensor_copy(
                            arow((i_, s_)), nps[0:1, 0:512]))
                    for (i_, s_) in KQ_PAIRS:
                        sq.do("v", lambda e, i_=i_, s_=s_: e.tensor_mul(
                            dprod[:, :], kphi[:, i_ * TTW:(i_ + 1) * TTW],
                            qphi[:, s_ * TTW:(s_ + 1) * TTW]))
                        sq.do("t", lambda e: e.matmul(
                            nps[:, 0:512], ones_col[:, :], dprod[:, :],
                            start=True, stop=True))
                        sq.do("v", lambda e, i_=i_, s_=s_: e.tensor_copy(
                            crow_((i_, s_)), nps[0:1, 0:512]))

                    u = lambda i: ubuf[0:64, i * TTW:(i + 1) * TTW]
                    tmp = ubuf[0:64, 4 * TTW:5 * TTW]
                    yac = ubuf[0:64, 5 * TTW:6 * TTW]
                    vsl = lambda s: qkv[hb:hb + 64, 2 * T + s * TTW: 2 * T + (s + 1) * TTW]

                    def rep64(row_ap, bank, bf=False):
                        orow = ones_row_bf if bf else ones_row
                        sq.do("t", lambda e, row_ap=row_ap, bank=bank, orow=orow:
                              e.matmul(rep_ps[0:64, bank * 512:(bank + 1) * 512],
                                       orow[0:1, 0:64], row_ap, start=True, stop=True))
                        return rep_ps[0:64, bank * 512:(bank + 1) * 512]

                    def betarow(s):
                        return brow[0:1, s * TTW:(s + 1) * TTW]

                    for s_ in range(NB):
                        sq.do("v", lambda e, s_=s_: e.tensor_copy(vtile[:, :], vsl(s_)))
                        if s_ == 0:
                            r = rep64(betarow(0), 0)
                            sq.do("v", lambda e, r=r: e.tensor_mul(u(0), vtile[:, :], r))
                        else:
                            for ii in range(s_):
                                r = rep64(arow((ii, s_)), ii % 2, bf=True)
                                if ii == 0:
                                    sq.do("v", lambda e, r=r, ii=ii: e.tensor_mul(
                                        tmp, u(ii), r))
                                else:
                                    sq.do("v", lambda e, r=r, ii=ii: e.tensor_mul(
                                        dprod[0:64, :], u(ii), r))
                                    sq.do("v", lambda e: e.tensor_add(
                                        tmp, tmp, dprod[0:64, :]))
                            sq.do("v", lambda e: e.tensor_sub(tmp, vtile[:, :], tmp))
                            r = rep64(betarow(s_), s_ % 2)
                            sq.do("v", lambda e, r=r, s_=s_: e.tensor_mul(u(s_), tmp, r))
                        for ii in range(s_ + 1):
                            r = rep64(crow_((ii, s_)), ii % 2, bf=True)
                            if ii == 0:
                                sq.do("v", lambda e, r=r, ii=ii: e.tensor_mul(
                                    yac, u(ii), r))
                            else:
                                sq.do("v", lambda e, r=r, ii=ii: e.tensor_mul(
                                    dprod[0:64, :], u(ii), r))
                                sq.do("v", lambda e: e.tensor_add(
                                    yac, yac, dprod[0:64, :]))
                        r = rep64(qrrow(s_), s_ % 2)
                        sq.do("v", lambda e, r=r: e.tensor_mul(ytile[:, :], yac, r))
                        sq.do("y", lambda e, h=h, s_=s_: e.dma_start(
                            dy.ap()[h * 64:(h + 1) * 64, s_ * TTW:(s_ + 1) * TTW],
                            ytile[:, :]), inc="st", n=16)
                c_prev_heads = P.cnt["ve"]
            ph.flush()

        with nc.sbuf_tensor("l1a_wos", [128, NCH * D], BF16) as wo_sb, \
             nc.sbuf_tensor("l1a_ab", [128, 2 * NCH * 512], BF16) as absl, \
             nc.psum_tensor("l1a_op", [128, 1024], F32) as op_ps:
            c_wo = ph.do("g", lambda e: e.dma_start(
                wo_sb.ap().rearrange("p (c m) -> p c m", c=NCH),
                dwo.ap().rearrange("p (c m) -> p c m", c=NCH)), inc="wl", n=16)
            c_y = P.cnt["st"]
            c_ldt = {}
            c_tt_pe = {}

            def load_a(tt):
                ph.wait("y", "st", c_y)
                if tt >= 2:
                    ph.wait("y", "pe", c_tt_pe[tt - 2])
                c_ldt[tt] = ph.do("y", lambda e, tt=tt: e.dma_start(
                    absl.ap().rearrange("p (h c w) -> p h c w", h=2, c=NCH)[
                        :, tt % 2, :, :],
                    dy.ap().rearrange("(c p) t -> p c t", p=128)[
                        :, :, tt * TTW:(tt + 1) * TTW]), inc="al", n=16)

            load_a(0)
            load_a(1)
            c_ev = {}
            for tt in range(NTT):
                for hc in range(NCH):
                    i = tt * NCH + hc
                    bank = i % 2
                    if hc == 0:
                        ph.wait("t", "al", c_ldt[tt])
                        if tt == 0:
                            ph.wait("t", "wl", c_wo)
                            ph.wait("t", "al", c_bo)
                    if i >= 2:
                        ph.wait("t", "ve", c_ev[i - 2])
                    for cc in range(NCH):
                        ph.do("t", lambda e, cc=cc, hc=hc, tt=tt, bank=bank: e.matmul(
                            op_ps[:, bank * 512:(bank + 1) * 512],
                            wo_sb[:, cc * D + hc * 128: cc * D + (hc + 1) * 128],
                            absl[:, (tt % 2) * NCH * 512 + cc * 512:
                                 (tt % 2) * NCH * 512 + (cc + 1) * 512],
                            start=(cc == 0), stop=(cc == NCH - 1)),
                            inc=(None if cc < NCH - 1 else "pe"))
                    c_pe = P.cnt["pe"]
                    ph.wait("v", "pe", c_pe)
                    c_ev[i] = ph.do("v", lambda e, hc=hc, tt=tt, bank=bank:
                                    e.scalar_tensor_tensor(
                        x_fm[:, hc * T + tt * TTW: hc * T + (tt + 1) * TTW],
                        op_ps[:, bank * 512:(bank + 1) * 512],
                        bo_sb[:, hc:hc + 1],
                        x_fm[:, hc * T + tt * TTW: hc * T + (tt + 1) * TTW],
                        ALU.add, ALU.add))
                c_tt_pe[tt] = P.cnt["pe"]
                if tt + 2 < NTT:
                    load_a(tt + 2)
            ph.flush()


# version 04
# speedup vs baseline: 1.2812x; 1.0022x over previous
import numpy as np
import ml_dtypes
import concourse.bass as bass
import concourse.mybir as mybir
from concourse.bass_utils import run_bass_kernel_spmd

F32 = mybir.dt.float32
BF16 = mybir.dt.bfloat16
AF = mybir.ActivationFunctionType
ALU = mybir.AluOpType
AXX = mybir.AxisListType.X

NCORES = 8
NB = 4
NL = 512
T = NB * NL
D = 1024
NCH = D // 128
NTT = 4
TTW = 512
SEQ = 4096
NH = 16
HD = 64
VDIM = NH * 65

BF = ml_dtypes.bfloat16
MOE_DBG = False
MOE_NEXP = 8
MOE_SKIP_FC3 = False
ESEM = {"t": "pe", "v": "ve", "s": "se", "g": "ge"}



def _blocks(wT, mblk=128):
    K, M = wT.shape
    kc = K // 128
    nb = M // mblk
    return np.ascontiguousarray(
        wT.reshape(kc, 128, nb, mblk).transpose(2, 1, 0, 3).reshape(nb, 128, kc * mblk))


def _chunkmajor(wT):
    K, M = wT.shape
    kc = K // 128
    return np.ascontiguousarray(wT.reshape(kc, 128, M).transpose(1, 0, 2).reshape(128, kc * M))


def _rope_tables(core):
    n = core * NL + np.arange(NL, dtype=np.float64)
    inv = 1.0 / (10000.0 ** (np.arange(0, 64, 2, dtype=np.float64) / 64.0))
    fr = n[:, None] * inv[None, :]
    emb = np.concatenate([fr, fr], axis=1)
    s = 64.0 ** -0.25
    cs = (np.cos(emb) * s).astype(np.float32).T
    sn = (np.sin(emb) * s).astype(np.float32).T
    return (np.ascontiguousarray(np.tile(cs, (2, NB))),
            np.ascontiguousarray(np.tile(sn, (2, NB))))


def _rotperm():
    m = np.zeros((128, 128), np.float32)
    for blk in (0, 64):
        for i in range(32):
            m[blk + 32 + i, blk + i] = -1.0
            m[blk + i, blk + 32 + i] = 1.0
    return m.astype(BF)


def _rollperm():
    m = np.zeros((128, 128), np.float32)
    for i in range(128):
        m[(i - 1) % 128, i] = 1.0
    return m.astype(BF)


def _pp(vec, n=NCH):
    return np.ascontiguousarray(np.asarray(vec).reshape(n, 128).T)


def host_prep(inputs):
    g = {}
    T_ = lambda a: np.ascontiguousarray(np.asarray(a).T)
    A_ = lambda a: np.ascontiguousarray(np.asarray(a))

    g["l0a_wqt"] = _chunkmajor(T_(inputs["L0_attn_Wq"]))
    g["l0a_wkt"] = _chunkmajor(T_(inputs["L0_attn_Wk"]))
    g["l0a_wvt"] = _chunkmajor(T_(inputs["L0_attn_Wv"]))
    g["l0a_wot"] = _chunkmajor(T_(inputs["L0_attn_Wout"]))
    g["l0a_bout"] = _pp(inputs["L0_attn_bout"])

    qkvT = T_(inputs["L0_ffn_qkv_w"])
    g["l0f_wqt"] = _chunkmajor(qkvT[:, 0:1024])
    g["l0f_wkt"] = _chunkmajor(qkvT[:, 1024:2048])
    g["l0f_wvt"] = _chunkmajor(qkvT[:, 2048:3072])
    qb = A_(inputs["L0_ffn_qkv_b"])
    g["l0f_qb"] = _pp(qb[0:1024])
    g["l0f_kb"] = _pp(qb[1024:2048])
    g["l0f_vb"] = qb[2048:3072].reshape(1, 1024).copy()
    g["l0f_wot"] = _chunkmajor(T_(inputs["L0_ffn_out_w"]))
    g["l0f_bout"] = _pp(inputs["L0_ffn_out_b"])
    g["l0f_n1w"] = _pp(inputs["L0_ffn_norm1_w"])
    g["l0f_n2w"] = _pp(inputs["L0_ffn_norm2_w"])

    w1 = np.asarray(inputs["L0_ffn_w1"]); w2 = np.asarray(inputs["L0_ffn_w2"])
    w3 = np.asarray(inputs["L0_ffn_w3"])
    g["l0f_w1b"] = np.stack([_blocks(T_(w1[e])) for e in range(8)])
    g["l0f_w2b"] = np.stack([_blocks(T_(w2[e])) for e in range(8)])
    g["l0f_w3b"] = np.stack([_blocks(T_(w3[e])) for e in range(8)])
    g["l0f_b1"] = np.stack([_pp(np.asarray(inputs["L0_ffn_b1"])[e], 16) for e in range(8)])
    g["l0f_b2"] = np.stack([_pp(np.asarray(inputs["L0_ffn_b2"])[e], 16) for e in range(8)])
    g["l0f_b3"] = A_(inputs["L0_ffn_b3"]).reshape(8, 1, 1024).copy()
    g["l0f_gatet"] = _chunkmajor(T_(inputs["L0_ffn_gate_w"]))

    g["l1a_wq"] = _chunkmajor(A_(inputs["L1_attn_Wq"]))
    g["l1a_wk"] = _chunkmajor(A_(inputs["L1_attn_Wk"]))
    g["l1a_wv"] = _chunkmajor(A_(inputs["L1_attn_Wv"]))
    g["l1a_wbm"] = _chunkmajor(A_(inputs["L1_attn_Wb"]))
    g["l1a_wo"] = _chunkmajor(A_(inputs["L1_attn_Wo"]))
    g["l1a_bo"] = _pp(inputs["L1_attn_bo"])

    g["l1f_wg"] = _chunkmajor(A_(inputs["L1_ffn_Wg"]))
    g["l1f_alpha"] = A_(inputs["L1_ffn_alpha"]).reshape(1, 8).copy()
    g["l1f_lnw"] = np.stack([_pp(np.asarray(inputs["L1_ffn_ln_w"])[e]) for e in range(8)])
    g["l1f_lnb"] = np.stack([_pp(np.asarray(inputs["L1_ffn_ln_b"])[e]) for e in range(8)])
    fc1 = np.asarray(inputs["L1_ffn_fc1_w"]); fc2 = np.asarray(inputs["L1_ffn_fc2_w"])
    g["l1f_fc1b"] = np.stack([_blocks(A_(fc1[e])) for e in range(8)])
    g["l1f_fc2b"] = np.stack([_blocks(A_(fc2[e])) for e in range(8)])
    g["l1f_b1"] = np.stack([_pp(np.asarray(inputs["L1_ffn_fc1_b"])[e], 32) for e in range(8)])
    g["l1f_b2"] = A_(inputs["L1_ffn_fc2_b"]).reshape(8, 1, 1024).copy()

    g["rotperm"] = _rotperm()
    g["rollperm"] = _rollperm()
    g["iota8"] = np.tile(np.arange(8, dtype=np.float32), (128, 1)).copy()

    x = np.asarray(inputs["L0_attn_x"])
    percore = []
    for c in range(NCORES):
        d = {"x_in": np.ascontiguousarray(x[:, c * NL:(c + 1) * NL, :])}
        cs, sn = _rope_tables(c)
        d["ropecs"] = cs
        d["ropesn"] = sn
        percore.append(d)
    return g, percore



class Ph:
    ENGS = ("t", "v", "s", "g", "y")

    def __init__(self, P):
        self.P = P
        self.nc = P.nc
        self.ops = {e: [] for e in self.ENGS}

    def wait(self, eng, sem, val):
        if val <= 0:
            return
        h = self.P.sems[sem]
        self.ops[eng].append(lambda e, h=h, v=val: e.wait_ge(h, v))

    def raw(self, eng, fn):
        self.ops[eng].append(fn)

    def do(self, eng, fn, inc="auto", n=1):
        if inc == "auto":
            inc = ESEM.get(eng)
        if inc is None:
            self.ops[eng].append(fn)
            return None
        h = self.P.sems[inc]
        self.ops[eng].append(lambda e, fn=fn, h=h, n=n: fn(e).then_inc(h, n))
        self.P.cnt[inc] += n
        return self.P.cnt[inc]

    def flush(self):
        with self.nc.Block() as block:
            for eng, sect in (("y", block.sync), ("g", block.gpsimd),
                              ("t", block.tensor), ("s", block.scalar),
                              ("v", block.vector)):
                ops = self.ops[eng]
                if ops:
                    sect(lambda e, ops=ops: [f(e) for f in ops])
        self.ops = {e: [] for e in self.ENGS}


class Seq:
    def __init__(self, ph, deps=()):
        self.ph = ph
        self.deps = list(deps)

    def do(self, eng, fn, inc="auto", n=1):
        for sem, cntv in self.deps:
            self.ph.wait(eng, sem, cntv)
        c = self.ph.do(eng, fn, inc=inc, n=n)
        if inc == "auto":
            inc = ESEM.get(eng)
        if inc is not None:
            self.deps = [(inc, c)]
        return c

    def also(self, eng, fn, inc="auto", n=1):
        c = self.ph.do(eng, fn, inc=inc, n=n)
        if inc == "auto":
            inc = ESEM.get(eng)
        if inc is not None:
            self.deps.append((inc, c))
        return c


class Prog:
    def __init__(self, nc):
        self.nc = nc
        self.cnt = {}
        self.sems = {}
        for name in ("wl", "al", "st", "pe", "ve", "se", "ge", "cc"):
            self.sems[name] = nc.alloc_semaphore(name)
            self.cnt[name] = 0
        self.dram = {}
        self.sb = {}
        self.innames = []

    def phase(self):
        return Ph(self)

    def din(self, name, shape, dtype=F32):
        if name not in self.dram:
            self.dram[name] = self.nc.dram_tensor(name, list(shape), dtype, kind="ExternalInput")
            self.innames.append(name)
        return self.dram[name]

    def dout(self, name, shape, dtype=F32):
        self.dram[name] = self.nc.dram_tensor(name, list(shape), dtype, kind="ExternalOutput")
        return self.dram[name]

    def dint(self, name, shape, dtype=F32):
        if name not in self.dram:
            self.dram[name] = self.nc.dram_tensor(name, list(shape), dtype, kind="Internal")
        return self.dram[name]

    def sbuf(self, name, shape, dtype=F32):
        if name not in self.sb:
            self.sb[name] = self.nc.alloc_sbuf_tensor(name, list(shape), dtype)
        return self.sb[name]



def emit_consts(P, ph):
    idn = P.sbuf("idn", [128, 128], F32)
    ones_col = P.sbuf("ones_col", [128, 1], F32)
    ones_row = P.sbuf("ones_row", [1, 128], F32)
    ones_col_bf = P.sbuf("ones_col_bf", [128, 1], BF16)
    ones_row_bf = P.sbuf("ones_row_bf", [1, 128], BF16)
    rotp = P.sbuf("rotp", [128, 128], BF16)
    rollp = P.sbuf("rollp", [128, 128], BF16)
    iota8v = P.sbuf("iota8v", [128, 8], F32)

    ph.do("g", lambda e: e.memset(ones_col[:, :], 1.0))
    ph.do("g", lambda e: e.memset(ones_row[:, :], 1.0))
    ph.do("g", lambda e: e.memset(ones_col_bf[:, :], 1.0))
    ph.do("g", lambda e: e.memset(ones_row_bf[:, :], 1.0))
    ph.do("g", lambda e: e.memset(idn[:, :], 0.0))
    ph.do("g", lambda e: e.affine_select(
        out=idn[:, :], in_=idn[:, :], compare_op=ALU.not_equal, fill=1.0,
        base=0, pattern=[[-1, 128]], channel_multiplier=1))
    for nm, dst in (("rotperm", rotp), ("rollperm", rollp), ("iota8", iota8v)):
        dt = BF16 if nm in ("rotperm", "rollperm") else F32
        d = P.din(nm, list(dst.shape), dt)
        ph.do("y", lambda e, d=d, dst=dst: e.dma_start(dst[:, :], d[:, :]), inc="al", n=16)


def emit_input_convert(P, ph):
    nc = P.nc
    x_fm = P.sbuf("x_fm", [128, NCH * T], F32)
    x_in = P.din("x_in", [NB, NL, D], F32)
    idn = P.sb["idn"]
    with nc.sbuf_tensor("xtok_i", [128, 16 * D], F32) as xtok, \
         nc.psum_tensor("ptr_i", [128, 4 * 512], F32) as ptr:
        ap_in = x_in.ap().rearrange("b (j p) d -> p (b j) d", p=128)
        ap_out = xtok.ap().rearrange("p (j d) -> p j d", j=16)
        c_al = ph.do("y", lambda e: e.dma_start(ap_out, ap_in), inc="al", n=16)
        base_pe = P.cnt["pe"]; base_ve = P.cnt["ve"]
        i = 0
        for j in range(16):
            for dc in range(NCH):
                bank = i % 4
                ph.wait("t", "al", c_al)
                if i >= 4:
                    ph.wait("t", "ve", base_ve + i - 3)
                ph.do("t", lambda e, j=j, dc=dc, bank=bank: e.transpose(
                    ptr[:, bank * 512:bank * 512 + 128],
                    xtok[:, j * D + dc * 128: j * D + (dc + 1) * 128],
                    idn[:, :]))
                ph.wait("v", "pe", base_pe + i + 1)
                ph.do("v", lambda e, j=j, dc=dc, bank=bank: e.tensor_copy(
                    x_fm[:, dc * T + j * 128: dc * T + (j + 1) * 128],
                    ptr[:, bank * 512:bank * 512 + 128]))
                i += 1
        ph.flush()


def emit_output_convert(P, ph):
    nc = P.nc
    x_fm = P.sb["x_fm"]
    x_out = P.dout("x_out", [NB, NL, D], F32)
    idn = P.sb["idn"]
    with nc.sbuf_tensor("xtok_o", [128, 16 * D], F32) as xtok, \
         nc.psum_tensor("ptr_o", [128, 4 * 512], F32) as ptr:
        base_pe = P.cnt["pe"]; base_ve = P.cnt["ve"]
        i = 0
        for j in range(16):
            for dc in range(NCH):
                bank = i % 4
                if i >= 4:
                    ph.wait("t", "ve", base_ve + i - 3)
                ph.do("t", lambda e, j=j, dc=dc, bank=bank: e.transpose(
                    ptr[:, bank * 512:bank * 512 + 128],
                    x_fm[:, dc * T + j * 128: dc * T + (j + 1) * 128],
                    idn[:, :]))
                ph.wait("v", "pe", base_pe + i + 1)
                ph.do("v", lambda e, j=j, dc=dc, bank=bank: e.tensor_copy(
                    xtok[:, j * D + dc * 128: j * D + (dc + 1) * 128],
                    ptr[:, bank * 512:bank * 512 + 128]))
                i += 1
        c_ve = P.cnt["ve"]
        ph.wait("y", "ve", c_ve)
        ap_out = x_out.ap().rearrange("b (j p) d -> p (b j) d", p=128)
        ap_in = xtok.ap().rearrange("p (j d) -> p j d", j=16)
        c_st = ph.do("y", lambda e: e.dma_start(ap_out, ap_in), inc="st", n=16)
        ph.wait("y", "st", c_st)
        ph.flush()


def emit_fm_load(P, ph):
    x_fm = P.sbuf("x_fm", [128, NCH * T], F32)
    d = P.din("xfm_in", [128, NCH * T], F32)
    c = ph.do("y", lambda e: e.dma_start(x_fm[:, :], d[:, :]), inc="al", n=16)
    ph.wait("y", "al", c)
    ph.flush()


def emit_fm_store(P, ph):
    x_fm = P.sb["x_fm"]
    d = P.dout("xfm_out", [128, NCH * T], F32)
    c = ph.do("y", lambda e: e.dma_start(d[:, :], x_fm[:, :]), inc="st", n=16)
    ph.wait("y", "st", c)
    ph.flush()



def emit_norm_fm(P, ph, srct, dstt, kind, eps, w_sb=None, uid="", tts=None,
                 dst_stride=T, dst_base=0, src_stride=T):
    nc = P.nc
    assert not (kind == "ln" and w_sb is not None)
    if tts is None:
        tts = list(range(NTT))
    ones_src = P.sb["ones_col"] if srct.dtype == F32 else P.sb["ones_col_bf"]
    ones_col = P.sb["ones_col"]
    ones_row = P.sb["ones_row"]
    src = lambda cc, t0, tw: srct[:, cc * src_stride + t0: cc * src_stride + t0 + tw]
    dst = lambda cc, t0, tw: dstt[:, dst_base + cc * dst_stride + t0 - tts[0] * TTW:
                                  dst_base + cc * dst_stride + t0 - tts[0] * TTW + tw]
    with nc.sbuf_tensor(f"nsq{uid}", [128, NCH * TTW // 4], F32) as sq, \
         nc.sbuf_tensor(f"nrw{uid}", [1, 3 * TTW], F32) as rows, \
         nc.psum_tensor(f"nmps{uid}", [1, 512], F32) as ps_sum, \
         nc.psum_tensor(f"nmpq{uid}", [1, 512], F32) as ps_sq, \
         nc.psum_tensor(f"nmpa{uid}", [128, 512], F32) as rep_a, \
         nc.psum_tensor(f"nmpb{uid}", [128, 512], F32) as rep_b:
        sq3 = sq.ap().rearrange("p (c w) -> p c w", c=NCH // 4)
        prev_ve = P.cnt["ve"]
        for tt in tts:
            t0 = tt * TTW
            src3 = srct.ap().rearrange("p (c t) -> p c t", c=NCH)[:, :, t0:t0 + TTW]
            ph.wait("s", "ve", prev_ve)
            if kind == "ln":
                ph.wait("t", "ve", prev_ve)
                for cc in range(NCH):
                    ph.do("t", lambda e, cc=cc, t0=t0: e.matmul(
                        ps_sum[:, :], ones_src[:, :], src(cc, t0, TTW),
                        start=(cc == 0), stop=(cc == NCH - 1)),
                        inc=(None if cc < NCH - 1 else "pe"))
            for half in range(4):
                if half >= 1:
                    ph.wait("s", "pe", P.cnt["pe"])
                c_sq = ph.do("s", lambda e, src3=src3, half=half: e.activation(
                    sq3, src3[:, half * (NCH // 4):(half + 1) * (NCH // 4), :],
                    AF.Square))
                ph.wait("t", "se", c_sq)
                for cc4 in range(NCH // 4):
                    cc = half * (NCH // 4) + cc4
                    ph.do("t", lambda e, cc4=cc4, cc=cc: e.matmul(
                        ps_sq[:, :], ones_col[:, :], sq[:, cc4 * TTW:(cc4 + 1) * TTW],
                        start=(cc == 0), stop=(cc == NCH - 1)),
                        inc=(None if cc < NCH - 1 else "pe"))
            c_pe = P.cnt["pe"]
            mu = rows[:, 0:TTW]; var = rows[:, TTW:2 * TTW]
            a = rows[:, 2 * TTW:3 * TTW]; b = rows[:, 0:TTW]
            s_ = Seq(ph, deps=[("pe", c_pe)])
            if kind == "ln":
                s_.do("v", lambda e: e.tensor_scalar(mu, ps_sum[0:1, :], 1.0 / D, None, ALU.mult))
                s_.do("v", lambda e: e.tensor_scalar(var, ps_sq[0:1, :], 1.0 / D, None, ALU.mult))
                s_.do("s", lambda e: e.activation(a, mu, AF.Square))
                s_.do("v", lambda e: e.tensor_sub(var, var, a))
            else:
                s_.do("v", lambda e: e.tensor_scalar(var, ps_sq[0:1, :], 1.0 / D, None, ALU.mult))
            s_.do("v", lambda e: e.tensor_scalar(var, var, float(eps), None, ALU.add))
            s_.do("s", lambda e: e.activation(var, var, AF.Sqrt))
            s_.do("v", lambda e: e.reciprocal(a, var))
            if kind == "ln":
                s_.do("v", lambda e: e.scalar_tensor_tensor(b, mu, -1.0, a, ALU.mult, ALU.mult))
            s_.do("t", lambda e: e.matmul(rep_a[:, :], ones_row[:, :], a, start=True, stop=True))
            if kind == "ln":
                s_.do("t", lambda e: e.matmul(rep_b[:, :], ones_row[:, :], b, start=True, stop=True))
            c_rep = s_.deps[0][1]
            ph.wait("v", "pe", c_rep)
            for cc in range(NCH):
                d_ap = dst(cc, t0, TTW)
                if kind == "ln":
                    ph.do("v", lambda e, cc=cc, t0=t0: e.tensor_mul(
                        sq[:, (cc % 2) * TTW:(cc % 2 + 1) * TTW], src(cc, t0, TTW),
                        rep_a[:, :]))
                    ph.do("v", lambda e, cc=cc, d_ap=d_ap: e.tensor_add(
                        d_ap, sq[:, (cc % 2) * TTW:(cc % 2 + 1) * TTW], rep_b[:, :]))
                else:
                    if w_sb is None:
                        ph.do("v", lambda e, cc=cc, t0=t0, d_ap=d_ap: e.tensor_mul(
                            d_ap, src(cc, t0, TTW), rep_a[:, :]))
                    else:
                        ph.do("v", lambda e, cc=cc, t0=t0, d_ap=d_ap: e.scalar_tensor_tensor(
                            d_ap, src(cc, t0, TTW), w_sb[:, cc:cc + 1], rep_a[:, :],
                            ALU.mult, ALU.mult))
            prev_ve = P.cnt["ve"]
    ph.flush()



def emit_proj_fm(P, ph, wsb, c_w, zt, outt, prj_ps, scale=1.0, bias_sb=None,
                 deps=(), hcs=None, outofs=None):
    if hcs is None:
        hcs = list(range(NCH))
    if outofs is None:
        outofs = lambda hc: hc * T
    base_se = P.cnt["se"]
    i = 0
    for hc in hcs:
        oofs = outofs(hc)
        for tt in range(NTT):
            t0 = tt * TTW
            bank = i % 2
            if i == 0:
                ph.wait("t", "wl", c_w)
                for sem, v in deps:
                    ph.wait("t", sem, v)
            if i >= 2:
                ph.wait("t", "se", base_se + i - 1)
            for cc in range(NCH):
                ph.do("t", lambda e, cc=cc, hc=hc, t0=t0, bank=bank: e.matmul(
                    prj_ps[:, bank * 512:(bank + 1) * 512],
                    wsb[:, cc * D + hc * 128: cc * D + (hc + 1) * 128],
                    zt[:, cc * T + t0: cc * T + t0 + TTW],
                    start=(cc == 0), stop=(cc == NCH - 1)),
                    inc=(None if cc < NCH - 1 else "pe"))
            c_pe = P.cnt["pe"]
            ph.wait("s", "pe", c_pe)
            bb = None if bias_sb is None else bias_sb[:, hc:hc + 1]
            def ev(e, oofs=oofs, t0=t0, bank=bank, bb=bb):
                kw = dict(scale=float(scale))
                if bb is not None:
                    kw["bias"] = bb
                return e.activation(outt[:, oofs + t0: oofs + t0 + TTW],
                                    prj_ps[:, bank * 512:(bank + 1) * 512],
                                    AF.Identity, **kw)
            ph.do("s", ev)
            i += 1
    return P.cnt["se"]



def emit_rope(P, ph, buf, rp_ps, tmp, c_ready_se, cs0, sn0):
    c = P.cnt["ve"]
    for hc in range(NCH):
        c = _emit_rope_tiles(
            P, ph, lambda tt, hc=hc: buf[:, hc * T + tt * TTW: hc * T + (tt + 1) * TTW],
            rp_ps, tmp, c_ready_se, cs0, sn0)
    return c



def _emit_rope_tiles(P, ph, get_sl, rp_ps, rtmp, c_ready_se, cs0, sn0):
    rotp = P.sb["rotp"]
    base_ve = P.cnt["ve"]
    for tt in range(NTT):
        t0 = tt * TTW
        bank = tt % 2
        sl = get_sl(tt)
        if tt == 0:
            ph.wait("t", "se", c_ready_se)
            ph.wait("t", "ve", base_ve)
        if tt >= 2:
            ph.wait("t", "ve", base_ve + (tt - 2) * 3 + 3)
        ph.do("t", lambda e, sl=sl, bank=bank: e.matmul(
            rp_ps[:, bank * 512:(bank + 1) * 512], rotp[:, :], sl,
            start=True, stop=True))
        c_pe = P.cnt["pe"]
        ph.wait("v", "pe", c_pe)
        ph.do("v", lambda e, bank=bank, t0=t0: e.tensor_mul(
            rtmp[:, 0:TTW], rp_ps[:, bank * 512:(bank + 1) * 512], sn0[:, t0:t0 + TTW]))
        ph.do("v", lambda e, sl=sl, t0=t0: e.tensor_mul(
            rtmp[:, TTW:2 * TTW], sl, cs0[:, t0:t0 + TTW]))
        ph.do("v", lambda e, sl=sl: e.tensor_add(sl, rtmp[:, TTW:2 * TTW], rtmp[:, 0:TTW]))
    return P.cnt["ve"]


def _emit_m2(P, ph, get_chunk, hcs, row_out, nps, sqbuf, dep_of=None):
    ones_col = P.sb["ones_col"]
    red = {}
    k = 0
    for hc in hcs:
        ch = get_chunk(hc)
        deps = [("pe", P.cnt["pe"])]
        if dep_of is not None:
            deps.append(dep_of(hc))
        for sem, v in deps:
            ph.wait("v", sem, v)
        c_sq = ph.do("v", lambda e, ch=ch: e.tensor_mul(sqbuf[:, :], ch, ch))
        ph.wait("t", "ve", c_sq)
        for j in range(2):
            hb = j * 64
            for b in range(NB):
                idx = (hc * 2 + j) * 4 + b
                bank = k % 2
                if k >= 2:
                    ph.wait("t", "ve", red[k - 2])
                c_mm = ph.do("t", lambda e, hb=hb, b=b, bank=bank: e.matmul(
                    nps[0:1, bank * 512:(bank + 1) * 512],
                    ones_col[hb:hb + 64, :], sqbuf[hb:hb + 64, b * 512:(b + 1) * 512],
                    start=True, stop=True))
                ph.wait("v", "pe", c_mm)
                red[k] = ph.do("v", lambda e, idx=idx, bank=bank: e.reduce_max(
                    row_out[0:1, idx:idx + 1], nps[0:1, bank * 512:(bank + 1) * 512],
                    axis=AXX))
                k += 1
    return P.cnt["ve"]



def emit_attn(P, ph, cfg):
    from contextlib import ExitStack
    nc = P.nc
    uid = cfg["uid"]
    x_fm = P.sb["x_fm"]
    ones_col = P.sb["ones_col"]; ones_row = P.sb["ones_row"]
    ones_col_bf = P.sb["ones_col_bf"]; ones_row_bf = P.sb["ones_row_bf"]; ones_row_bf = P.sb["ones_row_bf"]
    ginK = P.dint("ginK", [D, T], BF16)
    goutK = P.dint("goutK", [NCORES * D, T], BF16)
    ginV = P.dint("ginV", [T, D], BF16)
    goutV = P.dint("goutV", [NCORES * T, D], BF16)
    gin_s = P.dint("gin_s", [1, 64], F32)
    gout_s = P.dint("gout_s", [1, 64], F32)

    dwq = P.din(cfg["wq"], [128, NCH * D], F32)
    dwk = P.din(cfg["wk"], [128, NCH * D], F32)
    dwv = P.din(cfg["wv"], [128, NCH * D], F32)
    dwo = P.din(cfg["wo"], [128, NCH * D], F32)
    dbout = P.din(cfg["bout"], [128, NCH], F32)

    with ExitStack() as outer:
        qbuf = outer.enter_context(nc.sbuf_tensor(f"q{uid}", [128, NCH * T], BF16))
        rowbuf = outer.enter_context(nc.sbuf_tensor(f"rw{uid}", [1, 256], F32))
        negc_rep = outer.enter_context(nc.sbuf_tensor(f"ncr{uid}", [128, 64], F32))
        bout_sb = outer.enter_context(nc.sbuf_tensor(f"bo{uid}", [128, NCH], F32))
        qm2 = rowbuf[0:1, 0:64]
        km2 = rowbuf[0:1, 64:128]
        km2g = rowbuf[0:1, 128:192]
        t1 = rowbuf[0:1, 192:256]
        ph.do("y", lambda e: e.dma_start(bout_sb[:, :], dbout[:, :]), inc="al", n=16)

        with ExitStack() as stA:
            zbuf = stA.enter_context(nc.sbuf_tensor(f"z{uid}", [128, NCH * T], BF16))
            wsb0 = stA.enter_context(nc.sbuf_tensor(f"w0{uid}", [128, NCH * D], BF16))
            wsb1 = stA.enter_context(nc.sbuf_tensor(f"w1{uid}", [128, NCH * D], BF16))
            gain_sb = bias_q = bias_k = vb_sb = None
            if cfg.get("gain"):
                gain_d = P.din(cfg["gain"], [128, NCH], F32)
                gain_sb = stA.enter_context(nc.sbuf_tensor(f"gn{uid}", [128, NCH], F32))
                ph.do("y", lambda e: e.dma_start(gain_sb[:, :], gain_d[:, :]), inc="al", n=16)
            if cfg.get("qb"):
                dqb = P.din(cfg["qb"], [128, NCH], F32)
                dkb = P.din(cfg["kb"], [128, NCH], F32)
                bias_q = stA.enter_context(nc.sbuf_tensor(f"qb{uid}", [128, NCH], F32))
                bias_k = stA.enter_context(nc.sbuf_tensor(f"kb{uid}", [128, NCH], F32))
                ph.do("y", lambda e: e.dma_start(bias_q[:, :], dqb[:, :]), inc="al", n=16)
                ph.do("y", lambda e: e.dma_start(bias_k[:, :], dkb[:, :]), inc="al", n=16)
            if cfg.get("vb"):
                dvb = P.din(cfg["vb"], [1, D], F32)
                vb_sb = stA.enter_context(nc.sbuf_tensor(f"vb{uid}", [1, D], BF16))
                ph.do("g", lambda e: e.dma_start(vb_sb[:, :], dvb[:, :]), inc="al", n=16)
            cs0 = sn0 = None
            if cfg["rope"]:
                dcs = P.din("ropecs", [128, T], F32)
                dsn = P.din("ropesn", [128, T], F32)
                cs0 = stA.enter_context(nc.sbuf_tensor(f"cs{uid}", [128, T], F32))
                sn0 = stA.enter_context(nc.sbuf_tensor(f"sn{uid}", [128, T], F32))
                ph.do("y", lambda e: e.dma_start(cs0[:, :], dcs[:, :]), inc="al", n=16)
                ph.do("y", lambda e: e.dma_start(sn0[:, :], dsn[:, :]), inc="al", n=16)
            c_misc = P.cnt["al"]
            scale = 1.0 if cfg["rope"] else 64.0 ** -0.25

            ph.wait("v", "al", c_misc)
            ph.wait("s", "al", c_misc)
            emit_norm_fm(P, ph, x_fm, zbuf, cfg["kind"], cfg["eps"], w_sb=gain_sb, uid=uid)
            c_z = P.cnt["ve"]

            with nc.psum_tensor(f"prj{uid}", [128, 1024], F32) as prj_ps, \
                 nc.psum_tensor(f"rp{uid}", [128, 1024], F32) as rp_ps, \
                 nc.psum_tensor(f"nps{uid}", [1, 1024], F32) as nps, \
                 nc.psum_tensor(f"sps{uid}", [128, 64], F32) as sps, \
                 nc.sbuf_tensor(f"kch{uid}", [128, 2 * T], BF16) as kch, \
                 nc.sbuf_tensor(f"ksq{uid}", [128, T], F32) as ksq, \
                 nc.sbuf_tensor(f"rt{uid}", [128, 2 * TTW], F32) as rtmp, \
                 nc.sbuf_tensor(f"vst{uid}", [128, 2 * D], BF16) as vstage:

                c_wk = ph.do("g", lambda e: e.dma_start(
                    wsb1.ap().rearrange("p (c m) -> p c m", c=NCH),
                    dwk.ap().rearrange("p (c m) -> p c m", c=NCH)), inc="wl", n=16)
                c_wv = ph.do("g", lambda e: e.dma_start(
                    wsb0.ap().rearrange("p (c m) -> p c m", c=NCH),
                    dwv.ap().rearrange("p (c m) -> p c m", c=NCH)), inc="wl", n=16)

                kdma = {}
                kv_done = {}
                for hc in range(NCH):
                    half = hc % 2
                    buf = kch[:, half * T:(half + 1) * T]
                    deps = [("ve", c_z)]
                    if hc >= 2:
                        deps = [("st", kdma[hc - 2]), ("ve", kv_done[hc - 2])]
                    c_se = emit_proj_fm(P, ph, wsb1, c_wk, zbuf, buf, prj_ps,
                                        scale=scale, bias_sb=bias_k, deps=deps,
                                        hcs=[hc], outofs=lambda _hc: 0)
                    if cfg["rope"]:
                        c_rdy = _emit_rope_tiles(
                            P, ph, lambda tt, buf=buf: buf[:, tt * TTW:(tt + 1) * TTW],
                            rp_ps, rtmp, c_se, cs0, sn0)
                        kdep = ("ve", c_rdy)
                    else:
                        kdep = ("se", c_se)
                    c_m2 = _emit_m2(P, ph, lambda _hc, buf=buf: buf, [hc], km2,
                                    nps, ksq, dep_of=lambda _hc, kdep=kdep: kdep)
                    kv_done[hc] = c_m2
                    ph.wait("y", kdep[0], kdep[1])
                    kdma[hc] = ph.do("y", lambda e, hc=hc, buf=buf: e.dma_start(
                        ginK.ap()[hc * 128:(hc + 1) * 128, :], buf), inc="st", n=16)
                c_kcontrib = P.cnt["st"]
                c_kpe = P.cnt["pe"]

                c_km = P.cnt["ve"]
                ph.wait("y", "ve", c_km)
                c_kms = ph.do("y", lambda e: e.dma_start(gin_s[:, :], km2), inc="st", n=16)
                ph.wait("g", "st", c_kms)
                c_cc1 = ph.do("g", lambda e: e.collective_compute(
                    "AllReduce", ALU.max, replica_groups=[list(range(NCORES))],
                    ins=[gin_s.ap().opt()], outs=[gout_s.ap().opt()]), inc="cc")
                ph.wait("g", "st", c_kcontrib)
                c_cc2 = ph.do("g", lambda e: e.collective_compute(
                    "AllGather", ALU.bypass, replica_groups=[list(range(NCORES))],
                    ins=[ginK.ap().opt()], outs=[goutK.ap().opt()]), inc="cc")

                vdma = {}
                base_vse = P.cnt["se"]
                for tk in range(16):
                    for hh in range(2):
                        i = tk * 2 + hh
                        bank = i % 2
                        if i == 0:
                            ph.wait("t", "wl", c_wv)
                        if i >= 2:
                            ph.wait("t", "se", base_vse + i - 1)
                        if tk >= 2 and hh == 0:
                            ph.wait("t", "st", vdma[tk - 2])
                        for cc in range(NCH):
                            last = (cc == NCH - 1 and vb_sb is None)
                            ph.do("t", lambda e, cc=cc, tk=tk, hh=hh, bank=bank, last=last:
                                  e.matmul(prj_ps[:, bank * 512:(bank + 1) * 512],
                                           zbuf[:, cc * T + tk * 128: cc * T + (tk + 1) * 128],
                                           wsb0[:, cc * D + hh * 512: cc * D + hh * 512 + 512],
                                           start=(cc == 0), stop=last),
                                  inc=(None if not last else "pe"))
                        if vb_sb is not None:
                            ph.do("t", lambda e, hh=hh, bank=bank: e.matmul(
                                prj_ps[:, bank * 512:(bank + 1) * 512],
                                ones_row_bf[:, :],
                                vb_sb[0:1, hh * 512:hh * 512 + 512],
                                start=False, stop=True, skip_group_check=True), inc="pe")
                        c_pe = P.cnt["pe"]
                        ph.wait("s", "pe", c_pe)
                        ph.do("s", lambda e, hh=hh, bank=bank, tk=tk: e.activation(
                            vstage[:, (tk % 2) * D + hh * 512:(tk % 2) * D + hh * 512 + 512],
                            prj_ps[:, bank * 512:(bank + 1) * 512], AF.Identity))
                        if hh == 1:
                            c_se = P.cnt["se"]
                            ph.wait("y", "se", c_se)
                            vdma[tk] = ph.do("y", lambda e, tk=tk: e.dma_start(
                                ginV.ap()[tk * 128:(tk + 1) * 128, :],
                                vstage[:, (tk % 2) * D:(tk % 2) * D + D]), inc="st", n=16)
                c_vcontrib = P.cnt["st"]
                ph.wait("g", "st", c_vcontrib)
                c_cc3 = ph.do("g", lambda e: e.collective_compute(
                    "AllGather", ALU.bypass, replica_groups=[list(range(NCORES))],
                    ins=[ginV.ap().opt()], outs=[goutV.ap().opt()]), inc="cc")

                ph.wait("g", "pe", c_kpe)
                c_wq = ph.do("g", lambda e: e.dma_start(
                    wsb1.ap().rearrange("p (c m) -> p c m", c=NCH),
                    dwq.ap().rearrange("p (c m) -> p c m", c=NCH)), inc="wl", n=16)
                c_qse = emit_proj_fm(P, ph, wsb1, c_wq, zbuf, qbuf, prj_ps,
                                     scale=scale, bias_sb=bias_q,
                                     deps=[("ve", c_z), ("se", P.cnt["se"])])
                if cfg["rope"]:
                    emit_rope(P, ph, qbuf, rp_ps, rtmp, c_qse, cs0, sn0)
                    qdep = None
                else:
                    qdep = lambda _hc: ("se", c_qse)
                _emit_m2(P, ph, lambda hc: qbuf[:, hc * T:(hc + 1) * T],
                         range(NCH), qm2, nps, ksq, dep_of=qdep)

                ph.wait("y", "cc", c_cc1)
                c_al = ph.do("y", lambda e: e.dma_start(km2g, gout_s[:, :]), inc="al", n=16)
                sh = Seq(ph, deps=[("al", c_al)])
                sh.do("v", lambda e: e.tensor_mul(t1, qm2, km2g))
                sh.do("s", lambda e: e.activation(t1, t1, AF.Sqrt))
                sh.do("v", lambda e: e.tensor_scalar(t1, t1, -1.0, None, ALU.mult))
                sh.do("t", lambda e: e.matmul(sps[:, :], ones_row[:, :], t1,
                                              start=True, stop=True))
                sh.do("v", lambda e: e.tensor_copy(negc_rep[:, :], sps[:, :]))
                ph.flush()
                if cfg.get("dump_a"):
                    dz = P.dout("dump_z", [128, NCH * T], BF16)
                    dq = P.dout("dump_q", [128, NCH * T], BF16)
                    dnc = P.dout("dump_negc", [128, 64], F32)
                    dgk = P.dout("dump_ginK", [D, T], BF16)
                    dgv = P.dout("dump_ginV", [T, D], BF16)
                    c = ph.do("y", lambda e: e.dma_start(dz[:, :], zbuf[:, :]), inc="st", n=16)
                    c = ph.do("y", lambda e: e.dma_start(dq[:, :], qbuf[:, :]), inc="st", n=16)
                    c = ph.do("y", lambda e: e.dma_start(dnc[:, :], negc_rep[:, :]), inc="st", n=16)
                    c = ph.do("y", lambda e: e.dma_start(dgk[:, :], ginK[:, :]), inc="st", n=16)
                    c = ph.do("y", lambda e: e.dma_start(dgv[:, :], ginV[:, :]), inc="st", n=16)
                    ph.wait("y", "st", c)
                    ph.flush()
                    return

        with nc.sbuf_tensor(f"ks0{uid}", [128, 2 * 4096], BF16) as ksb0, \
             nc.sbuf_tensor(f"ks1{uid}", [128, 2 * 4096], BF16) as ksb1, \
             nc.sbuf_tensor(f"vs0{uid}", [128, 32 * 256], BF16) as vsb0, \
             nc.sbuf_tensor(f"vs1{uid}", [128, 32 * 256], BF16) as vsb1, \
             nc.sbuf_tensor(f"pb{uid}", [128, 2 * 1024], BF16) as psb, \
             nc.sbuf_tensor(f"abf{uid}", [128, NCH * 512], BF16) as abuf, \
             nc.sbuf_tensor(f"rr{uid}", [1, 512], F32) as rrow, \
             nc.sbuf_tensor(f"rs{uid}", [64, 512], F32) as rep_sb, \
             nc.sbuf_tensor(f"wo{uid}", [128, NCH * D], BF16) as wo_sb, \
             nc.psum_tensor(f"scp{uid}", [128, 2048], F32) as sc_ps, \
             nc.psum_tensor(f"avp{uid}", [128, 2048], F32) as av_ps:
            c_wo = ph.do("g", lambda e: e.dma_start(
                wo_sb.ap().rearrange("p (c m) -> p c m", c=NCH),
                dwo.ap().rearrange("p (c m) -> p c m", c=NCH)), inc="wl", n=16)
            ksbs = (ksb0, ksb1)
            vsbs = (vsb0, vsb1)
            c_al_iter = [0] * 16
            c_iter_done = [0] * 16

            def issue_loads(i):
                b, hg = divmod(i, 4)
                if i >= 2:
                    ph.wait("y", "pe", c_iter_done[i - 2])
                if i == 0:
                    ph.wait("y", "cc", c_cc2)
                kap = goutK.ap().rearrange("(c d) t -> d c t", c=NCORES)
                for pr in range(2):
                    h0 = hg * 4 + 2 * pr
                    src = kap[h0 * 64: h0 * 64 + 128, :, b * 512:(b + 1) * 512]
                    dst = ksbs[i % 2].ap().rearrange("p (r t) -> p r t", r=2)[
                        :, pr:pr + 1, :].rearrange(
                        "p o (c t) -> p (o c) t", c=NCORES)
                    ph.do("y", lambda e, src=src, dst=dst: e.dma_start(dst, src),
                          inc="al", n=16)
                if i == 0:
                    ph.wait("y", "cc", c_cc3)
                vap = goutV.ap().rearrange("(c t) h -> c t h", c=NCORES)
                for c in range(NCORES):
                    src = vap[c:c + 1, b * 512:(b + 1) * 512,
                              hg * 256:(hg + 1) * 256].rearrange(
                                  "o (tl p) h -> p (o tl) h", p=128)
                    dst = vsbs[i % 2].ap().rearrange("p (g h) -> p g h", g=32)[
                        :, c * 4:(c + 1) * 4, :]
                    ph.do("y", lambda e, src=src, dst=dst: e.dma_start(dst, src),
                          inc="al", n=16)
                c_al_iter[i] = P.cnt["al"]

            issue_loads(0)
            issue_loads(1)
            for i in range(16):
                b, hg = divmod(i, 4)
                ksb = ksbs[i % 2]
                vsb = vsbs[i % 2]
                c_prev_norm_ve = P.cnt["ve"]
                for j in range(4):
                    h = hg * 4 + j
                    idx = h * 4 + b
                    hb = (j % 2) * 64
                    pair = j // 2
                    qap = qbuf[(h % 2) * 64:(h % 2) * 64 + 64,
                               (h // 2) * T + b * 512:(h // 2) * T + b * 512 + 512]
                    if j == 0:
                        ph.wait("t", "al", c_al_iter[i])
                    ph.wait("t", "ve", P.cnt["ve"])
                    c_exp = {}
                    c_av = {}
                    for gg in range(16):
                        bank = gg % 2
                        if gg >= 2:
                            ph.wait("t", "se", c_exp[gg - 2])
                        for u in range(2):
                            g = 2 * gg + u
                            c_sc = ph.do("t", lambda e, g=g, bank=bank, u=u, pair=pair,
                                         hb=hb, qap=qap, ksb=ksb:
                                         e.matmul(sc_ps[:, (2 * bank + u) * 512:(2 * bank + u + 1) * 512],
                                                  ksb[hb:hb + 64,
                                                      pair * 4096 + g * 128: pair * 4096 + (g + 1) * 128],
                                                  qap, start=True, stop=True))
                        ph.wait("s", "pe", c_sc)
                        c_exp[gg] = ph.do("s", lambda e, bank=bank, idx=idx: e.activation(
                            psb[:, bank * 1024:(bank + 1) * 1024],
                            sc_ps[:, bank * 1024:(bank + 1) * 1024], AF.Exp,
                            bias=negc_rep[:, idx:idx + 1]))
                        ph.wait("t", "se", c_exp[gg])
                        for u in range(2):
                            g = 2 * gg + u
                            ph.do("t", lambda e, g=g, j=j, bank=bank, u=u, vsb=vsb: e.matmul(
                                av_ps[0:64, j * 512:(j + 1) * 512],
                                vsb[:, g * 256 + j * 64: g * 256 + j * 64 + 64],
                                psb[:, (2 * bank + u) * 512:(2 * bank + u + 1) * 512],
                                start=(g == 0), stop=False), inc=None)
                            c_av[gg] = ph.do("t", lambda e, g=g, j=j, bank=bank, u=u: e.matmul(
                                av_ps[64:65, j * 512:(j + 1) * 512],
                                ones_col_bf[:, :],
                                psb[:, (2 * bank + u) * 512:(2 * bank + u + 1) * 512],
                                start=(g == 0), stop=(g == 31), skip_group_check=True))
                    ph.wait("v", "pe", c_av[15])
                    c_rc = ph.do("v", lambda e, j=j: e.reciprocal(
                        rrow[:, :], av_ps[64:65, j * 512:(j + 1) * 512]))
                    ph.wait("t", "ve", c_rc)
                    c_rp = ph.do("t", lambda e: e.matmul(
                        sc_ps[0:64, 0:512], ones_row[0:1, 0:64], rrow[:, :],
                        start=True, stop=True))
                    ph.wait("v", "pe", c_rp)
                    ph.do("v", lambda e: e.tensor_copy(rep_sb[:, :], sc_ps[0:64, 0:512]))
                    hcd = h // 2
                    hbd = (h % 2) * 64
                    ph.do("v", lambda e, j=j, hcd=hcd, hbd=hbd: e.tensor_mul(
                        abuf[hbd:hbd + 64, hcd * 512:(hcd + 1) * 512],
                        av_ps[0:64, j * 512:(j + 1) * 512], rep_sb[:, :]))
                c_iter_done[i] = P.cnt["pe"]
                if i + 2 < 16:
                    issue_loads(i + 2)
                if hg == 3:
                    c_a = P.cnt["ve"]
                    c_evac = {}
                    for hc in range(NCH):
                        bank = hc % 2
                        if hc == 0:
                            ph.wait("t", "ve", c_a)
                            ph.wait("t", "wl", c_wo)
                        if hc >= 2:
                            ph.wait("t", "ve", c_evac[hc - 2])
                        for cc in range(NCH):
                            ph.do("t", lambda e, cc=cc, hc=hc, bank=bank: e.matmul(
                                sc_ps[:, bank * 512:(bank + 1) * 512],
                                wo_sb[:, cc * D + hc * 128: cc * D + (hc + 1) * 128],
                                abuf[:, cc * 512:(cc + 1) * 512],
                                start=(cc == 0), stop=(cc == NCH - 1)),
                                inc=(None if cc < NCH - 1 else "pe"))
                        c_pe = P.cnt["pe"]
                        ph.wait("v", "pe", c_pe)
                        c_evac[hc] = ph.do("v", lambda e, hc=hc, b=b, bank=bank:
                                           e.scalar_tensor_tensor(
                            x_fm[:, hc * T + b * 512: hc * T + b * 512 + 512],
                            sc_ps[:, bank * 512:(bank + 1) * 512],
                            bout_sb[:, hc:hc + 1],
                            x_fm[:, hc * T + b * 512: hc * T + b * 512 + 512],
                            ALU.add, ALU.add))
                    c_iter_done[i] = P.cnt["pe"]
            ph.flush()


CFG_L0A = dict(uid="A", kind="ln", eps=1e-5, gain=None, rope=False,
               wq="l0a_wqt", wk="l0a_wkt", wv="l0a_wvt", wo="l0a_wot",
               bout="l0a_bout", qb=None, kb=None, vb=None)
CFG_L0F_ATT = dict(uid="F", kind="rms", eps=1e-6, gain="l0f_n1w", rope=True,
                   wq="l0f_wqt", wk="l0f_wkt", wv="l0f_wvt", wo="l0f_wot",
                   bout="l0f_bout", qb="l0f_qb", kb="l0f_kb", vb="l0f_vb")


def emit_l0f_outer(P, ph):
    nc = P.nc
    x_fm = P.sb["x_fm"]
    xsp = P.dint("xspill", [128, NCH * T], F32)
    c = ph.do("y", lambda e: e.dma_start(xsp[:, :], x_fm[:, :]), inc="st", n=16)
    ph.wait("y", "st", c)
    ph.flush()
    emit_norm_fm(P, ph, x_fm, x_fm, "ln", 1e-5, uid="L0FO")
    ph.flush()


def emit_l0f_close(P, ph):
    nc = P.nc
    x_fm = P.sb["x_fm"]
    xsp = P.dram["xspill"]
    with nc.sbuf_tensor("xrel", [128, NCH * T], F32) as xr:
        c = ph.do("y", lambda e: e.dma_start(xr[:, :], xsp[:, :]), inc="al", n=16)
        ph.wait("v", "al", c)
        for cc in range(NCH):
            for tt in range(NTT):
                ph.do("v", lambda e, cc=cc, tt=tt: e.tensor_add(
                    x_fm[:, cc * T + tt * TTW: cc * T + (tt + 1) * TTW],
                    x_fm[:, cc * T + tt * TTW: cc * T + (tt + 1) * TTW],
                    xr[:, cc * T + tt * TTW: cc * T + (tt + 1) * TTW]))
        ph.flush()


def build_launch(phases):
    nc = bass.Bass("TRN2", target_bir_lowering=False)
    P = Prog(nc)
    ph = P.phase()
    emit_consts(P, ph)
    for p in phases:
        if p == "in":
            emit_input_convert(P, ph)
        elif p == "fmload":
            emit_fm_load(P, ph)
        elif p == "l0a":
            emit_attn(P, ph, CFG_L0A)
        elif p == "l0f_outer":
            emit_l0f_outer(P, ph)
        elif p == "l0f_att":
            emit_attn(P, ph, CFG_L0F_ATT)
        elif p == "l0f_moe":
            emit_moe_l0(P, ph)
        elif p == "l0f_close":
            emit_l0f_close(P, ph)
        elif p == "l1a":
            emit_l1_attn(P, ph)
        elif p == "l1f":
            emit_moe_l1(P, ph)
        elif p == "out":
            emit_output_convert(P, ph)
        elif p == "fmstore":
            emit_fm_store(P, ph)
        else:
            raise ValueError(p)
    return P


L0F = ["l0f_outer", "l0f_att", "l0f_moe", "l0f_close"]
PLAN = [["in", "l0a"] + L0F + ["l1a", "l1f", "out"]]


def kernel(**inputs):
    g, percore = host_prep(inputs)
    hand = None
    final = [None] * NCORES
    for phases in PLAN:
        P = build_launch(phases)
        innames = P.innames
        in_maps = []
        for c in range(NCORES):
            m = {}
            for n in innames:
                if n in percore[c]:
                    m[n] = percore[c][n]
                elif n == "xfm_in":
                    m[n] = hand[c]
                else:
                    m[n] = g[n]
            in_maps.append(m)
        res = run_bass_kernel_spmd(P.nc, in_maps, core_ids=list(range(NCORES)))
        r0 = res.results
        if "xfm_out" in r0[0]:
            hand = [r0[c]["xfm_out"] for c in range(NCORES)]
        if "x_out" in r0[0]:
            final = [r0[c]["x_out"] for c in range(NCORES)]
    return np.concatenate(final, axis=1)


def emit_moe_l0(P, ph):
    raise NotImplementedError


def emit_moe_l1(P, ph):
    raise NotImplementedError


def emit_l1_attn(P, ph):
    raise NotImplementedError


def emit_gating(P, ph, zget, gname, comb_fm, uid, tks, alpha_sb=None, tk0=0):
    nc = P.nc
    iota8v = P.sb["iota8v"]
    idn = P.sb["idn"]
    dg = P.din(gname, [128, NCH * 8], F32)
    with nc.sbuf_tensor(f"gw{uid}", [128, NCH * 8], BF16) as gsb, \
         nc.sbuf_tensor(f"gt{uid}", [128, 40], F32) as gt, \
         nc.sbuf_tensor(f"i100{uid}", [128, 8], F32) as i100, \
         nc.psum_tensor(f"gp{uid}", [128, 512], F32) as gps, \
         nc.psum_tensor(f"tp{uid}", [8, 1024], F32) as tps:
        c_g = ph.do("g", lambda e: e.dma_start(gsb[:, :], dg[:, :]), inc="wl", n=16)
        lg = gt[:, 0:8]; e1 = gt[:, 8:16]; t8 = gt[:, 16:24]
        m1 = gt[:, 24:25]; m2 = gt[:, 25:26]; p1 = gt[:, 26:27]; p2 = gt[:, 27:28]
        nm1 = gt[:, 28:29]; tx = gt[:, 29:30]; g1 = gt[:, 30:31]; g2 = gt[:, 31:32]
        mk1 = gt[:, 32:40]
        c_i = ph.do("v", lambda e: e.tensor_scalar(i100[:, :], iota8v[:, :], 100.0,
                                                   None, ALU.add))
        prev = {}
        for it, tk in enumerate(tks):
            deps = [("wl", c_g), ("ve", c_i)]
            if it >= 1:
                deps.append(("ve", prev[it - 1]))
            s_ = Seq(ph, deps=deps)
            for sem, v in deps:
                ph.wait("t", sem, v)
            for cc in range(NCH):
                ph.do("t", lambda e, cc=cc, tk=tk: e.matmul(
                    gps[:, 0:8], zget(cc, tk),
                    gsb[:, cc * 8:(cc + 1) * 8],
                    start=(cc == 0), stop=(cc == NCH - 1)),
                    inc=(None if cc < NCH - 1 else "pe"))
            s_.deps = [("pe", P.cnt["pe"])]
            s_.do("v", lambda e: e.tensor_copy(lg, gps[:, 0:8]))
            s_.do("v", lambda e: e.reduce_max(m1, lg, axis=AXX))
            s_.do("v", lambda e: e.tensor_scalar(e1, lg, m1, None, ALU.is_equal))
            s_.do("v", lambda e: e.scalar_tensor_tensor(t8, e1, -100.0, i100[:, :],
                                                        ALU.mult, ALU.add))
            s_.do("v", lambda e: e.tensor_reduce(p1, t8, AXX, ALU.min))
            s_.do("v", lambda e: e.tensor_scalar(mk1, iota8v[:, :], p1, None, ALU.is_equal))
            s_.do("v", lambda e: e.scalar_tensor_tensor(t8, mk1, -1e30, lg,
                                                        ALU.mult, ALU.add))
            s_.do("v", lambda e: e.reduce_max(m2, t8, axis=AXX))
            s_.do("v", lambda e: e.tensor_scalar(e1, t8, m2, None, ALU.is_equal))
            s_.do("v", lambda e: e.scalar_tensor_tensor(t8, e1, -100.0, i100[:, :],
                                                        ALU.mult, ALU.add))
            s_.do("v", lambda e: e.tensor_reduce(p2, t8, AXX, ALU.min))
            s_.do("v", lambda e: e.tensor_scalar(nm1, m1, -1.0, None, ALU.mult))
            s_.do("s", lambda e: e.activation(tx, m2, AF.Exp, bias=nm1))
            s_.do("v", lambda e: e.tensor_scalar(tx, tx, 1.0, None, ALU.add))
            s_.do("v", lambda e: e.reciprocal(g1, tx))
            s_.do("v", lambda e: e.tensor_scalar(g2, g1, -1.0, 1.0, ALU.mult, ALU.add))
            s_.do("v", lambda e: e.tensor_scalar(mk1, mk1, g1, None, ALU.mult))
            s_.do("v", lambda e: e.tensor_scalar(e1, iota8v[:, :], p2, None, ALU.is_equal))
            s_.do("v", lambda e: e.scalar_tensor_tensor(lg, e1, g2, mk1,
                                                        ALU.mult, ALU.add))
            if alpha_sb is not None:
                s_.do("v", lambda e: e.tensor_mul(lg, lg, alpha_sb[:, :]))
            bank = it % 2
            s_.do("t", lambda e, bank=bank: e.transpose(
                tps[:, bank * 512:bank * 512 + 128], lg, idn[:, :]))
            prev[it] = s_.do("v", lambda e, tk=tk, bank=bank: e.tensor_copy(
                comb_fm[0:8, (tk - tk0) * 128:(tk - tk0 + 1) * 128],
                tps[0:8, bank * 512:bank * 512 + 128]))
        ph.flush()
    return P.cnt["ve"]



def _moe_bufs(P, st, uid, NF, ntt, mode, blk1):
    nc = P.nc
    B = {}
    B["h"] = st.enter_context(nc.sbuf_tensor(f"h{uid}", [128, NF * ntt * TTW], BF16))
    B["w1"] = st.enter_context(nc.sbuf_tensor(f"w1{uid}", [128, 2 * blk1], BF16))
    B["w2"] = st.enter_context(nc.sbuf_tensor(f"w2{uid}", [128, 2 * blk1], BF16)) \
        if mode == "l0" else None
    B["w3"] = st.enter_context(nc.sbuf_tensor(f"w3{uid}", [128, 2 * NF * 128], BF16))
    B["b1"] = st.enter_context(nc.sbuf_tensor(f"b1{uid}", [128, NF], F32))
    B["b2"] = st.enter_context(nc.sbuf_tensor(f"b2{uid}", [128, NF], F32)) \
        if mode == "l0" else None
    B["b3"] = st.enter_context(nc.sbuf_tensor(f"b3{uid}", [1, D], BF16))
    B["cr"] = st.enter_context(nc.sbuf_tensor(f"cr{uid}", [1, ntt * TTW], BF16))
    B["cp"] = st.enter_context(nc.sbuf_tensor(f"cp{uid}", [128, ntt * TTW], F32))
    B["s1"] = st.enter_context(nc.sbuf_tensor(f"s1{uid}", [128, 2 * TTW], BF16))
    B["a2"] = st.enter_context(nc.sbuf_tensor(f"a2{uid}", [128, 2 * TTW], BF16)) \
        if mode == "l0" else None
    B["ze"] = st.enter_context(nc.sbuf_tensor(f"ze{uid}", [128, NCH * ntt * TTW], BF16)) \
        if mode == "l1" else None
    return B


def _emit_moe_experts(P, ph, B, zget, comb, x_fm, uid, NF, blk1,
                      dw1, dw2, dw3, db1, db2, db3, mode, tbase, ntt,
                      lnw=None, lnb=None, comb_base=None):
    nc = P.nc
    ones_row_bf = P.sb["ones_row_bf"]
    TGW = ntt * TTW
    h_sb = B["h"]; w1s = B["w1"]; w2s = B["w2"]; w3s = B["w3"]
    b1s = B["b1"]; b2s = B["b2"]; b3r = B["b3"]
    crow = B["cr"]; crep = B["cp"]; s1t = B["s1"]; a2t = B["a2"]; zes = B["ze"]
    with nc.psum_tensor(f"ap{uid}{tbase}", [128, 2048], F32) as aps, \
         nc.psum_tensor(f"yp{uid}{tbase}", [128, 1024], F32) as yps, \
         nc.psum_tensor(f"rp{uid}{tbase}", [128, 1024], F32) as rps:
        c_prev_y = 0
        c_prev_fc1 = 0
        c_prev_b = (0, 0)
        for e in range(MOE_NEXP):
            ph.wait("y", "se", c_prev_b[0])
            ph.wait("g", "pe", c_prev_b[1])
            c_b1 = ph.do("y", lambda e_, e=e: e_.dma_start(b1s[:, :], db1.ap()[e, :, :]),
                         inc="al", n=16)
            if mode == "l0":
                c_b1 = ph.do("y", lambda e_, e=e: e_.dma_start(b2s[:, :], db2.ap()[e, :, :]),
                             inc="al", n=16)
            c_b3 = ph.do("g", lambda e_, e=e: e_.dma_start(b3r[:, :], db3.ap()[e, :, :]),
                         inc="al", n=16)
            cb = tbase if comb_base is None else comb_base
            c_cr = ph.do("g", lambda e_, e=e, cb=cb: e_.dma_start(
                crow[:, :], comb[e:e + 1, cb:cb + TGW]), inc="al", n=16)
            ph.wait("t", "al", c_cr)
            ph.wait("t", "ve", P.cnt["ve"])
            c_crepq = {}
            for q in range(ntt):
                if q >= 2:
                    ph.wait("t", "ve", c_crepq[q - 2])
                ph.do("t", lambda e_, q=q: e_.matmul(
                    rps[:, (q % 2) * 512:(q % 2) * 512 + 512], ones_row_bf[:, :],
                    crow[0:1, q * TTW:(q + 1) * TTW], start=True, stop=True))
                c_pe = P.cnt["pe"]
                ph.wait("v", "pe", c_pe)
                c_crepq[q] = ph.do("v", lambda e_, q=q: e_.tensor_copy(
                    crep[:, q * TTW:(q + 1) * TTW],
                    rps[:, (q % 2) * 512:(q % 2) * 512 + 512]))
            c_creps = P.cnt["ve"]
            if mode == "l1":
                ph.wait("s", "pe", c_prev_fc1)
                ph.wait("s", "al", c_b1)
                for cc in range(NCH):
                    for q in range(ntt):
                        ph.do("s", lambda e_, cc=cc, q=q, e=e: e_.activation(
                            zes[:, cc * TGW + q * TTW: cc * TGW + (q + 1) * TTW],
                            zget(cc, q),
                            AF.Identity, bias=lnb[:, e * NCH + cc:e * NCH + cc + 1],
                            scale=lnw[:, e * NCH + cc:e * NCH + cc + 1]))
                c_ze = P.cnt["se"]
                zin = lambda cc, q: zes[:, cc * TGW + q * TTW: cc * TGW + (q + 1) * TTW]
            else:
                c_ze = 0
                zin = zget
            wdmas = {}

            def load_blk(fc, e=e):
                dep = wdmas.get(fc - 2)
                if dep is not None:
                    ph.wait("g", "pe", dep[1])
                c1 = ph.do("g", lambda e_, fc=fc, e=e: e_.dma_start(
                    w1s.ap().rearrange("p (h w) -> p h w", h=2)[:, fc % 2:fc % 2 + 1, :],
                    dw1.ap()[e, fc:fc + 1, :, :].rearrange("o p w -> p o w")),
                    inc="wl", n=16)
                if mode == "l0":
                    c1 = ph.do("g", lambda e_, fc=fc, e=e: e_.dma_start(
                        w2s.ap().rearrange("p (h w) -> p h w", h=2)[:, fc % 2:fc % 2 + 1, :],
                        dw2.ap()[e, fc:fc + 1, :, :].rearrange("o p w -> p o w")),
                        inc="wl", n=16)
                wdmas[fc] = (c1, 0)

            load_blk(0)
            load_blk(1)
            c_sil = {}
            c_a2c = {}
            c_hh = {}
            for fc in range(NF):
                for q in range(ntt):
                    i = fc * ntt + q
                    bank = i % 2
                    if i == 0:
                        ph.wait("t", "wl", wdmas[fc][0])
                        ph.wait("t", "al", c_b1)
                        ph.wait("t", "se", c_ze)
                        ph.wait("t", "ve", c_creps)
                    elif q == 0:
                        ph.wait("t", "wl", wdmas[fc][0])
                    if i >= 2:
                        ph.wait("t", "se", c_sil[i - 2])
                    for cc in range(NCH):
                        ph.do("t", lambda e_, cc=cc, fc=fc, q=q, bank=bank: e_.matmul(
                            aps[:, bank * 512:bank * 512 + 512],
                            w1s[:, (fc % 2) * blk1 + cc * 128:(fc % 2) * blk1 + (cc + 1) * 128],
                            zin(cc, q),
                            start=(cc == 0), stop=(cc == NCH - 1)),
                            inc=(None if cc < NCH - 1 else "pe"))
                    c_a1 = P.cnt["pe"]
                    if mode == "l0":
                        if i >= 2:
                            ph.wait("t", "ve", c_a2c[i - 2])
                        for cc in range(NCH):
                            ph.do("t", lambda e_, cc=cc, fc=fc, q=q, bank=bank: e_.matmul(
                                aps[:, 1024 + bank * 512:1024 + bank * 512 + 512],
                                w2s[:, (fc % 2) * blk1 + cc * 128:(fc % 2) * blk1 + (cc + 1) * 128],
                                zin(cc, q),
                                start=(cc == 0), stop=(cc == NCH - 1)),
                                inc=(None if cc < NCH - 1 else "pe"))
                        c_a2 = P.cnt["pe"]
                    hofs = fc * TGW + q * TTW
                    ph.wait("s", "pe", c_a1)
                    if i >= 2:
                        ph.wait("s", "ve", c_hh[i - 2])
                    if mode == "l0":
                        c_sil[i] = ph.do("s", lambda e_, fc=fc, bank=bank, i=i: e_.activation(
                            s1t[:, (i % 2) * TTW:(i % 2 + 1) * TTW],
                            aps[:, bank * 512:bank * 512 + 512], AF.Silu,
                            bias=b1s[:, fc:fc + 1]))
                        ph.wait("v", "pe", c_a2)
                        c_a2c[i] = ph.do("v", lambda e_, fc=fc, q=q, bank=bank, i=i:
                                         e_.scalar_tensor_tensor(
                            a2t[:, (i % 2) * TTW:(i % 2 + 1) * TTW],
                            aps[:, 1024 + bank * 512:1024 + bank * 512 + 512],
                            b2s[:, fc:fc + 1],
                            crep[:, q * TTW:(q + 1) * TTW], ALU.add, ALU.mult))
                        ph.wait("v", "se", c_sil[i])
                        c_hh[i] = ph.do("v", lambda e_, hofs=hofs, i=i: e_.tensor_mul(
                            h_sb[:, hofs:hofs + TTW],
                            s1t[:, (i % 2) * TTW:(i % 2 + 1) * TTW],
                            a2t[:, (i % 2) * TTW:(i % 2 + 1) * TTW]))
                    else:
                        c_sil[i] = ph.do("s", lambda e_, fc=fc, bank=bank, i=i: e_.activation(
                            s1t[:, (i % 2) * TTW:(i % 2 + 1) * TTW],
                            aps[:, bank * 512:bank * 512 + 512], AF.Gelu_apprx_tanh,
                            bias=b1s[:, fc:fc + 1]))
                        ph.wait("v", "se", c_sil[i])
                        c_hh[i] = ph.do("v", lambda e_, hofs=hofs, q=q, i=i: e_.tensor_mul(
                            h_sb[:, hofs:hofs + TTW],
                            s1t[:, (i % 2) * TTW:(i % 2 + 1) * TTW],
                            crep[:, q * TTW:(q + 1) * TTW]))
                if fc + 2 < NF:
                    wdmas[fc] = (wdmas[fc][0], P.cnt["pe"])
                    load_blk(fc + 2)
            c_prev_fc1 = P.cnt["pe"]
            c_hall = P.cnt["ve"]
            if MOE_SKIP_FC3:
                c_prev_y = P.cnt["pe"]
                c_prev_b = (P.cnt["se"], P.cnt["pe"])
                continue
            w3dmas = {}

            def load_w3(dc, e=e):
                dep = w3dmas.get(dc - 2)
                if dep is not None:
                    ph.wait("g", "pe", dep[1])
                c3 = ph.do("g", lambda e_, dc=dc, e=e: e_.dma_start(
                    w3s.ap().rearrange("p (h w) -> p h w", h=2)[:, dc % 2:dc % 2 + 1, :],
                    dw3.ap()[e, dc:dc + 1, :, :].rearrange("o p w -> p o w")),
                    inc="wl", n=16)
                w3dmas[dc] = (c3, 0)

            load_w3(0)
            load_w3(1)
            c_ev = {}
            for dc in range(NCH):
                for q in range(ntt):
                    i2 = dc * ntt + q
                    bank = i2 % 2
                    if q == 0:
                        ph.wait("t", "wl", w3dmas[dc][0])
                    if i2 == 0:
                        ph.wait("t", "ve", c_hall)
                        ph.wait("t", "al", c_b3)
                    if i2 >= 2:
                        ph.wait("t", "ve", c_ev[i2 - 2])
                    for fcc in range(NF):
                        ph.do("t", lambda e_, fcc=fcc, dc=dc, q=q, bank=bank: e_.matmul(
                            yps[:, bank * 512:bank * 512 + 512],
                            w3s[:, (dc % 2) * NF * 128 + fcc * 128:(dc % 2) * NF * 128 + (fcc + 1) * 128],
                            h_sb[:, fcc * TGW + q * TTW: fcc * TGW + (q + 1) * TTW],
                            start=(fcc == 0), stop=False), inc=None)
                    ph.do("t", lambda e_, dc=dc, q=q, bank=bank: e_.matmul(
                        yps[:, bank * 512:bank * 512 + 512],
                        b3r[0:1, dc * 128:(dc + 1) * 128],
                        crow[0:1, q * TTW:(q + 1) * TTW],
                        start=False, stop=True, skip_group_check=True), inc="pe")
                    c_y = P.cnt["pe"]
                    ph.wait("v", "pe", c_y)
                    c_ev[i2] = ph.do("v", lambda e_, dc=dc, q=q, bank=bank, tbase=tbase:
                                     e_.scalar_tensor_tensor(
                        x_fm[:, dc * T + tbase + q * TTW: dc * T + tbase + (q + 1) * TTW],
                        yps[:, bank * 512:bank * 512 + 512], 1.0,
                        x_fm[:, dc * T + tbase + q * TTW: dc * T + tbase + (q + 1) * TTW],
                        ALU.mult, ALU.add))
                if dc + 2 < NCH:
                    w3dmas[dc] = (w3dmas[dc][0], P.cnt["pe"])
                    load_w3(dc + 2)
            c_prev_y = P.cnt["pe"]
            c_prev_b = (P.cnt["se"], P.cnt["pe"])


def emit_moe_l0(P, ph):
    from contextlib import ExitStack
    nc = P.nc
    x_fm = P.sb["x_fm"]
    NF = 16
    dn2 = P.din("l0f_n2w", [128, NCH], F32)
    dw1 = P.din("l0f_w1b", [8, NF, 128, NCH * 128], F32)
    dw2 = P.din("l0f_w2b", [8, NF, 128, NCH * 128], F32)
    dw3 = P.din("l0f_w3b", [8, NCH, 128, NF * 128], F32)
    db1 = P.din("l0f_b1", [8, 128, NF], F32)
    db2 = P.din("l0f_b2", [8, 128, NF], F32)
    db3 = P.din("l0f_b3", [8, 1, D], F32)
    with ExitStack() as st:
        zp = st.enter_context(nc.sbuf_tensor("l0m_z", [128, NCH * T], BF16))
        comb = st.enter_context(nc.sbuf_tensor("l0m_c", [8, T], F32))
        n2w = st.enter_context(nc.sbuf_tensor("l0m_n2", [128, NCH], F32))
        c_n2 = ph.do("y", lambda e: e.dma_start(n2w[:, :], dn2[:, :]), inc="al", n=16)
        ph.wait("v", "al", c_n2)
        emit_norm_fm(P, ph, x_fm, zp, "rms", 1e-6, w_sb=n2w, uid="l0m")
        emit_gating(P, ph,
                    lambda cc, tk: zp[:, cc * T + tk * 128: cc * T + (tk + 1) * 128],
                    "l0f_gatet", comb, "l0m", tks=range(16))
        if MOE_DBG:
            dz = P.dout("dump_mz", [128, NCH * T], BF16)
            dc_ = P.dout("dump_comb", [8, T], F32)
            c = ph.do("y", lambda e: e.dma_start(dz[:, :], zp[:, :]), inc="st", n=16)
            c = ph.do("y", lambda e: e.dma_start(dc_[:, :], comb[:, :]), inc="st", n=16)
            ph.wait("y", "st", c)
            ph.flush()
            return
        B = _moe_bufs(P, st, "l0m", NF, NTT, "l0", NCH * 128)
        _emit_moe_experts(P, ph, B,
                          lambda cc, q: zp[:, cc * T + q * TTW: cc * T + (q + 1) * TTW],
                          comb, x_fm, "l0m", NF, NCH * 128,
                          dw1, dw2, dw3, db1, db2, db3, "l0", tbase=0, ntt=NTT)
        ph.flush()


def emit_moe_l1(P, ph):
    from contextlib import ExitStack
    nc = P.nc
    x_fm = P.sb["x_fm"]
    ones_row = P.sb["ones_row"]
    NF = 32
    NTG = 2
    NTTG = 2
    dw1 = P.din("l1f_fc1b", [8, NF, 128, NCH * 128], F32)
    dw3 = P.din("l1f_fc2b", [8, NCH, 128, NF * 128], F32)
    db1 = P.din("l1f_b1", [8, 128, NF], F32)
    db3 = P.din("l1f_b2", [8, 1, D], F32)
    dlnw = P.din("l1f_lnw", [8, 128, NCH], F32)
    dlnb = P.din("l1f_lnb", [8, 128, NCH], F32)
    dalpha = P.din("l1f_alpha", [1, 8], F32)
    with ExitStack() as st:
        zp = st.enter_context(nc.sbuf_tensor("l1m_z", [128, NCH * NTTG * TTW], BF16))
        comb = st.enter_context(nc.sbuf_tensor("l1m_c", [8, NTTG * TTW], F32))
        lnw = st.enter_context(nc.sbuf_tensor("l1m_lw", [128, 8 * NCH], F32))
        lnb = st.enter_context(nc.sbuf_tensor("l1m_lb", [128, 8 * NCH], F32))
        arow = st.enter_context(nc.sbuf_tensor("l1m_ar", [1, 8], F32))
        alph = st.enter_context(nc.sbuf_tensor("l1m_al", [128, 8], F32))
        ph.do("y", lambda e: e.dma_start(
            lnw.ap().rearrange("p (e c) -> p e c", e=8),
            dlnw.ap().rearrange("e p c -> p e c")), inc="al", n=16)
        ph.do("y", lambda e: e.dma_start(
            lnb.ap().rearrange("p (e c) -> p e c", e=8),
            dlnb.ap().rearrange("e p c -> p e c")), inc="al", n=16)
        c_a = ph.do("y", lambda e: e.dma_start(arow[:, :], dalpha[:, :]), inc="al", n=16)
        with nc.psum_tensor("l1m_alp", [128, 64], F32) as alps:
            sq = Seq(ph, deps=[("al", c_a)])
            sq.do("t", lambda e: e.matmul(alps[:, 0:8], ones_row[:, :], arow[:, :],
                                          start=True, stop=True))
            c_alph = sq.do("v", lambda e: e.tensor_copy(alph[:, :], alps[:, 0:8]))
        if True:
            B = _moe_bufs(P, st, "l1m", NF, NTTG, "l1", NCH * 128)
            for tg in range(NTG):
                tbase = tg * NTTG * TTW
                tts = [tg * NTTG + k for k in range(NTTG)]
                emit_norm_fm(P, ph, x_fm, zp, "ln", 1e-5, uid=f"l1o{tg}",
                             tts=tts, dst_stride=NTTG * TTW)
                emit_gating(P, ph,
                            lambda cc, tk, tbase=tbase: zp[:, cc * NTTG * TTW + tk * 128 - tbase:
                                                           cc * NTTG * TTW + (tk + 1) * 128 - tbase],
                            "l1f_wg", comb, f"l1m{tg}",
                            tks=range(tg * NTTG * 4, (tg + 1) * NTTG * 4),
                            alpha_sb=alph, tk0=tg * NTTG * 4)
                emit_norm_fm(P, ph, zp, zp, "ln", 1e-5, uid=f"l1i{tg}",
                             tts=list(range(NTTG)), dst_stride=NTTG * TTW,
                             src_stride=NTTG * TTW)
                _emit_moe_experts(P, ph, B,
                                  lambda cc, q: zp[:, cc * NTTG * TTW + q * TTW:
                                                   cc * NTTG * TTW + (q + 1) * TTW],
                                  comb, x_fm, f"l1m{tg}", NF, NCH * 128,
                                  dw1, None, dw3, db1, None, db3, "l1",
                                  tbase=tbase, ntt=NTTG, lnw=lnw, lnb=lnb, comb_base=0)
        ph.flush()


KK_PAIRS = [(0, 1), (0, 2), (1, 2), (0, 3), (1, 3), (2, 3)]
KQ_PAIRS = [(0, 0), (0, 1), (1, 1), (0, 2), (1, 2), (2, 2),
            (0, 3), (1, 3), (2, 3), (3, 3)]


def emit_l1_attn(P, ph):
    from contextlib import ExitStack
    nc = P.nc
    x_fm = P.sb["x_fm"]
    ones_col = P.sb["ones_col"]; ones_row = P.sb["ones_row"]
    ones_col_bf = P.sb["ones_col_bf"]; ones_row_bf = P.sb["ones_row_bf"]
    rollp = P.sb["rollp"]
    dwq = P.din("l1a_wq", [128, NCH * D], F32)
    dwk = P.din("l1a_wk", [128, NCH * D], F32)
    dwv = P.din("l1a_wv", [128, NCH * D], F32)
    dwb = P.din("l1a_wbm", [128, NCH * 16], F32)
    dwo = P.din("l1a_wo", [128, NCH * D], F32)
    dbo = P.din("l1a_bo", [128, NCH], F32)

    dy = P.dint("l1a_y", [D, T], BF16)
    with ExitStack() as st:
        zbuf = st.enter_context(nc.sbuf_tensor("l1a_z", [128, NCH * T], BF16))
        beta_fm = st.enter_context(nc.sbuf_tensor("l1a_bfs", [16, T], BF16))
        brow = st.enter_context(nc.sbuf_tensor("l1a_br", [1, T], F32))
        rows = st.enter_context(nc.sbuf_tensor("l1a_rw", [1, 5 * TTW], F32))
        rowsb = st.enter_context(nc.sbuf_tensor("l1a_rwb", [1, 16 * TTW], BF16))
        bo_sb = st.enter_context(nc.sbuf_tensor("l1a_bos", [128, NCH], F32))
        c_bo = ph.do("y", lambda e: e.dma_start(bo_sb[:, :], dbo[:, :]), inc="al", n=16)

        emit_norm_fm(P, ph, x_fm, zbuf, "ln", 1e-5, uid="l1a")
        c_z = P.cnt["ve"]

        with nc.sbuf_tensor("l1a_wbs", [128, NCH * 16], BF16) as wbs, \
             nc.psum_tensor("l1a_bp", [16, 512], F32) as bps:
            c_wb = ph.do("g", lambda e: e.dma_start(wbs[:, :], dwb[:, :]), inc="wl", n=16)
            base_se = P.cnt["se"]
            for tt in range(NTT):
                if tt == 0:
                    ph.wait("t", "wl", c_wb)
                    ph.wait("t", "ve", c_z)
                if tt >= 1:
                    ph.wait("t", "se", base_se + tt)
                for cc in range(NCH):
                    ph.do("t", lambda e, cc=cc, tt=tt: e.matmul(
                        bps[:, :], wbs[:, cc * 16:(cc + 1) * 16],
                        zbuf[:, cc * T + tt * TTW: cc * T + (tt + 1) * TTW],
                        start=(cc == 0), stop=(cc == NCH - 1)),
                        inc=(None if cc < NCH - 1 else "pe"))
                c_pe = P.cnt["pe"]
                ph.wait("s", "pe", c_pe)
                ph.do("s", lambda e, tt=tt: e.activation(
                    beta_fm[:, tt * TTW:(tt + 1) * TTW], bps[:, :], AF.Sigmoid))
            c_brow = P.cnt["se"]
            ph.flush()

        with nc.sbuf_tensor("l1a_wp", [128, 2 * 3 * NCH * 128], BF16) as wpair, \
             nc.sbuf_tensor("l1a_qkv", [128, 3 * T], BF16) as qkv, \
             nc.sbuf_tensor("l1a_cat", [128, 2 * T], BF16) as cat, \
             nc.sbuf_tensor("l1a_phi", [128, T], BF16) as phif, \
             nc.sbuf_tensor("l1a_kphi", [128, T], BF16) as kphi, \
             nc.sbuf_tensor("l1a_qphi", [128, T], BF16) as qphi, \
             nc.sbuf_tensor("l1a_dp", [128, TTW], F32) as dprod, \
             nc.sbuf_tensor("l1a_u", [64, 6 * TTW], F32) as ubuf, \
             nc.sbuf_tensor("l1a_yt", [64, TTW], BF16) as ytile, \
             nc.sbuf_tensor("l1a_vt", [64, TTW], BF16) as vtile, \
             nc.psum_tensor("l1a_pj", [128, 1024], F32) as prj_ps, \
             nc.psum_tensor("l1a_rp", [128, 1024], F32) as rp_ps, \
             nc.psum_tensor("l1a_np", [1, 1024], F32) as nps, \
             nc.psum_tensor("l1a_rr", [128, 1024], F32) as rep_ps:
            arow = lambda i: rowsb[0:1, KK_PAIRS.index(i) * TTW:
                                   (KK_PAIRS.index(i) + 1) * TTW]
            crow_ = lambda i: rowsb[0:1, (6 + KQ_PAIRS.index(i)) * TTW:
                                    (6 + KQ_PAIRS.index(i) + 1) * TTW]
            qrrow = lambda s: rows[0:1, s * TTW:(s + 1) * TTW]
            mrow = rows[0:1, 4 * TTW:5 * TTW]
            PW = NCH * 128

            wdma = {}

            def load_pair(pr):
                half = pr % 2
                dep = wdma.get(pr - 2)
                if dep is not None:
                    ph.wait("g", "pe", dep[1])
                c = 0
                for wi, dw in enumerate((dwq, dwk, dwv)):
                    c = ph.do("g", lambda e, dw=dw, half=half, wi=wi, pr=pr: e.dma_start(
                        wpair.ap().rearrange("p (h w c) -> p h w c", h=2, w=3)[
                            :, half, wi, :].rearrange("p (cc m) -> p cc m", cc=NCH),
                        dw.ap().rearrange("p (cc m) -> p cc m", cc=NCH)[
                            :, :, pr * 128:(pr + 1) * 128]), inc="wl", n=16)
                wdma[pr] = (c, 0)

            load_pair(0)
            load_pair(1)
            c_prev_heads = c_z
            for pr in range(8):
                half = pr % 2
                base_se = P.cnt["se"]
                i = 0
                for wi in range(3):
                    for tt in range(NTT):
                        bank = i % 2
                        if i == 0:
                            ph.wait("t", "wl", wdma[pr][0])
                            ph.wait("t", "ve", c_prev_heads)
                        if i >= 2:
                            ph.wait("t", "se", base_se + i - 1)
                        for cc in range(NCH):
                            ph.do("t", lambda e, cc=cc, wi=wi, tt=tt, bank=bank, half=half:
                                  e.matmul(
                                      prj_ps[:, bank * 512:(bank + 1) * 512],
                                      wpair[:, half * 3 * PW + wi * PW + cc * 128:
                                            half * 3 * PW + wi * PW + (cc + 1) * 128],
                                      zbuf[:, cc * T + tt * TTW: cc * T + (tt + 1) * TTW],
                                      start=(cc == 0), stop=(cc == NCH - 1)),
                                  inc=(None if cc < NCH - 1 else "pe"))
                        c_pe = P.cnt["pe"]
                        ph.wait("s", "pe", c_pe)
                        ph.do("s", lambda e, wi=wi, tt=tt, bank=bank: e.activation(
                            qkv[:, wi * T + tt * TTW: wi * T + (tt + 1) * TTW],
                            prj_ps[:, bank * 512:(bank + 1) * 512], AF.Identity))
                        i += 1
                wdma[pr] = (wdma[pr][0], P.cnt["pe"])
                if pr + 2 < 8:
                    load_pair(pr + 2)
                c_proj = P.cnt["se"]

                sq = Seq(ph, deps=[("se", c_proj)])
                for hj in range(2):
                    h = pr * 2 + hj
                    hb = hj * 64
                    sq.do("g", lambda e, h=h: e.dma_start(
                        brow[:, :], beta_fm[h:h + 1, :]), inc="al", n=16)
                    for which, (src_off, phidst) in enumerate(
                            ((T, kphi), (0, qphi))):
                        sq.do("s", lambda e, src_off=src_off, hb=hb: e.activation(
                            cat[0:64, 0:T], qkv[src_off + 0:src_off + 0][0:0, 0:0]
                            if False else qkv[hb:hb + 64, src_off:src_off + T],
                            AF.Relu))
                        sq.do("s", lambda e, src_off=src_off, hb=hb: e.activation(
                            cat[64:128, 0:T], qkv[hb:hb + 64, src_off:src_off + T],
                            AF.Relu, scale=-1.0))
                        for tt in range(NTT):
                            bank = tt % 2
                            sq.do("t", lambda e, tt=tt, bank=bank: e.matmul(
                                rp_ps[:, bank * 512:(bank + 1) * 512], rollp[:, :],
                                cat[:, tt * TTW:(tt + 1) * TTW], start=True, stop=True))
                            sq.do("v", lambda e, tt=tt, bank=bank: e.tensor_mul(
                                phif[:, tt * TTW:(tt + 1) * TTW],
                                cat[:, tt * TTW:(tt + 1) * TTW],
                                rp_ps[:, bank * 512:(bank + 1) * 512]))
                            sq.do("t", lambda e, tt=tt, bank=bank: e.matmul(
                                nps[:, bank * 512:(bank + 1) * 512], ones_col_bf[:, :],
                                phif[:, tt * TTW:(tt + 1) * TTW], start=True, stop=True))
                            sq.do("v", lambda e, tt=tt, bank=bank: e.tensor_scalar(
                                mrow[:, tt * 128: tt * 128 + 512][0:1, 0:512]
                                if False else mrow,
                                nps[0:1, bank * 512:(bank + 1) * 512], 1e-6, None,
                                ALU.add))
                            sq.do("v", lambda e, which=which, tt=tt: e.reciprocal(
                                (qrrow(tt) if which == 1 else mrow), mrow))
                            if which == 0:
                                sq.do("t", lambda e, tt=tt, bank=bank: e.matmul(
                                    rep_ps[:, bank * 512:(bank + 1) * 512][0:64, :]
                                    if False else rep_ps[0:64, bank * 512:(bank + 1) * 512],
                                    ones_row[0:1, 0:64], mrow, start=True, stop=True))
                                sq.do("t", lambda e, tt=tt, bank=bank: e.matmul(
                                    rep_ps[64:128, bank * 512:(bank + 1) * 512],
                                    ones_row[0:1, 64:128], mrow, start=True, stop=True))
                                sq.do("v", lambda e, tt=tt, bank=bank: e.tensor_mul(
                                    kphi[:, tt * TTW:(tt + 1) * TTW],
                                    phif[:, tt * TTW:(tt + 1) * TTW],
                                    rp_ps[:, bank * 512:(bank + 1) * 512] if False
                                    else rep_ps[:, bank * 512:(bank + 1) * 512]))
                            else:
                                sq.do("v", lambda e, tt=tt: e.tensor_copy(
                                    qphi[:, tt * TTW:(tt + 1) * TTW],
                                    phif[:, tt * TTW:(tt + 1) * TTW]))
                    for (i_, s_) in KK_PAIRS:
                        sq.do("v", lambda e, i_=i_, s_=s_: e.tensor_mul(
                            dprod[:, :], kphi[:, i_ * TTW:(i_ + 1) * TTW],
                            kphi[:, s_ * TTW:(s_ + 1) * TTW]))
                        sq.do("t", lambda e: e.matmul(
                            nps[:, 0:512], ones_col[:, :], dprod[:, :],
                            start=True, stop=True))
                        sq.do("v", lambda e, i_=i_, s_=s_: e.tensor_copy(
                            arow((i_, s_)), nps[0:1, 0:512]))
                    for (i_, s_) in KQ_PAIRS:
                        sq.do("v", lambda e, i_=i_, s_=s_: e.tensor_mul(
                            dprod[:, :], kphi[:, i_ * TTW:(i_ + 1) * TTW],
                            qphi[:, s_ * TTW:(s_ + 1) * TTW]))
                        sq.do("t", lambda e: e.matmul(
                            nps[:, 0:512], ones_col[:, :], dprod[:, :],
                            start=True, stop=True))
                        sq.do("v", lambda e, i_=i_, s_=s_: e.tensor_copy(
                            crow_((i_, s_)), nps[0:1, 0:512]))

                    u = lambda i: ubuf[0:64, i * TTW:(i + 1) * TTW]
                    tmp = ubuf[0:64, 4 * TTW:5 * TTW]
                    yac = ubuf[0:64, 5 * TTW:6 * TTW]
                    vsl = lambda s: qkv[hb:hb + 64, 2 * T + s * TTW: 2 * T + (s + 1) * TTW]

                    def rep64(row_ap, bank, bf=False):
                        orow = ones_row_bf if bf else ones_row
                        sq.do("t", lambda e, row_ap=row_ap, bank=bank, orow=orow:
                              e.matmul(rep_ps[0:64, bank * 512:(bank + 1) * 512],
                                       orow[0:1, 0:64], row_ap, start=True, stop=True))
                        return rep_ps[0:64, bank * 512:(bank + 1) * 512]

                    def betarow(s):
                        return brow[0:1, s * TTW:(s + 1) * TTW]

                    for s_ in range(NB):
                        sq.do("v", lambda e, s_=s_: e.tensor_copy(vtile[:, :], vsl(s_)))
                        if s_ == 0:
                            r = rep64(betarow(0), 0)
                            sq.do("v", lambda e, r=r: e.tensor_mul(u(0), vtile[:, :], r))
                        else:
                            for ii in range(s_):
                                r = rep64(arow((ii, s_)), ii % 2, bf=True)
                                if ii == 0:
                                    sq.do("v", lambda e, r=r, ii=ii: e.tensor_mul(
                                        tmp, u(ii), r))
                                else:
                                    sq.do("v", lambda e, r=r, ii=ii: e.tensor_mul(
                                        dprod[0:64, :], u(ii), r))
                                    sq.do("v", lambda e: e.tensor_add(
                                        tmp, tmp, dprod[0:64, :]))
                            sq.do("v", lambda e: e.tensor_sub(tmp, vtile[:, :], tmp))
                            r = rep64(betarow(s_), s_ % 2)
                            sq.do("v", lambda e, r=r, s_=s_: e.tensor_mul(u(s_), tmp, r))
                        for ii in range(s_ + 1):
                            r = rep64(crow_((ii, s_)), ii % 2, bf=True)
                            if ii == 0:
                                sq.do("v", lambda e, r=r, ii=ii: e.tensor_mul(
                                    yac, u(ii), r))
                            else:
                                sq.do("v", lambda e, r=r, ii=ii: e.tensor_mul(
                                    dprod[0:64, :], u(ii), r))
                                sq.do("v", lambda e: e.tensor_add(
                                    yac, yac, dprod[0:64, :]))
                        r = rep64(qrrow(s_), s_ % 2)
                        sq.do("v", lambda e, r=r: e.tensor_mul(ytile[:, :], yac, r))
                        sq.do("y", lambda e, h=h, s_=s_: e.dma_start(
                            dy.ap()[h * 64:(h + 1) * 64, s_ * TTW:(s_ + 1) * TTW],
                            ytile[:, :]), inc="st", n=16)
                c_prev_heads = P.cnt["ve"]
            ph.flush()

        with nc.sbuf_tensor("l1a_wos", [128, NCH * D], BF16) as wo_sb, \
             nc.sbuf_tensor("l1a_ab", [128, 2 * NCH * 512], BF16) as absl, \
             nc.psum_tensor("l1a_op", [128, 1024], F32) as op_ps:
            c_wo = ph.do("g", lambda e: e.dma_start(
                wo_sb.ap().rearrange("p (c m) -> p c m", c=NCH),
                dwo.ap().rearrange("p (c m) -> p c m", c=NCH)), inc="wl", n=16)
            c_y = P.cnt["st"]
            c_ldt = {}
            c_tt_pe = {}

            def load_a(tt):
                ph.wait("y", "st", c_y)
                if tt >= 2:
                    ph.wait("y", "pe", c_tt_pe[tt - 2])
                c_ldt[tt] = ph.do("y", lambda e, tt=tt: e.dma_start(
                    absl.ap().rearrange("p (h c w) -> p h c w", h=2, c=NCH)[
                        :, tt % 2, :, :],
                    dy.ap().rearrange("(c p) t -> p c t", p=128)[
                        :, :, tt * TTW:(tt + 1) * TTW]), inc="al", n=16)

            load_a(0)
            load_a(1)
            c_ev = {}
            for tt in range(NTT):
                for hc in range(NCH):
                    i = tt * NCH + hc
                    bank = i % 2
                    if hc == 0:
                        ph.wait("t", "al", c_ldt[tt])
                        if tt == 0:
                            ph.wait("t", "wl", c_wo)
                            ph.wait("t", "al", c_bo)
                    if i >= 2:
                        ph.wait("t", "ve", c_ev[i - 2])
                    for cc in range(NCH):
                        ph.do("t", lambda e, cc=cc, hc=hc, tt=tt, bank=bank: e.matmul(
                            op_ps[:, bank * 512:(bank + 1) * 512],
                            wo_sb[:, cc * D + hc * 128: cc * D + (hc + 1) * 128],
                            absl[:, (tt % 2) * NCH * 512 + cc * 512:
                                 (tt % 2) * NCH * 512 + (cc + 1) * 512],
                            start=(cc == 0), stop=(cc == NCH - 1)),
                            inc=(None if cc < NCH - 1 else "pe"))
                    c_pe = P.cnt["pe"]
                    ph.wait("v", "pe", c_pe)
                    c_ev[i] = ph.do("v", lambda e, hc=hc, tt=tt, bank=bank:
                                    e.scalar_tensor_tensor(
                        x_fm[:, hc * T + tt * TTW: hc * T + (tt + 1) * TTW],
                        op_ps[:, bank * 512:(bank + 1) * 512],
                        bo_sb[:, hc:hc + 1],
                        x_fm[:, hc * T + tt * TTW: hc * T + (tt + 1) * TTW],
                        ALU.add, ALU.add))
                c_tt_pe[tt] = P.cnt["pe"]
                if tt + 2 < NTT:
                    load_a(tt + 2)
            ph.flush()
